# version 37
# baseline (speedup 1.0000x reference)
import math
import re
import contextlib
import numpy as np

import concourse.bass as bass
import concourse.mybir as mybir
from concourse.bass_utils import run_bass_kernel_spmd

F32 = mybir.dt.float32
BF16 = mybir.dt.bfloat16
I32 = mybir.dt.int32
AF = mybir.ActivationFunctionType
ALU = mybir.AluOpType
AX = mybir.AxisListType

NCORES = 8
D = 1024
DK = 1024
DV = 2048
H = 4
DKH = 256
DVH = 512
R = 16
DH = 4096
C = 128
DEBUG_DUMPS = False



class Step:
    def __init__(self, eng, fn, waits, incs, tag):
        self.eng, self.fn, self.waits, self.incs, self.tag = eng, fn, waits, incs, tag


class Builder:

    ENG_SEM = {"tensor": "te", "vector": "ve", "scalar": "se",
               "gpsimd": "gc", "sync": "sy"}

    CONST_BUFS = {"identt", "maskt", "ncols", "no_ct", "ngk2_t", "ninv_t",
                  "amask_t", "am1_t", "bmask_t", "uph_t", "dnh_t", "bex_t",
                  "upb_ct", "bcon_ct", "dnb_ct", "onesr"}

    def __init__(self, nc, T):
        self.nc = nc
        self.T = T
        self.NT = T // C
        self.steps = []
        self.sems = {}
        self._es = contextlib.ExitStack()
        for name in ("te", "ve", "se", "gc", "sy", "gd", "cc", "sd"):
            self.sems[name] = self._es.enter_context(nc.semaphore("sem_" + name))
        self.counts = {k: 0 for k in self.sems}
        self.last_write = {}
        self.readers = {}
        self.tensors = {}
        self.alias_debt = {}

    def _dma_sem(self, buf, eng="sync"):
        key = re.sub(r"[0-9_]+$", "", buf)
        if key in self.CONST_BUFS or buf in self.CONST_BUFS:
            key = "const"
        key = ("d_" if eng == "sync" else "dg_") + key
        if key not in self.sems:
            self.sems[key] = self._es.enter_context(self.nc.semaphore("sem_" + key))
            self.counts[key] = 0
        return key

    def sb(self, name, shape, dtype):
        t = self._es.enter_context(self.nc.sbuf_tensor(name, shape, dtype))
        self.tensors[name] = t
        return t

    def ps(self, name, shape):
        t = self._es.enter_context(self.nc.psum_tensor(name, shape, F32))
        self.tensors[name] = t
        return t

    def alias(self, name, oldnames):
        self.alias_debt.setdefault(name, []).extend(oldnames)

    def step(self, eng, fn, reads=(), writes=(), tag="", sem=None, inc=1):
        sem = sem or self.ENG_SEM[eng]
        writes = list(writes)
        for b_ in list(writes):
            if b_ in self.alias_debt:
                writes.extend(self.alias_debt.pop(b_))
        waits = {}
        def add_wait(ev):
            if ev is None:
                return
            s, v = ev
            if waits.get(s, 0) < v:
                waits[s] = v
        for b in reads:
            add_wait(self.last_write.get(b))
        for b in writes:
            add_wait(self.last_write.get(b))
            for ev in self.readers.get(b, ()):
                add_wait(ev)
        for s_ in list(waits):
            if s_.startswith(("d_", "dg_")):
                waits[s_] = self.counts[s_]
        self.counts[sem] += inc
        ev = (sem, self.counts[sem])
        for b in reads:
            self.readers.setdefault(b, []).append(ev)
        for b in writes:
            self.last_write[b] = ev
            self.readers[b] = []
        self.steps.append(Step(eng, fn, sorted(waits.items()), [(sem, inc)], tag))
        return ev

    def dma(self, eng, out, in_, reads=(), writes=(), tag=""):
        sem = self._dma_sem(writes[0] if writes else "misc", eng)
        return self.step(eng, lambda e: e.dma_start(out, in_),
                         reads=reads, writes=writes, tag=tag, sem=sem, inc=16)

    def barrier_event(self):
        return dict(self.counts)

    def emit(self):
        nc = self.nc
        per_eng = {e: [] for e in self.ENG_SEM}
        for st in self.steps:
            per_eng[st.eng].append(st)
        marks = {e: {} for e in self.ENG_SEM}

        with nc.Block() as block:
            def make_body(eng_name):
                def run(eng):
                    wm = marks[eng_name]
                    for st in per_eng[eng_name]:
                        for s, v in st.waits:
                            if wm.get(s, 0) >= v:
                                continue
                            eng.wait_ge(self.sems[s], v)
                            wm[s] = v
                        inst = st.fn(eng)
                        (s, k), = st.incs
                        if inst is None:
                            eng.sem_inc(self.sems[s], k)
                        else:
                            inst.then_inc(self.sems[s], k)
                return run
            block.tensor(make_body("tensor"))
            block.vector(make_body("vector"))
            block.scalar(make_body("scalar"))
            block.gpsimd(make_body("gpsimd"))
            block.sync(make_body("sync"))
        self._es.close()


def build_program(T):
    NT = T // C
    nc = bass.Bass("TRN2", target_bir_lowering=False)
    dt = nc.dram_tensor

    xs = dt("xs", [T, D], F32, kind="ExternalInput")
    ident = dt("ident", [128, 128], F32, kind="ExternalInput")
    trimask = dt("trimask", [128, 128], F32, kind="ExternalInput")
    wqT_sl = dt("wqT_sl", [128, DK], F32, kind="ExternalInput")
    wkT_sl = dt("wkT_sl", [128, DK], F32, kind="ExternalInput")
    wvT_sl = dt("wvT_sl", [128, DV], F32, kind="ExternalInput")
    wgT_sl = dt("wgT_sl", [128, DV], F32, kind="ExternalInput")
    woT_sl = dt("woT_sl", [128, 2 * D], F32, kind="ExternalInput")
    wgk1T_f = dt("wgk1T_f", [128, 8 * R], F32, kind="ExternalInput")
    wgk2T_f = dt("wgk2T_f", [16, DK], F32, kind="ExternalInput")
    nq_c = dt("nq_c", [128, 8], F32, kind="ExternalInput")
    nk_c = dt("nk_c", [128, 8], F32, kind="ExternalInput")
    nv_c = dt("nv_c", [128, 8], F32, kind="ExternalInput")
    ng_c = dt("ng_c", [128, 8], F32, kind="ExternalInput")
    ngk1_c = dt("ngk1_c", [128, 8], F32, kind="ExternalInput")
    no_c = dt("no_c", [128, 16], F32, kind="ExternalInput")
    ngk2_r = dt("ngk2_r", [128, 16], F32, kind="ExternalInput")
    bgk2_row = dt("bgk2_row", [128, 8], F32, kind="ExternalInput")
    ninv_row = dt("ninv_row", [128, 8], F32, kind="ExternalInput")
    lowU_sl = dt("lowU_sl", [1, 7 * 4 * 256], F32, kind="ExternalInput")
    lowD_sl = dt("lowD_sl", [1, 7 * 4 * 256], F32, kind="ExternalInput")
    uph_w = dt("uph_w", [128, 5 * 16 * 4], F32, kind="ExternalInput")
    dnh_w = dt("dnh_w", [128, 5 * 16 * 4], F32, kind="ExternalInput")
    wexpT_sl = dt("wexpT_sl", [4096, 128], F32, kind="ExternalInput")
    wcon_sl = dt("wcon_sl", [4096, 128], F32, kind="ExternalInput")
    bexp_c = dt("bexp_c", [128, 32], F32, kind="ExternalInput")
    upb_c = dt("upb_c", [128, 32], F32, kind="ExternalInput")
    bcon_c = dt("bcon_c", [128, 8], F32, kind="ExternalInput")
    dnb_c = dt("dnb_c", [128, 8], F32, kind="ExternalInput")
    ind4_in = dt("ind4_in", [4, 512], F32, kind="ExternalInput")
    amask = dt("amask", [128, 8], F32, kind="ExternalInput")
    am1 = dt("am1", [128, 8], F32, kind="ExternalInput")
    bmask = dt("bmask", [128, 8], F32, kind="ExternalInput")
    ys = dt("ys", [T, D], F32, kind="ExternalOutput")

    _K = "ExternalOutput" if DEBUG_DUMPS else "Internal"
    op_dr = dt("op_dr", [NT, 128, DV], BF16, kind=_K)
    sg_dr = dt("sg_dr", [NT, 128, DV], BF16, kind=_K)
    qb_dr = dt("qb_dr", [NT, 128, DK], BF16, kind=_K)
    x2_dr = dt("x2_dr", [NT, 128, D], F32, kind=_K)
    x2f_dr = dt("x2f_dr", [NT, 128, D], BF16, kind="Internal")
    e1_in = dt("e1_in", [1, 8], F32, kind="Internal")
    e1_out = dt("e1_out", [8, 8], F32, kind="Internal")
    e2_in = dt("e2_in", [128, 8192], mybir.dt.int8, kind="Internal")
    e2_out = dt("e2_out", [1024, 8192], mybir.dt.int8, kind="Internal")
    e3_in = dt("e3_in", [128, 1024], F32, kind="Internal")
    e3_out = dt("e3_out", [1024, 1024], F32, kind="Internal")
    e4_in = dt("e4_in", [128, 8192], BF16, kind="Internal")
    e4_out = dt("e4_out", [1024, 8192], BF16, kind="Internal")
    e5_in = dt("e5_in", [128, 4104], BF16, kind="Internal")
    e5_out = dt("e5_out", [1024, 4104], BF16, kind="Internal")
    btot_dr = dt("btot_dr", [128, 32], F32, kind="Internal")
    dbg_dr = dt("dbg_dr", [8, 128, DV], F32, kind=_K)
    b2_dr = dt("b2_dr", [1, D], F32, kind="Internal")

    DBG = DEBUG_DUMPS
    b = Builder(nc, T)
    sb = b.sb
    st = b.step
    dma = b.dma

    identt = sb("identt", [128, 128], F32)
    maskt = sb("maskt", [128, 128], BF16)
    onesr = sb("onesr", [1, 256], F32)
    epsc = sb("epsc", [128, 2], F32)
    ncols = sb("ncols", [128, 72], F32)
    no_ct = sb("no_ct", [128, 16], F32)
    ngk2_t = sb("ngk2_t", [128, 16], F32)
    ninv_t = sb("ninv_t", [128, 8], F32)
    mx_cols = sb("mx_cols", [128, 8], F32)
    th_cols = sb("th_cols", [128, 8], F32)
    nth_cols = sb("nth_cols", [128, 8], F32)
    m127_cols = sb("m127_cols", [128, 8], F32)
    ccol = sb("ccol", [128, 1], F32)
    c1col = sb("c1col", [128, 1], F32)
    colsA = sb("colsA", [128, 32], F32)
    colsB = sb("colsB", [128, 48], F32)
    colsQ = sb("colsQ", [128, NT], F32)
    eoff_t = sb("eoff_t", [128, 8], F32)
    offb_t = sb("offb_t", [128, 8], F32)
    dprod = sb("dprod", [128, 8], F32)
    gcur = sb("gcur", [128, 8], F32)
    wrk8 = sb("wrk8", [128, 8], F32)
    amask_t = sb("amask_t", [128, 8], F32)
    am1_t = sb("am1_t", [128, 8], F32)
    bmask_t = sb("bmask_t", [128, 8], F32)
    stats = sb("stats", [128, 12], F32)
    btsc = sb("btsc", [128, 2], F32)
    msum = sb("msum", [128, 8], F32)
    y1t = sb("y1t", [128, 16], F32)
    sq16 = sb("sq16", [128, 16], F32)
    t2g = sb("t2g", [128, 16], F32)
    rq2 = sb("rq2", [128, 16], I32)
    u4 = sb("u4", [128, 4 * 64], F32)
    tsc = u4
    pmx = u4[:, 0:128]
    y1sb = u4[0:16, 0:128]
    srow = u4[0:1, 0:256]
    e1sb = u4[0:8, 192:200]
    prow = u4[0:1, 200:216]
    dmod = wrk8
    S_f = sb("S_f", [128, 4096], F32)
    rint = sb("rint", [128, 1024], mybir.dt.int16)
    xq_a = sb("xq_a", [128, D], BF16)
    xq_b = sb("xq_b", [128, D], BF16)
    q_fm = sb("q_fm", [128, DK], BF16)
    k_fm = sb("k_fm", [128, DK], BF16)
    qt_h = sb("qt_h", [128, 256], BF16)
    kt_h = sb("kt_h", [128, 256], BF16)
    qb_h = sb("qb_h", [128, 256], BF16)
    wgk1_ar = sb("wgk1_ar", [128, 8 * R], BF16)
    xq2_tm = sb("xq2_tm", [128, 16], BF16)
    xq2_fm = sb("xq2_fm", [16, 128], BF16)

    upb_ct = sb("upb_ct", [128, 32], F32)
    bcon_ct = sb("bcon_ct", [128, 8], F32)
    dnb_ct = sb("dnb_ct", [128, 8], F32)

    SLAB1N = 68608
    slab1 = sb("slab1_mem", [128, SLAB1N], BF16)
    def s1(off, n):
        return slab1[:, off:off + n]
    wq_ar = s1(0, 8192)
    wk_ar = s1(8192, 8192)
    wv_ar = s1(16384, 16384)
    wg_ar = s1(32768, 16384)
    S_bf = s1(49152, 4096)
    p1b = s1(53248, 2048)
    p2b = s1(55296, 2048)
    op_t = s1(53248, 2048)
    sg_tm = s1(55296, 2048)
    h2g0 = s1(67584, 512)
    h2g1 = s1(68096, 512)
    xq_c = s1(59392, 1024)
    xq_d = s1(60416, 1024)
    EsA = s1(61440, 768)
    EsB = s1(62208, 768)
    khtm4 = s1(62976, 1024)
    am4 = s1(64000, 512)
    qh4 = s1(64512, 1024)
    v_tm = s1(57344, 2048)
    qstg = s1(59392, 2048)
    wo_ar = s1(0, 16384)
    Sin_bf = s1(16384, 4096)
    W1_ar = s1(0, 32768)
    W2_ar = s1(32768, 32768)
    xst = s1(65536, 2048)
    wgk2_ar = slab1[0:16, 67584:68608]
    xh2 = q_fm
    xh2fm = k_fm
    b.alias("op_t", ["p1b"])
    b.alias("sg_tm", ["p2b"])
    b.alias("wo_ar", ["wq_ar", "wk_ar"])
    b.alias("Sin_bf", ["wv_ar"])
    b.alias("W1_ar", ["wq_ar", "wk_ar", "wv_ar", "wo_ar", "Sin_bf"])
    b.alias("W2_ar", ["wg_ar", "S_bf0", "S_bf1", "S_bf2", "S_bf3", "op_t", "sg_tm",
                      "v_tm", "qstg", "p1b", "p2b", "xq_c", "xq_d",
                      "EsA", "EsB", "khtm4", "am4", "qh4"])
    b.alias("xq_c", ["qstg"])
    b.alias("xq_d", ["qstg"])

    SLAB2N = 10240
    slab2 = sb("slab2_mem", [128, SLAB2N], F32)
    def s2(off, n):
        return slab2[:, off:off + n]
    wsl_f = s2(0, 2048)
    gk1_f = slab2[:, 2048:2048 + 128]
    bex_t = slab2[:, 640:672]
    bfold = slab2[:, 672:704]
    pfold = s2(2048, 4096)
    bltU = s2(6144, 4096)
    lowr = slab2[0:1, 1024:2048]
    b.alias("lowr", ["wsl_f"])
    lowr2 = slab2[0:1, 0:1024]
    b.alias("lowr2", ["wsl_f"])
    b.alias("fbU0", ["gk1_f"])
    b.alias("uph_t", ["wsl_f"])
    b.alias("dnh_t", ["wsl_f"])
    b.alias("bex_t", ["wsl_f"])
    b.alias("bfold", ["wsl_f"])
    xt = s2(0, 1024)
    xh_tm = s2(1024, 1024)
    xh_fm = s2(2048, 1024)
    beta = s2(3072, 1024)
    sgm = s2(4096, 1024)
    lsg = sgm
    El_t = s2(5120, 1024)
    tp = s2(6144, 1024)
    usc = tp
    Eq_t = s2(7168, 256)
    Ek_t = s2(7424, 256)
    EK_t = s2(7680, 256)
    KHp = slab2[:, 7168:7424]
    ind4 = slab2[0:4, 7936:8448]
    b.alias("ind4", ["bltU"])
    pmA = slab2[:, 8448:8576]
    pmB = slab2[:, 8576:8704]
    b.alias("pmA", ["bltU"])
    b.alias("pmB", ["bltU"])
    uph_t = slab2[:, 0:320]
    dnh_t = slab2[:, 320:640]
    bgk2_t = slab2[0:1, 9216:9216 + 1024]
    b.alias("bgk2_c", ["msum"])
    b.alias("xt", ["wsl_f", "lowr", "lowr2", "uph_t", "dnh_t", "bex_t", "bfold"])
    b.alias("xh_tm", ["wsl_f", "lowr"])
    b.alias("xh_fm", ["pf0"])
    b.alias("pf0", ["fbU0", "fbU1", "fbU2", "fbU3",
                    "fbD0", "fbD1", "fbD2", "fbD3"])
    b.alias("beta", ["pf1"])
    b.alias("sgm", ["pf2"])
    b.alias("El_t", ["pf3"])
    b.alias("tp", ["bltU"])
    b.alias("Eq_t", ["bltU"])
    b.alias("Ek_t", ["bltU"])
    b.alias("EK_t", ["bltU"])
    b.alias("bgk2_t", ["bltU"])
    Sin_f = s2(0, 4096)
    Ur_f = s2(4096, 4104)
    og = s2(0, 2048)
    ogfm = s2(2048, 2048)
    x2t = s2(4096, 1024)
    xtB = s2(5120, 1024)
    xtBodd = s2(7424, 1024)
    b.alias("xtBodd", ["ind4"])
    x2a = s2(0, 1024)
    x2b = s2(1024, 1024)
    ya = s2(2048, 1024)
    yb = s2(3072, 1024)
    btot = slab2[0:1, 4096:4096 + 4096]
    b2r = slab2[0:1, 8192:8192 + 1024]

    PA = b.ps("PA", [128, 1024])
    PB = b.ps("PB", [128, 1024])
    PC = b.ps("PC", [128, 512])
    PD = b.ps("PD", [128, 512])
    PE = b.ps("PE", [128, 512])
    PF = b.ps("PF", [128, 512])

    dma("sync", identt[:], ident.ap(), writes=["identt"])
    dma("gpsimd", maskt[:], trimask.ap(), writes=["maskt"])
    for i_, src in enumerate([nq_c, nk_c, nv_c, ng_c, ngk1_c]):
        dma("sync", ncols[:, i_ * 8:(i_ + 1) * 8], src.ap(), writes=["ncols"])
    dma("sync", no_ct[:], no_c.ap(), writes=["no_ct"])
    dma("sync", ngk2_t[:], ngk2_r.ap(), writes=["ngk2_t"])
    dma("sync", ninv_t[:], ninv_row.ap(), writes=["ninv_t"])
    dma("sync", amask_t[:], amask.ap(), writes=["amask_t"])
    dma("sync", am1_t[:], am1.ap(), writes=["am1_t"])
    dma("sync", bmask_t[:], bmask.ap(), writes=["bmask_t"])
    dma("sync", upb_ct[:], upb_c.ap(), writes=["upb_ct"])
    dma("sync", bcon_ct[:], bcon_c.ap(), writes=["bcon_ct"])
    dma("sync", dnb_ct[:], dnb_c.ap(), writes=["dnb_ct"])

    st("vector", lambda e: e.scalar_tensor_tensor(
        ncols[:, 40:72], ncols[:, 0:32], -1.0, ncols[:, 0:32], ALU.mult, ALU.max),
       reads=["ncols"], writes=["ncols"])
    st("vector", lambda e: e.memset(onesr[:], 1.0), writes=["onesr"])
    def _eps(e):
        e.memset(epsc[:, 0:1], 1e-5)
        return e.memset(epsc[:, 1:2], 1e-8)
    st("vector", _eps, writes=["epsc"])
    st("vector", lambda e: e.memset(colsB[:, 39:40], 1.0), writes=["colsB"])
    st("vector", lambda e: e.memset(S_f[:], 0.0),
       writes=["S_f0", "S_f1", "S_f2", "S_f3"])
    st("vector", lambda e: e.memset(dprod[:], 1.0), writes=["dprod"])
    st("vector", lambda e: e.memset(gcur[:], 1.0), writes=["gcur"])
    st("vector", lambda e: e.memset(offb_t[:], 0.0), writes=["offb_t"])

    slmap = [(wqT_sl, DK, 0), (wkT_sl, DK, 1), (wvT_sl, DV, 2),
             (wgT_sl, DV, 3), (woT_sl, 2 * D, 4)]
    for src, w, slot in slmap:
        if w <= 2048:
            dma("sync", wsl_f[:, 0:w], src.ap(), writes=["wsl_f"], tag="p1ld%d" % slot)
            st("vector", (lambda w=w, slot=slot: (lambda e: e.reduce_sum(
                colsA[:, slot:slot + 1], wsl_f[:, 0:w], axis=AX.X,
                apply_absolute_value=True)))(),
               reads=["wsl_f"], writes=["colsA"])
    dma("sync", gk1_f[:], wgk1T_f.ap(), writes=["gk1_f"])
    st("vector", lambda e: e.reduce_sum(colsA[:, 5:6], gk1_f[:], axis=AX.X,
                                        apply_absolute_value=True),
       reads=["gk1_f"], writes=["colsA"])
    dma("sync", wsl_f[0:16, 0:DK], wgk2T_f.ap(), writes=["wsl_f"], tag="p1ldgk2")
    st("vector", lambda e: e.reduce_sum(colsA[0:16, 6:7], wsl_f[0:16, 0:DK], axis=AX.X,
                                        apply_absolute_value=True),
       reads=["wsl_f"], writes=["colsA"])

    def _summms(e):
        for slot in range(6):
            e.matmul(PC[0:1, slot:slot + 1], colsB[:, 39:40],
                     colsA[:, slot:slot + 1], start=True, stop=True)
        e.matmul(PC[0:1, 6:7], colsB[0:16, 39:40], colsA[0:16, 6:7],
                 start=True, stop=True)
        return e.matmul(PC[0:1, 7:8], colsB[0:16, 39:40], colsA[0:16, 6:7],
                        start=True, stop=True)
    st("tensor", _summms, reads=["colsA", "colsB"], writes=["PC"])
    st("vector", lambda e: e.tensor_copy(prow[:, 0:8], PC[0:1, 0:8]),
       reads=["PC"], writes=["u4"])
    dma("sync", e1_in.ap(), prow[:, 0:8], reads=["u4"], writes=["e1_in_d"])
    st("gpsimd", lambda e: e.collective_compute(
        "AllGather", ALU.bypass, ins=[e1_in.ap()], outs=[e1_out.ap()],
        replica_groups=[list(range(NCORES))]),
        reads=["e1_in_d"], writes=["e1_out_d"], sem="cc")
    dma("sync", e1sb[:], e1_out.ap(), reads=["e1_out_d"], writes=["u4"])
    st("tensor", lambda e: e.matmul(PC[0:1, 8:16], colsB[0:8, 39:40], e1sb[:],
                                    start=True, stop=True),
       reads=["u4", "colsB"], writes=["PC"])
    st("vector", lambda e: e.tensor_copy(prow[:, 8:16], PC[0:1, 8:16]),
       reads=["PC"], writes=["u4"])
    st("tensor", lambda e: e.matmul(PD[:, 0:8], onesr[0:1, 0:128], prow[:, 8:16],
                                    start=True, stop=True),
       reads=["u4", "onesr"], writes=["PD"])
    st("vector", lambda e: e.tensor_copy(msum[:], PD[:, 0:8]),
       reads=["PD"], writes=["msum"])
    st("vector", lambda e: e.tensor_mul(mx_cols[:], msum[:], ninv_t[:]),
       reads=["msum", "ninv_t"], writes=["mx_cols"])
    st("vector", lambda e: e.tensor_scalar_max(mx_cols[:], mx_cols[:], 1e-5),
       reads=["mx_cols"], writes=["mx_cols"])
    st("vector", lambda e: e.tensor_scalar_mul(th_cols[:], mx_cols[:], 0.5),
       reads=["mx_cols"], writes=["th_cols"])
    st("vector", lambda e: e.tensor_scalar_mul(nth_cols[:], th_cols[:], -1.0),
       reads=["th_cols"], writes=["nth_cols"])
    st("vector", lambda e: e.tensor_scalar_mul(m127_cols[:], mx_cols[:], 1.0 / 127.0),
       reads=["mx_cols"], writes=["m127_cols"])
    st("vector", lambda e: e.memset(ccol[:], 1.0 / 16.0), writes=["ccol"])

    e2off = {0: 0, 1: 1024, 2: 2048, 3: 4096, 4: 6144}
    for src, w, slot in slmap:
        dma("sync", wsl_f[:, 0:w], src.ap(), writes=["wsl_f"], tag="p2ld%d" % slot)
        st("vector", (lambda w=w, slot=slot: (lambda e: e.tensor_single_scalar(
            p1b[:, 0:w], wsl_f[:, 0:w], th_cols[:, slot:slot + 1], ALU.is_ge)))(),
           reads=["wsl_f", "th_cols"], writes=["p1b"])
        st("vector", (lambda w=w, slot=slot: (lambda e: e.tensor_single_scalar(
            p2b[:, 0:w], wsl_f[:, 0:w], nth_cols[:, slot:slot + 1], ALU.is_le)))(),
           reads=["wsl_f", "nth_cols"], writes=["p2b"])
        st("vector", (lambda w=w: (lambda e: e.tensor_sub(
            qstg[:, 0:w], p1b[:, 0:w], p2b[:, 0:w])))(),
           reads=["p1b", "p2b"], writes=["qstg"])
        dma("gpsimd", e2_in.ap()[:, e2off[slot]:e2off[slot] + w], qstg[:, 0:w],
            reads=["qstg"], writes=["e2_in_d"], tag="p2st%d" % slot)
    st("vector", lambda e: e.tensor_single_scalar(p1b[:, 0:128], gk1_f[:],
                                                  th_cols[:, 5:6], ALU.is_ge),
       reads=["gk1_f", "th_cols"], writes=["p1b"])
    st("vector", lambda e: e.tensor_single_scalar(p2b[:, 0:128], gk1_f[:],
                                                  nth_cols[:, 5:6], ALU.is_le),
       reads=["gk1_f", "nth_cols"], writes=["p2b"])
    st("vector", lambda e: e.tensor_sub(wgk1_ar[:], p1b[:, 0:128], p2b[:, 0:128]),
       reads=["p1b", "p2b"], writes=["wgk1_ar"])
    dma("sync", wsl_f[0:16, 0:DK], wgk2T_f.ap(), writes=["wsl_f"], tag="p2ldgk2")
    st("vector", lambda e: e.tensor_single_scalar(p1b[0:16, 0:DK], wsl_f[0:16, 0:DK],
                                                  th_cols[0:16, 6:7], ALU.is_ge),
       reads=["wsl_f", "th_cols"], writes=["p1b"])
    st("vector", lambda e: e.tensor_single_scalar(p2b[0:16, 0:DK], wsl_f[0:16, 0:DK],
                                                  nth_cols[0:16, 6:7], ALU.is_le),
       reads=["wsl_f", "nth_cols"], writes=["p2b"])
    st("vector", lambda e: e.tensor_sub(wgk2_ar[:], p1b[0:16, 0:DK], p2b[0:16, 0:DK]),
       reads=["p1b", "p2b"], writes=["wgk2_ar"])

    st("gpsimd", lambda e: e.collective_compute(
        "AllGather", ALU.bypass, ins=[e2_in.ap()], outs=[e2_out.ap()],
        replica_groups=[list(range(NCORES))]),
        reads=["e2_in_d"], writes=["e2_out_d"], sem="cc")

    dma("sync", uph_t, uph_w.ap(), writes=["uph_t"], tag="lduph")
    dma("sync", dnh_t, dnh_w.ap(), writes=["dnh_t"], tag="lddnh")

    for bfi, (low_dram, stoff) in enumerate([(lowU_sl, 0), (lowD_sl, 512)]):
        fbn = lambda j, bfi=bfi: "fb%s%d" % ("UD"[bfi], j)
        def _init(e, stoff=stoff):
            last = None
            for j in range(4):
                last = e.tensor_copy(pfold[:, stoff + j * 128: stoff + (j + 1) * 128],
                                     identt[:])
            return last
        st("vector", _init, reads=["identt"], writes=[fbn(j) for j in range(4)])
        for s in range(7):
            sig = 2 ** s
            lr, lrn = (lowr, "lowr") if s % 2 == 0 else (lowr2, "lowr2")
            dma("sync", lr[:], low_dram.ap()[:, s * 1024:(s + 1) * 1024],
                writes=[lrn], tag="lowld%d_%d" % (bfi, s))
            for j in range(4):
                base = (j * 4) * 64
                Pj, Pjn = (PC, "PC") if j % 2 == 0 else (PD, "PD")
                Sj, Sjn = (PE, "PE") if j % 2 == 0 else (PF, "PF")
                def _wmm(e, base=base, lr=lr, Pj=Pj):
                    last = None
                    for w in range(4):
                        last = e.matmul(Pj[:, w * 64:(w + 1) * 64],
                                        onesr[0:1, 0:128],
                                        lr[0:1, base + w * 64: base + (w + 1) * 64],
                                        start=True, stop=True)
                    return last
                st("tensor", _wmm, reads=[lrn, "onesr"], writes=[Pjn])
                blk = pfold[:, stoff + j * 128: stoff + (j + 1) * 128]
                bv = blk.rearrange("p (u two t) -> p u two t", two=2, t=sig)
                av, bvv = bv[:, :, 0, :], bv[:, :, 1, :]
                def _umul(e, av=av, bvv=bvv, sig=sig, Pj=Pj, Sj=Sj):
                    e.tensor_mul(u4[:, 0:64].rearrange("p (u t) -> p u t", t=sig), av,
                                 Pj[:, 0:64].rearrange("p (u t) -> p u t", t=sig))
                    e.tensor_mul(Sj[:, 0:64].rearrange("p (u t) -> p u t", t=sig), bvv,
                                 Pj[:, 64:128].rearrange("p (u t) -> p u t", t=sig))
                    e.tensor_mul(u4[:, 128:192].rearrange("p (u t) -> p u t", t=sig), av,
                                 Pj[:, 128:192].rearrange("p (u t) -> p u t", t=sig))
                    return e.tensor_mul(Sj[:, 64:128].rearrange("p (u t) -> p u t", t=sig), bvv,
                                        Pj[:, 192:256].rearrange("p (u t) -> p u t", t=sig))
                st("vector", _umul, reads=[fbn(j), Pjn], writes=["u4", Sjn])
                def _upd(e, av=av, bvv=bvv, sig=sig, Sj=Sj):
                    e.tensor_add(av, u4[:, 0:64].rearrange("p (u t) -> p u t", t=sig),
                                 Sj[:, 0:64].rearrange("p (u t) -> p u t", t=sig))
                    return e.tensor_add(bvv, u4[:, 128:192].rearrange("p (u t) -> p u t", t=sig),
                                        Sj[:, 64:128].rearrange("p (u t) -> p u t", t=sig))
                st("vector", _upd, reads=["u4", Sjn], writes=[fbn(j)])
    dma("sync", e3_in.ap(), pfold[:, 0:1024],
        reads=["fbU%d" % j for j in range(4)] + ["fbD%d" % j for j in range(4)],
        writes=["e3_in_d"])
    st("gpsimd", lambda e: e.collective_compute(
        "AllGather", ALU.bypass, ins=[e3_in.ap()], outs=[e3_out.ap()],
        replica_groups=[list(range(NCORES))]),
        reads=["e3_in_d"], writes=["e3_out_d"], sem="cc")
    for r_ in range(8):
        dma("sync", bltU.rearrange("p (r j t) -> p r j t", r=8, j=4)[:, r_],
            e3_out.ap()[r_ * 128:(r_ + 1) * 128, 0:512].rearrange("p (j t) -> p j t", j=4),
            reads=["e3_out_d"], writes=["bltU", "lowr"], tag="unpBltU%d" % r_)

    def high_pairs(sidx):
        sig = 2 ** sidx
        return [(u, u + sig) for u in range(32) if (u >> sidx) & 1 == 0]

    def wcol(tbl, sidx, pi, w):
        idx = (sidx * 16 + pi) * 4 + w
        return tbl[:, idx:idx + 1]

    def fold_high(tbl, tbln, reverse, swap):
        order = range(4, -1, -1) if reverse else range(5)
        for sidx in order:
            for pi, (ua, ub) in enumerate(high_pairs(sidx)):
                A = pfold[:, ua * 128:(ua + 1) * 128]
                Bc = pfold[:, ub * 128:(ub + 1) * 128]
                w00 = wcol(tbl, sidx, pi, 0)
                w01 = wcol(tbl, sidx, pi, 2 if swap else 1)
                w10 = wcol(tbl, sidx, pi, 1 if swap else 2)
                w11 = wcol(tbl, sidx, pi, 3)
                ts_, tsn = (tsc, "u4") if pi % 2 == 0 else (PF[:, 0:256], "PF")
                pn = ["pf%d" % (ua // 8), "pf%d" % (ub // 8)]
                st("vector", (lambda A=A, Bc=Bc, w01=w01, w10=w10, ts_=ts_: (lambda e: (
                    e.tensor_single_scalar(ts_[:, 0:128], Bc, w01, ALU.mult),
                    e.tensor_single_scalar(ts_[:, 128:256], A, w10, ALU.mult))[-1]))(),
                   reads=pn + [tbln], writes=[tsn])
                st("vector", (lambda A=A, Bc=Bc, w00=w00, w11=w11, ts_=ts_: (lambda e: (
                    e.scalar_tensor_tensor(A, A, w00, ts_[:, 0:128], ALU.mult, ALU.add),
                    e.scalar_tensor_tensor(Bc, Bc, w11, ts_[:, 128:256], ALU.mult, ALU.add))[-1]))(),
                   reads=[tsn, tbln], writes=pn)

    dma("sync", pfold.rearrange("p (u d) -> p u d", u=32),
        wexpT_sl.ap().rearrange("(u p) d -> p u d", p=128),
        writes=["pf0", "pf1", "pf2", "pf3"], tag="ldwexp")
    for unp_e, (nm_, ar_, off_, w_) in zip(
            ("gpsimd", "gpsimd", "gpsimd", "gpsimd"),
            [("wq_ar", wq_ar, 0, DK), ("wk_ar", wk_ar, 1024, DK),
             ("wv_ar", wv_ar, 2048, DV), ("wg_ar", wg_ar, 4096, DV)]):
        dma(unp_e,
            ar_.rearrange("p (c w) -> p c w", c=8),
            e2_out.ap().rearrange("(c p) w -> p c w", p=128)[:, :, off_:off_ + w_],
            reads=["e2_out_d"], writes=[nm_], tag="unp" + nm_)
    for rnd in range(4):
        def _pmm(e, rnd=rnd):
            last = None
            for j in range(8):
                u = rnd * 8 + j
                last = e.matmul(PA[:, j * 128:(j + 1) * 128],
                                bltU[:, u * 128:(u + 1) * 128],
                                pfold[:, u * 128:(u + 1) * 128],
                                start=True, stop=True)
            return last
        st("tensor", _pmm, reads=["bltU", "pf%d" % rnd], writes=["PA"], tag="w1p%d" % rnd)
        st("scalar", (lambda rnd=rnd: (lambda e: e.activation(
            pfold[:, rnd * 1024:(rnd + 1) * 1024], PA[:], AF.Copy)))(),
           reads=["PA"], writes=["pf%d" % rnd], tag="w1pe%d" % rnd)
    fold_high(uph_t, "uph_t", reverse=False, swap=False)
    for rnd in range(4):
        def _w1t(e, rnd=rnd):
            last = None
            for j in range(8):
                u = rnd * 8 + j
                last = e.transpose(PA[:, j * 128:(j + 1) * 128],
                                   pfold[:, u * 128:(u + 1) * 128], identt[:])
            return last
        st("tensor", _w1t, reads=["pf%d" % rnd, "identt"], writes=["PA"], tag="w1t%d" % rnd)
        st("scalar", (lambda rnd=rnd: (lambda e: e.activation(
            qstg[:, 0:1024], PA[:], AF.Copy)))(),
           reads=["PA"], writes=["qstg"])
        dma("sync", e4_in.ap()[:, rnd * 1024:(rnd + 1) * 1024], qstg[:, 0:1024],
            reads=["qstg"], writes=["e4_in_d"], tag="w1st%d" % rnd)

    dma("sync", bex_t, bexp_c.ap(), writes=["bex_t"], tag="ldbex")
    def _bmm(e):
        last = None
        for u in range(32):
            last = e.matmul(PC[:, u:u + 1], bltU[:, u * 128:(u + 1) * 128],
                            bex_t[:, u:u + 1], start=True, stop=True)
        return last
    st("tensor", _bmm, reads=["bltU", "bex_t"], writes=["PC"])
    st("vector", lambda e: e.tensor_copy(bfold, PC[:, 0:32]),
       reads=["PC"], writes=["bfold"])
    for sidx in range(5):
        sig = 2 ** sidx
        bvw = bfold.rearrange("p (q two r) -> p q two r", two=2, r=sig)
        av, bv = bvw[:, :, 0, :], bvw[:, :, 1, :]
        wv = uph_t[:, sidx * 64:(sidx + 1) * 64].rearrange(
            "p (q r w) -> p w q r", r=sig, w=4)
        w00, w01, w10, w11 = wv[:, 0], wv[:, 1], wv[:, 2], wv[:, 3]
        u3 = lambda a, b: u4[:, a:b].rearrange("p (q r) -> p q r", r=sig)
        def _bs1(e, av=av, bv=bv, w01=w01, w10=w10, u3=u3):
            e.tensor_mul(u3(0, 16), bv, w01)
            return e.tensor_mul(u3(16, 32), av, w10)
        st("vector", _bs1, reads=["bfold", "uph_t"], writes=["u4"])
        def _bs2(e, av=av, bv=bv, w00=w00, w11=w11, u3=u3):
            e.tensor_mul(u3(32, 48), av, w00)
            return e.tensor_mul(u3(48, 64), bv, w11)
        st("vector", _bs2, reads=["bfold", "uph_t"], writes=["u4"])
        def _bs3(e, av=av, bv=bv, u3=u3):
            e.tensor_add(av, u3(32, 48), u3(0, 16))
            return e.tensor_add(bv, u3(48, 64), u3(16, 32))
        st("vector", _bs3, reads=["u4"], writes=["bfold"])
    st("vector", lambda e: e.tensor_add(bfold, bfold, upb_ct[:]),
       reads=["bfold", "upb_ct"], writes=["bfold"])
    dma("sync", btot_dr.ap(), bfold, reads=["bfold"], writes=["btot_d"])
    st("vector", lambda e: e.tensor_add(bfold[:, 0:8], bcon_ct[:], dnb_ct[:]),
       reads=["bcon_ct", "dnb_ct"], writes=["bfold"])
    st("tensor", lambda e: e.transpose(PD[0:8, 0:128], bfold[:, 0:8], identt[:]),
       reads=["bfold", "identt"], writes=["PD"])
    st("vector", lambda e: e.tensor_copy(pmx[0:8, :], PD[0:8, 0:128]),
       reads=["PD"], writes=["u4"])
    dma("sync", b2_dr.ap().rearrange("o (u p) -> o u p", u=8), pmx[0:8, :],
        reads=["u4"], writes=["b2_d"])

    for r_ in range(8):
        dma("sync", bltU.rearrange("p (r j t) -> p r j t", r=8, j=4)[:, r_],
            e3_out.ap()[r_ * 128:(r_ + 1) * 128, 512:1024].rearrange("p (j t) -> p j t", j=4),
            reads=["e3_out_d"], writes=["bltU"], tag="unpBltD%d" % r_)
    for rnd in range(4):
        def _trs(e, rnd=rnd):
            last = None
            for j in range(8):
                u = rnd * 8 + j
                last = e.transpose(PA[:, j * 128:(j + 1) * 128],
                                   bltU[:, u * 128:(u + 1) * 128], identt[:])
            return last
        st("tensor", _trs, reads=["bltU", "identt"], writes=["PA"], tag="bltT%d" % rnd)
        st("scalar", (lambda rnd=rnd: (lambda e: e.activation(
            bltU[:, rnd * 1024:(rnd + 1) * 1024], PA[:], AF.Copy)))(),
           reads=["PA"], writes=["bltU"])
    dma("sync", pfold.rearrange("p (u o) -> p u o", u=32),
        wcon_sl.ap().rearrange("(u p) o -> p u o", p=128),
        writes=["pf0", "pf1", "pf2", "pf3"], tag="ldwcon")
    fold_high(dnh_t, "dnh_t", reverse=True, swap=True)
    for rnd in range(4):
        def _w2mm(e, rnd=rnd):
            last = None
            for j in range(8):
                u = rnd * 8 + j
                last = e.matmul(PA[:, j * 128:(j + 1) * 128],
                                bltU[:, u * 128:(u + 1) * 128],
                                pfold[:, u * 128:(u + 1) * 128],
                                start=True, stop=True)
            return last
        st("tensor", _w2mm, reads=["bltU", "pf%d" % rnd], writes=["PA"], tag="w2m%d" % rnd)
        st("scalar", (lambda rnd=rnd: (lambda e: e.activation(
            qstg[:, 0:1024], PA[:], AF.Copy)))(),
           reads=["PA"], writes=["qstg"])
        dma("sync", e4_in.ap()[:, 4096 + rnd * 1024: 4096 + (rnd + 1) * 1024],
            qstg[:, 0:1024], reads=["qstg"], writes=["e4_in_d"], tag="w2st%d" % rnd)


    st("gpsimd", lambda e: e.memset(S_bf[:], 0.0),
       writes=["S_bf0", "S_bf1", "S_bf2", "S_bf3"])
    dma("sync", ind4[:], ind4_in.ap(), writes=["ind4"], tag="ldind4")

    betav = beta[:].rearrange("p (c t) -> p c t", c=8)
    bmidv = betav[:, :, 63]
    bendv = betav[:, :, 127]

    def quant_fm(nslot, mslot, xq_dst, xq_name, tag, dq_dst, dq_name):
        def _tmul(e):
            last = None
            for c in range(8):
                last = e.tensor_single_scalar(
                    tp[:, c * 128:(c + 1) * 128], xh_fm[:, c * 128:(c + 1) * 128],
                    ncols[:, nslot * 8 + c: nslot * 8 + c + 1], ALU.mult)
            return last
        st("vector", _tmul, reads=["xh_fm", "ncols"], writes=["tp"], tag=tag + "_t")
        st("vector", lambda e: e.reduce_max(
            pmx[:], tp[:, 0:1024].rearrange("p (c t) -> p t c", c=8),
            axis=AX.X, apply_absolute_value=True),
           reads=["tp"], writes=["u4"], tag=tag + "_pm")
        st("tensor", lambda e: e.transpose(PD[:, 0:128], pmx[:], identt[:]),
           reads=["u4", "identt"], writes=["PD"], tag=tag + "_pmT")
        st("vector", lambda e: e.reduce_max(colsA[:, 5:6], PD[:, 0:128], axis=AX.X),
           reads=["PD"], writes=["colsA"])
        st("vector", lambda e: e.tensor_scalar_max(colsA[:, 6:7], colsA[:, 5:6], 1e-5),
           reads=["colsA"], writes=["colsA"])
        st("vector", lambda e: e.reciprocal(colsA[:, 7:8], colsA[:, 6:7]),
           reads=["colsA"], writes=["colsA"])
        st("vector", lambda e: e.tensor_scalar_mul(colsA[:, 8:9], colsA[:, 7:8], 127.0),
           reads=["colsA"], writes=["colsA"])
        st("vector", (lambda mslot=mslot, dq_dst=dq_dst: (lambda e: e.tensor_mul(
            dq_dst, colsA[:, 6:7], m127_cols[:, mslot:mslot + 1])))(),
           reads=["colsA", "m127_cols"], writes=[dq_name])
        st("tensor", lambda e: e.transpose(PD[0:1, 128:256], colsA[:, 8:9], identt[:]),
           reads=["colsA", "identt"], writes=["PD"])
        st("vector", lambda e: e.tensor_copy(srow[0:1, 0:128], PD[0:1, 128:256]),
           reads=["PD"], writes=["u4"])
        st("tensor", lambda e: e.matmul(PD[:, 256:384], onesr[0:1, 0:128],
                                        srow[0:1, 0:128], start=True, stop=True),
           reads=["u4", "onesr"], writes=["PD"])
        def _usc(e):
            last = None
            for c in range(8):
                last = e.tensor_mul(usc[:, c * 128:(c + 1) * 128],
                                    tp[:, c * 128:(c + 1) * 128], PD[:, 256:384])
            return last
        st("vector", _usc, reads=["tp", "PD"], writes=["tp"])
        st("vector", lambda e: e.tensor_scalar_add(rint[:, 0:1024], usc[:, 0:1024], 256.0),
           reads=["tp"], writes=["rint"])
        st("vector", lambda e: e.scalar_tensor_tensor(
            xq_dst[:, 0:1024], rint[:, 0:1024], -255.5, usc[:, 0:1024],
            ALU.add, ALU.is_le),
           reads=["rint", "tp"], writes=[xq_name])
        st("vector", lambda e: e.scalar_tensor_tensor(
            xq_dst[:, 0:1024], rint[:, 0:1024], -256.0, xq_dst[:, 0:1024],
            ALU.add, ALU.add),
           reads=["rint", xq_name], writes=[xq_name], tag=tag + "_dq")

    def _projfm(e, war, xqt, P):
        last = None
        for m in range(8):
            for kk in range(8):
                last = e.matmul(P[:, m * 128:(m + 1) * 128],
                                war[:, kk * 1024 + m * 128: kk * 1024 + (m + 1) * 128],
                                xqt[:, kk * 128:(kk + 1) * 128],
                                start=(kk == 0), stop=(kk == 7))
        return last

    def _projtm(e, war, xqt):
        last = None
        pslices = [PA[:, 0:512], PA[:, 512:1024], PB[:, 0:512], PB[:, 512:1024]]
        for kk in range(8):
            for n in range(4):
                last = e.matmul(pslices[n],
                                xqt[:, kk * 128:(kk + 1) * 128],
                                war[:, kk * 2048 + n * 512: kk * 2048 + (n + 1) * 512],
                                start=(kk == 0), stop=(kk == 7))
        return last

    SITES = [("q", 0, 0, xq_a, "xq_a"),
             ("k", 1, 1, xq_b, "xq_b"),
             ("v", 2, 2, xq_c, "xq_c"),
             ("g", 3, 3, xq_d, "xq_d")]

    def S0(ti):
        tg = "A%d" % ti
        dma("sync", xt[:], xs.ap()[ti * 128:(ti + 1) * 128, :], writes=["xt"], tag=tg)
        def _bns(e):
            e.bn_stats(stats[:, 0:6], xt[:, 0:512])
            return e.bn_stats(stats[:, 6:12], xt[:, 512:1024])
        st("vector", _bns, reads=["xt"], writes=["stats"])
        st("vector", lambda e: e.bn_aggr(colsA[:, 0:2], stats[:]),
           reads=["stats"], writes=["colsA"])
        st("scalar", lambda e: e.activation(colsA[:, 2:3], colsA[:, 1:2], AF.Ln,
                                            bias=epsc[:, 0:1]),
           reads=["colsA", "epsc"], writes=["colsA"])
        st("scalar", lambda e: e.activation(colsA[:, 3:4], colsA[:, 2:3], AF.Exp,
                                            scale=-0.5),
           reads=["colsA"], writes=["colsA"])
        st("vector", lambda e: e.tensor_scalar(colsA[:, 4:5], colsA[:, 0:1],
                                               colsA[:, 3:4], -1.0, ALU.mult, ALU.mult),
           reads=["colsA"], writes=["colsA"])
        st("vector", lambda e: e.tensor_scalar(xh_tm[:], xt[:], colsA[:, 3:4],
                                               colsA[:, 4:5], ALU.mult, ALU.add),
           reads=["xt", "colsA"], writes=["xh_tm"])

    def S1(ti):
        tg = "A%d" % ti
        def _xT(e):
            last = None
            for c in range(8):
                last = e.transpose(PB[:, c * 128:(c + 1) * 128],
                                   xh_tm[:, c * 128:(c + 1) * 128], identt[:])
            return last
        st("tensor", _xT, reads=["xh_tm", "identt"], writes=["PB"])
        st("scalar", lambda e: e.activation(xh_fm[:], PB[:], AF.Copy),
           reads=["PB"], writes=["xh_fm"])
        if ti == 0 and DBG:
            dma("sync", dbg_dr.ap()[0][:, 0:1024], xh_tm[:], reads=["xh_tm"], writes=["dbg0"])
            dma("sync", dbg_dr.ap()[1][:, 0:1024], xh_fm[:], reads=["xh_fm"], writes=["dbg1"])

    def S2_chunks(ti):
        tg = "A%d" % ti
        chunks = []
        def _stage1(j):
            snm, nslot, mslot, xq, xqn = SITES[j]
            tb, tbn = (tp, "tp") if j % 2 == 0 else (sgm, "sgm")
            def _tmul(e, nslot=nslot, tb=tb):
                last = None
                for c in range(8):
                    last = e.tensor_single_scalar(
                        tb[:, c * 128:(c + 1) * 128], xh_fm[:, c * 128:(c + 1) * 128],
                        ncols[:, nslot * 8 + c: nslot * 8 + c + 1], ALU.mult)
                return last
            st("vector", _tmul, reads=["xh_fm", "ncols"], writes=[tbn], tag=tg + snm + "_t")
        def _stage1pm(j):
            snm, nslot, mslot, xq, xqn = SITES[j]
            tb, tbn = (tp, "tp") if j % 2 == 0 else (sgm, "sgm")
            pm, pmn = (pmA, "pmA") if j < 2 else (pmB, "pmB")
            st("vector", (lambda tb=tb, pm=pm: (lambda e: e.reduce_max(
                pm, tb[:, 0:1024].rearrange("p (c t) -> p t c", c=8),
                axis=AX.X, apply_absolute_value=True)))(),
               reads=[tbn], writes=[pmn], tag=tg + snm + "_pm")
            st("tensor", (lambda pm=pm, j=j: (lambda e: e.transpose(
                PC[:, j * 128:(j + 1) * 128], pm, identt[:])))(),
               reads=[pmn, "identt"], writes=["PC"], tag=tg + snm + "_pmT")
        chunks.append(lambda: (_stage1(0), _stage1pm(0)))
        chunks.append(lambda: (_stage1(2), _stage1pm(2)))
        chunks.append(lambda: (_stage1(1), _stage1pm(1), _stage1(3), _stage1pm(3)))
        def _stage2():
            for j in range(4):
                st("vector", (lambda j=j: (lambda e: e.reduce_max(
                    colsA[:, 8 + j:9 + j], PC[:, j * 128:(j + 1) * 128], axis=AX.X)))(),
                   reads=["PC"], writes=["colsA"])
            st("vector", lambda e: e.tensor_scalar_max(colsA[:, 12:16], colsA[:, 8:12], 1e-5),
               reads=["colsA"], writes=["colsA"])
            st("vector", lambda e: e.reciprocal(colsA[:, 16:20], colsA[:, 12:16]),
               reads=["colsA"], writes=["colsA"])
            st("vector", lambda e: e.tensor_scalar_mul(colsA[:, 20:24], colsA[:, 16:20], 127.0),
               reads=["colsA"], writes=["colsA"])
            dqds = [(colsQ[:, ti:ti + 1], "colsQ"), (colsB[:, 41:42], "dqk"),
                    (colsB[:, 42:43], "dqv"), (colsB[:, 43:44], "dqg")]
            for j, (snm, nslot, mslot, xq, xqn) in enumerate(SITES):
                dqd, dqn = dqds[j]
                st("vector", (lambda j=j, dqd=dqd, mslot=mslot: (lambda e: e.tensor_mul(
                    dqd, colsA[:, 12 + j:13 + j], m127_cols[:, mslot:mslot + 1])))(),
                   reads=["colsA", "m127_cols"], writes=[dqn])
            st("vector", lambda e: e.tensor_mul(colsB[:, 46:47], colsB[:, 41:42],
                                                colsB[:, 42:43]),
               reads=["dqk", "dqv"], writes=["dqkv"])
            st("tensor", lambda e: e.transpose(PC[0:4, 0:128], colsA[:, 20:24], identt[:]),
               reads=["colsA", "identt"], writes=["PC"])
            st("vector", lambda e: e.tensor_copy(u4[0:4, 0:128], PC[0:4, 0:128]),
               reads=["PC"], writes=["u4"])
            def _bc4(e):
                last = None
                for j in range(4):
                    last = e.matmul(PC[:, j * 128:(j + 1) * 128],
                                    ind4[0:4, j * 128:(j + 1) * 128], u4[0:4, 0:128],
                                    start=True, stop=True)
                return last
            st("tensor", _bc4, reads=["u4", "ind4"], writes=["PC"])
        chunks.append(_stage2)
        def _stage4(j):
            snm, nslot, mslot, xq, xqn = SITES[j]
            def _usc(e, nslot=nslot, j=j):
                last = None
                for c in range(8):
                    last = e.scalar_tensor_tensor(
                        tp[:, c * 128:(c + 1) * 128], xh_fm[:, c * 128:(c + 1) * 128],
                        ncols[:, nslot * 8 + c: nslot * 8 + c + 1],
                        PC[:, j * 128:(j + 1) * 128], ALU.mult, ALU.mult)
                return last
            st("vector", _usc, reads=["xh_fm", "ncols", "PC"], writes=["tp"],
               tag=tg + snm + "_u")
            st("vector", lambda e: e.tensor_scalar_add(
                rint[:, 0:1024], tp[:, 0:1024], 256.0),
               reads=["tp"], writes=["rint"])
            st("vector", (lambda xq=xq: (lambda e: e.scalar_tensor_tensor(
                xq[:, 0:1024], rint[:, 0:1024], -255.5, tp[:, 0:1024],
                ALU.add, ALU.is_le)))(),
               reads=["rint", "tp"], writes=[xqn])
            st("vector", (lambda xq=xq: (lambda e: e.scalar_tensor_tensor(
                xq[:, 0:1024], rint[:, 0:1024], -256.0, xq[:, 0:1024],
                ALU.add, ALU.add)))(),
               reads=["rint", xqn], writes=[xqn], tag=tg + snm + "_r")
        for _j in range(4):
            chunks.append((lambda _j=_j: (lambda: _stage4(_j)))())
        return chunks

    def S3q(ti):
        tg = "A%d" % ti
        if ti == 0 and DBG:
            dma("gpsimd", dbg_dr.ap()[2][:, 0:1024], xq_a[:], reads=["xq_a"], writes=["dbg2"])
        st("tensor", lambda e: _projfm(e, wq_ar, xq_a, PA),
           reads=["wq_ar", "xq_a"], writes=["PA"], tag=tg + "qmm")
        st("scalar", lambda e: e.activation(q_fm[:], PA[:], AF.Copy),
           reads=["PA"], writes=["q_fm"])
        if ti == 0 and DBG:
            dma("gpsimd", dbg_dr.ap()[3][:, 0:1024], q_fm[:], reads=["q_fm"], writes=["dbg3"])

    def S3k(ti):
        tg = "A%d" % ti
        st("tensor", lambda e: _projfm(e, wk_ar, xq_b, PB),
           reads=["wk_ar", "xq_b"], writes=["PB"], tag=tg + "kmm")
        st("scalar", lambda e: e.activation(k_fm[:], PB[:], AF.Copy),
           reads=["PB"], writes=["k_fm"])

    def S3v(ti):
        tg = "A%d" % ti
        st("tensor", lambda e: _projtm(e, wv_ar, xq_c),
           reads=["wv_ar", "xq_c"], writes=["PA", "PB"], tag=tg + "vmm")
        def _vev(e):
            e.activation(v_tm[:, 0:1024], PA[:], AF.Copy, scale=colsB[:, 46:47])
            return e.activation(v_tm[:, 1024:2048], PB[:], AF.Copy,
                                scale=colsB[:, 46:47])
        st("scalar", _vev, reads=["PA", "PB", "dqkv"], writes=["v_tm"])
        if ti == 0 and DBG:
            dma("gpsimd", dbg_dr.ap()[4], v_tm[:], reads=["v_tm"], writes=["dbg4"])

    def S3g(ti):
        tg = "A%d" % ti
        st("tensor", lambda e: _projtm(e, wg_ar, xq_d),
           reads=["wg_ar", "xq_d"], writes=["PA", "PB"], tag=tg + "gmm")
        def _gev(e):
            e.activation(sg_tm[:, 0:1024], PA[:], AF.Silu, scale=colsB[:, 43:44])
            return e.activation(sg_tm[:, 1024:2048], PB[:], AF.Silu,
                                scale=colsB[:, 43:44])
        st("scalar", _gev, reads=["PA", "PB", "dqg"], writes=["sg_tm"])
        dma("sync", sg_dr.ap()[ti], sg_tm[:], reads=["sg_tm"], writes=["sg_dr" + str(ti)], tag=tg + "sgst")

    def S3(ti):
        S3q(ti); S3k(ti); S3v(ti); S3g(ti)

    def S4a(ti):
        tg = "A%d" % ti
        quant_fm(4, 5, xq_a, "xq_a", tg + "g1", colsB[:, 44:45], "dq1")

    def S4b(ti, filler=None):
        tg = "A%d" % ti
        def _y1mm(e):
            last = None
            for kk in range(8):
                last = e.matmul(PD[0:16, 0:128],
                                wgk1_ar[:, kk * 16:(kk + 1) * 16],
                                xq_a[:, kk * 128:(kk + 1) * 128],
                                start=(kk == 0), stop=(kk == 7))
            return last
        st("tensor", _y1mm, reads=["wgk1_ar", "xq_a"], writes=["PD"], tag=tg + "y1")
        st("vector", lambda e: e.tensor_copy(y1sb[:], PD[0:16, 0:128]),
           reads=["PD"], writes=["u4"])
        st("tensor", lambda e: e.transpose(PD[:, 128:144], y1sb[:], identt[0:16, 0:16]),
           reads=["u4", "identt"], writes=["PD"])
        st("vector", lambda e: e.tensor_single_scalar(y1t[:], PD[:, 128:144],
                                                      colsB[:, 44:45], ALU.mult),
           reads=["PD", "dq1"], writes=["y1t"])
        st("vector", lambda e: e.tensor_mul(sq16[:], y1t[:], y1t[:]),
           reads=["y1t"], writes=["sq16"])
        st("vector", lambda e: e.reduce_sum(colsA[:, 10:11], sq16[:], axis=AX.X),
           reads=["sq16"], writes=["colsA"])
        st("vector", lambda e: e.tensor_scalar_mul(colsA[:, 11:12], colsA[:, 10:11], 1.0 / 16.0),
           reads=["colsA"], writes=["colsA"])
        st("scalar", lambda e: e.activation(colsA[:, 12:13], colsA[:, 11:12], AF.Ln,
                                            bias=epsc[:, 1:2]),
           reads=["colsA", "epsc"], writes=["colsA"])
        st("scalar", lambda e: e.activation(colsA[:, 13:14], colsA[:, 12:13], AF.Exp,
                                            scale=-0.5),
           reads=["colsA"], writes=["colsA"])
        st("vector", lambda e: e.tensor_mul(t2g[:], y1t[:], ngk2_t[:]),
           reads=["y1t", "ngk2_t"], writes=["t2g"])
        st("vector", lambda e: e.reduce_max(colsA[:, 14:15], t2g[:], axis=AX.X,
                                            apply_absolute_value=True),
           reads=["t2g"], writes=["colsA"])
        st("vector", lambda e: e.tensor_mul(colsA[:, 15:16], colsA[:, 14:15], colsA[:, 13:14]),
           reads=["colsA"], writes=["colsA"])
        st("vector", lambda e: e.tensor_scalar_max(colsB[:, 32:33], colsA[:, 15:16], 1e-5),
           reads=["colsA"], writes=["colsB"])
        st("vector", lambda e: e.reciprocal(colsB[:, 33:34], colsB[:, 32:33]),
           reads=["colsB"], writes=["colsB"])
        st("vector", lambda e: e.tensor_scalar_mul(colsB[:, 34:35], colsB[:, 33:34], 127.0),
           reads=["colsB"], writes=["colsB"])
        st("vector", lambda e: e.tensor_mul(colsB[:, 35:36], colsA[:, 13:14], colsB[:, 34:35]),
           reads=["colsA", "colsB"], writes=["colsB"])
        st("vector", lambda e: e.tensor_mul(colsB[:, 36:37], colsB[:, 32:33], m127_cols[:, 6:7]),
           reads=["colsB", "m127_cols"], writes=["colsB"])
        st("vector", lambda e: e.tensor_single_scalar(sq16[:], t2g[:], colsB[:, 35:36],
                                                      ALU.mult),
           reads=["t2g", "colsB"], writes=["sq16"])
        st("vector", lambda e: e.tensor_scalar_add(rq2[:], sq16[:], 256.0),
           reads=["sq16"], writes=["rq2"])
        st("vector", lambda e: e.scalar_tensor_tensor(t2g[:], rq2[:], -255.5, sq16[:],
                                                      ALU.add, ALU.is_le),
           reads=["rq2", "sq16"], writes=["t2g"])
        st("vector", lambda e: e.scalar_tensor_tensor(sq16[:], rq2[:], -256.0, t2g[:],
                                                      ALU.add, ALU.add),
           reads=["rq2", "t2g"], writes=["sq16"])
        st("vector", lambda e: e.tensor_single_scalar(sq16[:], sq16[:], colsB[:, 36:37],
                                                      ALU.mult),
           reads=["sq16", "colsB"], writes=["sq16"])
        st("tensor", lambda e: e.transpose(PD[0:16, 384:512], sq16[:], identt[:]),
           reads=["sq16", "identt"], writes=["PD"])
        st("vector", lambda e: e.tensor_copy(xq2_fm[:], PD[0:16, 384:512]),
           reads=["PD"], writes=["xq2_fm"])
        def _zmm(e):
            last = None
            for m in range(8):
                last = e.matmul(PA[:, m * 128:(m + 1) * 128],
                                wgk2_ar[:, m * 128:(m + 1) * 128], xq2_fm[:],
                                start=True, stop=True)
            return last
        st("tensor", _zmm, reads=["wgk2_ar", "xq2_fm"],
           writes=["PA"], tag=tg + "z")
        if filler is not None:
            filler()
        def _sig(e):
            last = None
            for m in range(8):
                last = e.activation(sgm[:, m * 128:(m + 1) * 128],
                                    PA[:, m * 128:(m + 1) * 128], AF.Sigmoid,
                                    bias=msum[:, m:m + 1])
            return last
        st("scalar", _sig, reads=["PA", "bgk2_c"], writes=["sgm"])
        st("scalar", lambda e: e.activation(lsg[:], sgm[:], AF.Ln),
           reads=["sgm"], writes=["sgm"])
        def _cums(e):
            last = None
            for c in range(8):
                last = e.tensor_tensor_scan(beta[:, c * 128:(c + 1) * 128],
                                            lsg[:, c * 128:(c + 1) * 128],
                                            lsg[:, c * 128:(c + 1) * 128],
                                            0.0, ALU.add, ALU.bypass)
            return last
        st("vector", _cums, reads=["sgm", "ones_t"], writes=["beta"], tag=tg + "cum")
        if ti == 0 and DBG:
            dma("sync", dbg_dr.ap()[5][:, 0:1024], beta[:], reads=["beta"], writes=["dbg5"])
        def _bcols(e):
            e.tensor_scalar_mul(colsB[:, 0:8], bmidv, -1.0 / 16.0)
            e.tensor_scalar_mul(colsB[:, 8:16], bmidv, 1.0 / 16.0)
            return e.tensor_scalar_mul(colsB[:, 16:24], bendv, 1.0 / 16.0)
        st("vector", _bcols, reads=["beta"], writes=["colsB"])
        st("scalar", lambda e: e.activation(colsB[:, 24:32], bendv, AF.Exp, scale=1.0 / 16.0),
           reads=["beta"], writes=["colsB_lam"])
        st("scalar", lambda e: e.activation(eoff_t[:], offb_t[:], AF.Exp, scale=1.0 / 16.0),
           reads=["offb_t"], writes=["eoff_t"])
        st("vector", lambda e: e.tensor_add(offb_t[:], offb_t[:], bendv),
           reads=["offb_t", "beta"], writes=["offb_t"])
        st("vector", lambda e: e.tensor_mul(dprod[:], dprod[:], colsB[:, 24:32]),
           reads=["dprod", "colsB_lam"], writes=["dprod"])
        st("scalar", lambda e: e.activation(El_t[:], beta[:], AF.Exp, scale=1.0 / 16.0),
           reads=["beta"], writes=["El_t"])

    def S5(ti):
        tg = "A%d" % ti
        for h in range(H):
            c0, c1 = 2 * h, 2 * h + 1
            hsl = slice(c0 * 128, (c1 + 1) * 128)
            Es, esn = (EsA, "EsA") if h % 2 == 0 else (EsB, "EsB")
            Eq_s, Ek_s, EK_s = Es[:, 0:256], Es[:, 256:512], Es[:, 512:768]
            def _es(e, c0=c0, c1=c1, Eq_s=Eq_s, Ek_s=Ek_s, EK_s=EK_s):
                last = None
                for (dst, sc, boff) in ((Eq_s, 1.0 / 16.0, 0), (Ek_s, -1.0 / 16.0, 8),
                                        (EK_s, -1.0 / 16.0, 16)):
                    e.activation(dst[:, 0:128], beta[:, c0 * 128:(c0 + 1) * 128],
                                 AF.Exp, scale=sc, bias=colsB[:, boff + c0:boff + c0 + 1])
                    last = e.activation(dst[:, 128:256], beta[:, c1 * 128:(c1 + 1) * 128],
                                        AF.Exp, scale=sc, bias=colsB[:, boff + c1:boff + c1 + 1])
                return last
            st("scalar", _es, reads=["beta", "colsB"], writes=[esn], tag=tg + "es%d" % h)
            def _mults(e, hsl=hsl, h=h, Eq_s=Eq_s, Ek_s=Ek_s):
                e.tensor_mul(qt_h[:], q_fm[:, hsl], Eq_s)
                return e.tensor_mul(kt_h[:], k_fm[:, hsl], Ek_s)
            st("vector", _mults, reads=["q_fm", "k_fm", esn],
               writes=["qt_h", "kt_h"], tag=tg + "mu%d" % h)
            def _mults2(e, hsl=hsl, h=h, EK_s=EK_s):
                e.tensor_mul(KHp[:], k_fm[:, hsl], EK_s)
                return e.tensor_mul(qh4[:, h * 256:(h + 1) * 256], q_fm[:, hsl],
                                    El_t[:, hsl])
            st("vector", _mults2, reads=["q_fm", "k_fm", esn, "El_t"],
               writes=["KHp", "qh4"], tag=tg + "mu2_%d" % h)
            def _qbar(e, c0=c0, c1=c1, h=h):
                e.tensor_single_scalar(qb_h[:, 0:128], qh4[:, h * 256:h * 256 + 128],
                                       eoff_t[:, c0:c0 + 1], ALU.mult)
                return e.tensor_single_scalar(qb_h[:, 128:256],
                                              qh4[:, h * 256 + 128:h * 256 + 256],
                                              eoff_t[:, c1:c1 + 1], ALU.mult)
            st("vector", _qbar, reads=["qh4", "eoff_t"], writes=["qb_h"])
            dma("sync", qb_dr.ap()[ti][:, c0 * 128:(c1 + 1) * 128], qb_h[:],
                reads=["qb_h"], writes=["qb_dr%d_%d" % (ti, h)], tag=tg + "qbst%d" % h)
            def _amm(e):
                e.matmul(PD[:, 0:128], kt_h[:, 0:128], qt_h[:, 0:128], start=True, stop=False)
                return e.matmul(PD[:, 0:128], kt_h[:, 128:256], qt_h[:, 128:256],
                                start=False, stop=True)
            st("tensor", _amm, reads=["kt_h", "qt_h"], writes=["PD"], tag=tg + "A%d" % h)
            st("vector", (lambda h=h: (lambda e: e.tensor_mul(
                am4[:, h * 128:(h + 1) * 128], PD[:, 0:128], maskt[:])))(),
               reads=["PD", "maskt"], writes=["am4"])
            def _kT(e):
                e.transpose(PD[:, 128:256], KHp[:, 0:128], identt[:])
                return e.transpose(PD[:, 256:384], KHp[:, 128:256], identt[:])
            st("tensor", _kT, reads=["KHp", "identt"], writes=["PD"])
            st("vector", (lambda h=h: (lambda e: e.tensor_copy(
                khtm4[:, h * 256:(h + 1) * 256], PD[:, 128:384])))(),
               reads=["PD"], writes=["khtm4"])

    def S6(ti, heads):
        tg = "A%d" % ti
        for h in heads:
            c0, c1 = 2 * h, 2 * h + 1
            sfn, sbn = "S_f%d" % h, "S_bf%d" % h
            def _umm(e, h=h):
                e.matmul(PD[:], khtm4[:, h * 256:h * 256 + 128],
                         v_tm[:, h * 512:(h + 1) * 512], start=True, stop=True)
                return e.matmul(PE[:], khtm4[:, h * 256 + 128:h * 256 + 256],
                                v_tm[:, h * 512:(h + 1) * 512], start=True, stop=True)
            st("tensor", _umm, reads=["khtm4", "v_tm"], writes=["PD", "PE"],
               tag=tg + "U%d" % h)
            def _omm(e, h=h, c0=c0, c1=c1):
                e.matmul(PF[:], am4[:, h * 128:(h + 1) * 128],
                         v_tm[:, h * 512:(h + 1) * 512], start=True, stop=False)
                e.matmul(PF[:], qh4[:, h * 256:h * 256 + 128],
                         S_bf[:, c0 * 512:(c0 + 1) * 512], start=False, stop=False)
                return e.matmul(PF[:], qh4[:, h * 256 + 128:h * 256 + 256],
                                S_bf[:, c1 * 512:(c1 + 1) * 512], start=False, stop=True)
            st("tensor", _omm, reads=["am4", "v_tm", "qh4", sbn], writes=["PF"],
               tag=tg + "o%d" % h)
            st("scalar", (lambda h=h, ti=ti: (lambda e: e.activation(
                op_t[:, h * 512:(h + 1) * 512], PF[:], AF.Copy,
                scale=colsQ[:, ti:ti + 1])))(),
               reads=["PF", "colsQ"], writes=["op_t"], tag=tg + "oe%d" % h)
            def _supd(e, c0=c0, c1=c1):
                e.scalar_tensor_tensor(S_f[:, c0 * 512:(c0 + 1) * 512],
                                       S_f[:, c0 * 512:(c0 + 1) * 512],
                                       colsB[:, 24 + c0:24 + c0 + 1], PD[:],
                                       ALU.mult, ALU.add)
                return e.scalar_tensor_tensor(S_f[:, c1 * 512:(c1 + 1) * 512],
                                              S_f[:, c1 * 512:(c1 + 1) * 512],
                                              colsB[:, 24 + c1:24 + c1 + 1], PE[:],
                                              ALU.mult, ALU.add)
            st("vector", _supd, reads=["PD", "PE", "colsB_lam", sfn], writes=[sfn],
               tag=tg + "S%d" % h)
            st("scalar", (lambda c0=c0: (lambda e: e.activation(
                S_bf[:, c0 * 512:(c0 + 2) * 512], S_f[:, c0 * 512:(c0 + 2) * 512],
                AF.Copy)))(),
               reads=[sfn], writes=[sbn], tag=tg + "Sbf%d" % h)

    S0(0); S1(0)
    for ch in S2_chunks(0):
        ch()
    S3(0)
    st("gpsimd", lambda e: e.collective_compute(
        "AllGather", ALU.bypass, ins=[e4_in.ap()], outs=[e4_out.ap()],
        replica_groups=[list(range(NCORES))]),
        reads=["e4_in_d"], writes=["e4_out_d"], sem="cc")
    dma("sync", msum[:], bgk2_row.ap(), writes=["bgk2_c"], tag="ldbgk2")
    S4a(0)
    if NT > 1:
        S0(1); S1(1)
    S4b(0)
    pend = {}
    for ti in range(NT):
        tg = "A%d" % ti
        S5(ti)
        if ti + 1 < NT:
            chunks = pend.pop(ti + 1, None)
            if chunks is None:
                chunks = S2_chunks(ti + 1)
                chunks[0]()
                chunks[1]()
            chunks[2]()
            chunks[3]()
            S6(ti, (0,))
            chunks[4]()
            S6(ti, (1,))
            chunks[5]()
            S6(ti, (2,))
            S3q(ti + 1)
            chunks[6]()
            S6(ti, (3,))
            S3k(ti + 1)
            chunks[7]()
            S3v(ti + 1)
            S3g(ti + 1)
        else:
            S6(ti, (0, 1, 2, 3))
        dma("sync", op_dr.ap()[ti], op_t[:], reads=["op_t"], writes=["op_dr" + str(ti)], tag=tg + "opst")
        if ti + 1 < NT:
            S4a(ti + 1)
            def _filler(ti=ti):
                if ti + 2 < NT:
                    S0(ti + 2)
                    S1(ti + 2)
                    pend[ti + 2] = S2_chunks(ti + 2)
                    pend[ti + 2][0]()
                    pend[ti + 2][1]()
            S4b(ti + 1, _filler)

    b.alias("Sin_f", ["xt", "xh_tm", "xh_fm", "beta"])
    b.alias("Ur_a", ["sgm", "El_t", "tp"])
    b.alias("Ur_b", ["sgm", "El_t", "tp"])
    dma("gpsimd", e5_in.ap()[:, 0:4096], S_f[:],
        reads=["S_f0", "S_f1", "S_f2", "S_f3"], writes=["e5_in_d"])
    st("vector", lambda e: e.memset(Sin_f[:], 0.0), writes=["Sin_f"])
    dma("gpsimd", e5_in.ap()[:, 4096:4104], dprod[:], reads=["dprod"], writes=["e5_in_d"])
    st("gpsimd", lambda e: e.collective_compute(
        "AllGather", ALU.bypass, ins=[e5_in.ap()], outs=[e5_out.ap()],
        replica_groups=[list(range(NCORES))]),
        reads=["e5_in_d"], writes=["e5_out_d"], sem="cc")
    for r in (6, 5, 4, 2, 1, 0):
        dma("gpsimd", Ur_f[:, 0:2048], e5_out.ap()[r * 128:(r + 1) * 128, 0:2048],
            reads=["e5_out_d"], writes=["Ur_a"], tag="DUlda%d" % r)
        dma("gpsimd", Ur_f[:, 2048:4104], e5_out.ap()[r * 128:(r + 1) * 128, 2048:4104],
            reads=["e5_out_d"], writes=["Ur_b"], tag="DUldb%d" % r)
        st("vector", (lambda r=r: (lambda e: e.tensor_single_scalar(
            wrk8[:], gcur[:], bmask_t[:, r:r + 1], ALU.mult)))(),
           reads=["gcur", "bmask_t"], writes=["wrk8"])
        def _acca(e):
            last = None
            for hc in range(4):
                last = e.scalar_tensor_tensor(
                    Sin_f[:, hc * 512:(hc + 1) * 512],
                    Ur_f[:, hc * 512:(hc + 1) * 512],
                    wrk8[:, hc:hc + 1],
                    Sin_f[:, hc * 512:(hc + 1) * 512],
                    ALU.mult, ALU.add)
            return last
        st("vector", _acca, reads=["Ur_a", "wrk8", "Sin_f"], writes=["Sin_f"],
           tag="DUacca%d" % r)
        def _accb(e):
            last = None
            for hc in range(4, 8):
                last = e.scalar_tensor_tensor(
                    Sin_f[:, hc * 512:(hc + 1) * 512],
                    Ur_f[:, hc * 512:(hc + 1) * 512],
                    wrk8[:, hc:hc + 1],
                    Sin_f[:, hc * 512:(hc + 1) * 512],
                    ALU.mult, ALU.add)
            return last
        st("vector", _accb, reads=["Ur_b", "wrk8", "Sin_f"], writes=["Sin_f"],
           tag="DUaccb%d" % r)
        st("vector", (lambda r=r: (lambda e: e.tensor_scalar(
            dmod[:], Ur_f[:, 4096:4104], amask_t[:, r:r + 1], am1_t[:, r:r + 1],
            ALU.mult, ALU.add)))(),
           reads=["Ur_b", "amask_t", "am1_t"], writes=["wrk8"])
        st("vector", lambda e: e.tensor_mul(gcur[:], gcur[:], dmod[:]),
           reads=["gcur", "wrk8"], writes=["gcur"])
    st("gpsimd", lambda e: e.tensor_copy(Sin_bf[:], Sin_f[:]),
       reads=["Sin_f"], writes=["Sin_bf"])

    dma("sync", op_t[:], op_dr.ap()[0], reads=["op_dr0"], writes=["op_t"], tag="B0")
    dma("sync", sg_tm[:], sg_dr.ap()[0], reads=["sg_dr0"], writes=["sg_tm"], tag="B0")
    dma("sync", xq_a[:], qb_dr.ap()[0],
        reads=["qb_dr0_0", "qb_dr0_1", "qb_dr0_2", "qb_dr0_3"],
        writes=["xq_a"], tag="B0")
    for r_ in range(8):
        dma("gpsimd", wo_ar.rearrange("p (r s w) -> p r s w", r=8, s=2)[:, r_],
            e2_out.ap()[r_ * 128:(r_ + 1) * 128, 6144:8192].rearrange("p (s w) -> p s w", s=2),
            reads=["e2_out_d"], writes=["wo_ar"], tag="unpwo%d" % r_)

    b.alias("og", ["Sin_f"])
    b.alias("xtB", ["Ur_a", "Ur_b", "sgm", "El_t"])
    b.alias("ogfm", ["Sin_f"])
    b.alias("x2t", ["Ur_a"])
    b.alias("tp", ["Ur_b"])

    def BL(ti):
        tg = "B%d" % ti
        xb, xbn = (xtB, "xtB") if ti % 2 == 0 else (xtBodd, "xtBodd")
        if ti > 0:
            dma("sync", op_t[:], op_dr.ap()[ti], reads=["op_dr" + str(ti)], writes=["op_t"], tag=tg)
            dma("sync", sg_tm[:], sg_dr.ap()[ti], reads=["sg_dr" + str(ti)], writes=["sg_tm"], tag=tg)
            dma("sync", xq_a[:], qb_dr.ap()[ti],
                reads=["qb_dr%d_0" % ti, "qb_dr%d_1" % ti, "qb_dr%d_2" % ti, "qb_dr%d_3" % ti],
                writes=["xq_a"], tag=tg)
        dma("sync", xb[:], xs.ap()[ti * 128:(ti + 1) * 128, :], writes=[xbn], tag=tg + "x")

    def BC(ti):
        tg = "B%d" % ti
        def _corr(e):
            psl = [PA[:, 0:512], PA[:, 512:1024], PB[:, 0:512], PB[:, 512:1024]]
            last = None
            for h in range(H):
                for cc in range(2):
                    c = 2 * h + cc
                    last = e.matmul(psl[h], xq_a[:, c * 128:(c + 1) * 128],
                                    Sin_bf[:, c * 512:(c + 1) * 512],
                                    start=(cc == 0), stop=(cc == 1))
            return last
        st("tensor", _corr, reads=["xq_a", "Sin_bf"], writes=["PA", "PB"], tag=tg + "corr")

    def B1(ti):
        tg = "B%d" % ti
        def _osum(e, ti=ti):
            e.scalar_tensor_tensor(og[:, 0:1024], PA[:], colsQ[:, ti:ti + 1],
                                   op_t[:, 0:1024], ALU.mult, ALU.add)
            return e.scalar_tensor_tensor(og[:, 1024:2048], PB[:], colsQ[:, ti:ti + 1],
                                          op_t[:, 1024:2048], ALU.mult, ALU.add)
        st("vector", _osum, reads=["op_t", "PA", "PB", "colsQ"], writes=["og"])
        st("vector", lambda e: e.scalar_tensor_tensor(og[:], og[:], ccol[:, 0:1],
                                                      sg_tm[:], ALU.mult, ALU.mult),
           reads=["og", "ccol", "sg_tm"], writes=["og"])
        def _bnso(e):
            last = None
            for c4 in range(4):
                last = e.bn_stats(stats[:, 0:6] if c4 == 0 else stats[:, 6:12] if c4 == 1
                                  else colsB[:, 0:6] if c4 == 2 else colsB[:, 6:12],
                                  og[:, c4 * 512:(c4 + 1) * 512])
            return last
        st("vector", _bnso, reads=["og"], writes=["stats", "colsB"])
        def _bnao(e):
            e.bn_aggr(colsA[:, 0:2], stats[:])
            return e.bn_aggr(colsA[:, 2:4], colsB[:, 0:12])
        st("vector", _bnao, reads=["stats", "colsB"], writes=["colsA"])
        def _msq(e):
            e.tensor_mul(colsA[:, 4:5], colsA[:, 0:1], colsA[:, 0:1])
            return e.tensor_mul(colsA[:, 5:6], colsA[:, 2:3], colsA[:, 2:3])
        st("vector", _msq, reads=["colsA"], writes=["colsA"])
        def _msq2(e):
            e.tensor_add(colsA[:, 6:7], colsA[:, 4:5], colsA[:, 1:2])
            return e.tensor_add(colsA[:, 7:8], colsA[:, 5:6], colsA[:, 3:4])
        st("vector", _msq2, reads=["colsA"], writes=["colsA"])
        st("vector", lambda e: e.tensor_add(colsA[:, 8:9], colsA[:, 6:7], colsA[:, 7:8]),
           reads=["colsA"], writes=["colsA"])
        st("vector", lambda e: e.tensor_scalar_mul(colsA[:, 9:10], colsA[:, 8:9], 0.5),
           reads=["colsA"], writes=["colsA"])
        st("scalar", lambda e: e.activation(colsA[:, 10:11], colsA[:, 9:10], AF.Ln,
                                            bias=epsc[:, 1:2]),
           reads=["colsA", "epsc"], writes=["colsA"])
        st("scalar", lambda e: e.activation(colsA[:, 11:12], colsA[:, 10:11], AF.Exp,
                                            scale=-0.5),
           reads=["colsA"], writes=["colsA"])

    def B2(ti):
        tg = "B%d" % ti
        def _ogT(e):
            last = None
            for c in range(16):
                dst = PA[:, (c % 8) * 128:((c % 8) + 1) * 128] if c < 8 else \
                      PB[:, ((c - 8)) * 128:((c - 8) + 1) * 128]
                last = e.transpose(dst, og[:, c * 128:(c + 1) * 128], identt[:])
            return last
        st("tensor", _ogT, reads=["og", "identt"], writes=["PA", "PB"], tag=tg + "ogT")
        def _ogev(e):
            e.activation(ogfm[:, 0:1024], PA[:], AF.Copy)
            return e.activation(ogfm[:, 1024:2048], PB[:], AF.Copy)
        st("scalar", _ogev, reads=["PA", "PB"], writes=["ogfm"])

    def B3a(ti):
        tg = "B%d" % ti
        for half, (tb, tbn) in enumerate(((tp, "tp"), (x2t, "x2t"))):
            def _tmo(e, half=half, tb=tb):
                last = None
                for c in range(8):
                    cc = half * 8 + c
                    last = e.tensor_single_scalar(tb[:, c * 128:(c + 1) * 128],
                                                  ogfm[:, cc * 128:(cc + 1) * 128],
                                                  no_ct[:, cc:cc + 1], ALU.mult)
                return last
            st("vector", _tmo, reads=["ogfm", "no_ct"], writes=[tbn], tag=tg + "to%d" % half)
            st("vector", (lambda half=half, tb=tb: (lambda e: e.reduce_max(
                pmx[:] if half == 0 else u4[:, 128:256],
                tb[:, 0:1024].rearrange("p (c t) -> p t c", c=8),
                axis=AX.X, apply_absolute_value=True)))(),
               reads=[tbn], writes=["u4"], tag=tg + "pm%d" % half)
        st("vector", lambda e: e.tensor_max(pmx[:], pmx[:], u4[:, 128:256]),
           reads=["u4"], writes=["u4"])
        st("tensor", lambda e: e.transpose(PC[:, 0:128], pmx[:], identt[:]),
           reads=["u4", "identt"], writes=["PC"])
        st("vector", lambda e: e.reduce_max(colsA[:, 12:13], PC[:, 0:128], axis=AX.X),
           reads=["PC"], writes=["colsA"])
        st("vector", lambda e: e.tensor_mul(colsA[:, 13:14], colsA[:, 12:13], colsA[:, 11:12]),
           reads=["colsA"], writes=["colsA"])
        st("vector", lambda e: e.tensor_scalar_max(colsA[:, 14:15], colsA[:, 13:14], 1e-5),
           reads=["colsA"], writes=["colsA"])
        st("vector", lambda e: e.reciprocal(colsA[:, 15:16], colsA[:, 14:15]),
           reads=["colsA"], writes=["colsA"])
        st("vector", lambda e: e.tensor_scalar_mul(colsB[:, 32:33], colsA[:, 15:16], 127.0),
           reads=["colsA"], writes=["colsB"])
        st("vector", lambda e: e.tensor_mul(colsB[:, 33:34], colsA[:, 11:12], colsB[:, 32:33]),
           reads=["colsA", "colsB"], writes=["colsB"])
        st("vector", lambda e: e.tensor_mul(colsB[:, 34:35], colsA[:, 14:15], m127_cols[:, 4:5]),
           reads=["colsA", "m127_cols"], writes=["dqo"])

    def B3b(ti):
        tg = "B%d" % ti
        st("tensor", lambda e: e.transpose(PC[0:1, 128:256], colsB[:, 33:34], identt[:]),
           reads=["colsB", "identt"], writes=["PC"])
        st("vector", lambda e: e.tensor_copy(srow[0:1, 0:128], PC[0:1, 128:256]),
           reads=["PC"], writes=["u4"])
        st("tensor", lambda e: e.matmul(PC[:, 256:384], onesr[0:1, 0:128],
                                        srow[0:1, 0:128], start=True, stop=True),
           reads=["u4", "onesr"], writes=["PC"])
        for half, (tb, tbn) in enumerate(((tp, "tp"), (x2t, "x2t"))):
            def _usco(e, tb=tb):
                last = None
                for c in range(8):
                    last = e.tensor_mul(tb[:, c * 128:(c + 1) * 128],
                                        tb[:, c * 128:(c + 1) * 128], PC[:, 256:384])
                return last
            st("vector", _usco, reads=[tbn, "PC"], writes=[tbn])
            st("vector", (lambda tb=tb: (lambda e: e.tensor_scalar_add(
                rint[:], tb[:, 0:1024], 256.0)))(),
               reads=[tbn], writes=["rint"], tag=tg + "ri%d" % half)
            st("vector", (lambda half=half, tb=tb: (lambda e: e.scalar_tensor_tensor(
                v_tm[:, half * 1024:(half + 1) * 1024], rint[:, 0:1024], -255.5,
                tb[:, 0:1024], ALU.add, ALU.is_le)))(),
               reads=["rint", tbn], writes=["v_tm"])
            st("vector", (lambda half=half: (lambda e: e.scalar_tensor_tensor(
                v_tm[:, half * 1024:(half + 1) * 1024], rint[:, 0:1024], -256.0,
                v_tm[:, half * 1024:(half + 1) * 1024], ALU.add, ALU.add)))(),
               reads=["rint", "v_tm"], writes=["v_tm"], tag=tg + "xqo%d" % half)

    def B4(ti):
        tg = "B%d" % ti
        xb, xbn = (xtB, "xtB") if ti % 2 == 0 else (xtBodd, "xtBodd")
        for nh, (P, pn) in enumerate(((PE, "PE"), (PF, "PF"))):
            def _womm(e, nh=nh, P=P):
                last = None
                for kk in range(16):
                    last = e.matmul(P[:], v_tm[:, kk * 128:(kk + 1) * 128],
                                    wo_ar[:, kk * 1024 + nh * 512: kk * 1024 + (nh + 1) * 512],
                                    start=(kk == 0), stop=(kk == 15))
                return last
            st("tensor", _womm, reads=["v_tm", "wo_ar"], writes=[pn], tag=tg + "wo%d" % nh)
            st("vector", (lambda nh=nh, P=P, xb=xb: (lambda e: e.scalar_tensor_tensor(
                x2t[:, nh * 512:(nh + 1) * 512], P[:], colsB[:, 34:35],
                xb[:, nh * 512:(nh + 1) * 512], ALU.mult, ALU.add)))(),
               reads=[xbn, pn, "dqo"], writes=["x2t"], tag=tg + "x2%d" % nh)
        dma("sync", x2_dr.ap()[ti], x2t[:], reads=["x2t"], writes=["x2_dr" + str(ti)], tag=tg + "x2")

    def B5(ti):
        tg = "B%d" % ti
        def _bns2(e):
            e.bn_stats(stats[:, 0:6], x2t[:, 0:512])
            return e.bn_stats(stats[:, 6:12], x2t[:, 512:1024])
        st("vector", _bns2, reads=["x2t"], writes=["stats"])
        st("vector", lambda e: e.bn_aggr(colsA[:, 0:2], stats[:]),
           reads=["stats"], writes=["colsA"])
        st("scalar", lambda e: e.activation(colsA[:, 2:3], colsA[:, 1:2], AF.Ln,
                                            bias=epsc[:, 0:1]),
           reads=["colsA", "epsc"], writes=["colsA"])
        st("scalar", lambda e: e.activation(colsA[:, 3:4], colsA[:, 2:3], AF.Exp,
                                            scale=-0.5),
           reads=["colsA"], writes=["colsA"])
        st("vector", lambda e: e.tensor_scalar(colsA[:, 4:5], colsA[:, 0:1],
                                               colsA[:, 3:4], -1.0, ALU.mult, ALU.mult),
           reads=["colsA"], writes=["colsA"])
        st("vector", lambda e: e.tensor_scalar(og[:, 0:1024], x2t[:], colsA[:, 3:4],
                                               colsA[:, 4:5], ALU.mult, ALU.add),
           reads=["x2t", "colsA"], writes=["og"])
        def _x2T(e):
            last = None
            for c in range(8):
                dst = PC[:, c * 128:(c + 1) * 128] if c < 4 else \
                      PD[:, (c - 4) * 128:(c - 3) * 128]
                last = e.transpose(dst, og[:, c * 128:(c + 1) * 128], identt[:])
            return last
        st("tensor", _x2T, reads=["og", "identt"], writes=["PC", "PD"], tag=tg + "x2T")
        def _x2ev(e):
            e.activation(xh2fm[:, 0:512], PC[:], AF.Copy)
            return e.activation(xh2fm[:, 512:1024], PD[:], AF.Copy)
        st("scalar", _x2ev, reads=["PC", "PD"], writes=["k_fm"])
        dma("sync", x2f_dr.ap()[ti], xh2fm[:], reads=["k_fm"],
            writes=["x2f_dr" + str(ti)], tag=tg + "x2f")

    BL(0)
    BC(0)
    B1(0)
    B2(0)
    for ti in range(NT):
        B3a(ti)
        if ti + 1 < NT:
            BL(ti + 1)
            BC(ti + 1)
            B1(ti + 1)
            B2(ti + 1)
        B3b(ti)
        B4(ti)
        B5(ti)

    b.alias("x2a", ["og"])
    b.alias("x2b", ["og"])
    b.alias("ya", ["ogfm"])
    b.alias("yb", ["ogfm"])
    b.alias("btot_c", ["upb_ct"])
    b.alias("b2r", ["Eq_t", "Ek_t", "EK_t", "ind4", "pmA", "pmB"])
    dma("sync", upb_ct[:], btot_dr.ap(), reads=["btot_d"], writes=["btot_c"], tag="ldbtot")
    dma("sync", b2r, b2_dr.ap(), reads=["b2_d"], writes=["b2r"], tag="ldb2")
    dma("sync", W1_ar.rearrange("p (k h) -> p k h", k=8),
        e4_out.ap()[:, 0:4096].rearrange("(k p) h -> p k h", p=128),
        reads=["e4_out_d"], writes=["W1_ar"])
    for r_ in range(8):
        dma("sync",
            W2_ar.rearrange("p (u r oo) -> p u r oo", u=32, r=8)[:, :, r_],
            e4_out.ap()[r_ * 128:(r_ + 1) * 128, 4096:8192].rearrange(
                "p (u oo) -> p u oo", u=32),
            reads=["e4_out_d"], writes=["W2_ar"], tag="unpW2_%d" % r_)

    NST = NT // 2
    b.alias("h2g0", ["wgk2_ar"])
    b.alias("h2g1", ["wgk2_ar"])
    def CL(s):
        tg = "C%d" % s
        t0, t1 = 2 * s, 2 * s + 1
        if s % 2 == 0:
            bufA, bufB, nmA, nmB = xst[:, 0:1024], xst[:, 1024:2048], "xstA", "xstB"
        else:
            bufA, bufB, nmA, nmB = xq_a, xq_b, "xq_a", "xq_b"
        for j, tj in enumerate((t0, t1)):
            dma("sync",
                bufA.rearrange("p (c t) -> p c t", c=4)[:, :, j * 128:(j + 1) * 128],
                x2f_dr.ap()[tj][:, 0:512].rearrange("p (c t) -> p c t", c=4),
                reads=["x2f_dr" + str(tj)], writes=[nmA], tag=tg + "xla%d" % j)
            dma("sync",
                bufB.rearrange("p (c t) -> p c t", c=4)[:, :, j * 128:(j + 1) * 128],
                x2f_dr.ap()[tj][:, 512:1024].rearrange("p (c t) -> p c t", c=4),
                reads=["x2f_dr" + str(tj)], writes=[nmB], tag=tg + "xlb%d" % j)

    CL(0)
    if NST > 1:
        CL(1)
    dma("sync", x2a[:], x2_dr.ap()[0], reads=["x2_dr0"], writes=["x2a"], tag="C0")
    dma("sync", x2b[:], x2_dr.ap()[1], reads=["x2_dr1"], writes=["x2b"], tag="C0")
    for s in range(NST):
        tg = "C%d" % s
        t0, t1 = 2 * s, 2 * s + 1
        if s % 2 == 0:
            bufA, bufB, nmA, nmB = xst[:, 0:1024], xst[:, 1024:2048], "xstA", "xstB"
        else:
            bufA, bufB, nmA, nmB = xq_a, xq_b, "xq_a", "xq_b"
        for hg in range(17):
            if hg < 16:
                hc0, hc1 = 2 * hg, 2 * hg + 1
                P1, p1n = (PE, "PE") if hg % 2 == 0 else (PF, "PF")
                hb, hbn = (h2g0, "h2g0") if hg % 2 == 0 else (h2g1, "h2g1")
                def _ffn1(e, hc0=hc0, hc1=hc1, P1=P1, bufA=bufA, bufB=bufB):
                    last = None
                    for hcc, hc in enumerate((hc0, hc1)):
                        for kk in range(8):
                            rhs = (bufA[:, kk * 256:(kk + 1) * 256] if kk < 4
                                   else bufB[:, (kk - 4) * 256:(kk - 3) * 256])
                            last = e.matmul(
                                P1[:, hcc * 256:(hcc + 1) * 256],
                                W1_ar[:, kk * 4096 + hc * 128: kk * 4096 + (hc + 1) * 128],
                                rhs, start=(kk == 0), stop=(kk == 7))
                    return last
                st("tensor", _ffn1, reads=["W1_ar", nmA, nmB],
                   writes=[p1n], tag=tg + "f1_%d" % hg)
                def _gel(e, hc0=hc0, hc1=hc1, P1=P1, hb=hb):
                    e.activation(hb[:, 0:256], P1[:, 0:256], AF.Gelu,
                                 bias=upb_ct[:, hc0:hc0 + 1])
                    return e.activation(hb[:, 256:512], P1[:, 256:512], AF.Gelu,
                                        bias=upb_ct[:, hc1:hc1 + 1])
                st("scalar", _gel, reads=[p1n, "btot_c"], writes=[hbn],
                   tag=tg + "gel%d" % hg)
            if hg >= 1:
                pg = hg - 1
                pc0, pc1 = 2 * pg, 2 * pg + 1
                phb, phbn = (h2g0, "h2g0") if pg % 2 == 0 else (h2g1, "h2g1")
                def _ffn2(e, pg=pg, pc0=pc0, pc1=pc1, phb=phb):
                    last = None
                    for hcc, hc in enumerate((pc0, pc1)):
                        for tj, P in ((0, PA), (1, PB)):
                            for n in range(2):
                                last = e.matmul(
                                    P[:, n * 512:(n + 1) * 512],
                                    phb[:, hcc * 256 + tj * 128: hcc * 256 + (tj + 1) * 128],
                                    W2_ar[:, hc * 1024 + n * 512: hc * 1024 + (n + 1) * 512],
                                    start=(pg == 0 and hcc == 0), stop=False)
                    return last
                st("tensor", _ffn2, reads=[phbn, "W2_ar"], writes=["PA", "PB"],
                   tag=tg + "f2_%d" % pg)
        if s + 2 < NST:
            CL(s + 2)
        def _bias2(e):
            last = None
            for P in (PA, PB):
                for n in range(2):
                    last = e.matmul(P[:, n * 512:(n + 1) * 512], onesr[0:1, 0:128],
                                    b2r[0:1, n * 512:(n + 1) * 512],
                                    start=False, stop=True)
            return last
        st("tensor", _bias2, reads=["b2r", "onesr"], writes=["PA", "PB"], tag=tg + "b2")
        def _yout(e):
            e.tensor_add(ya[:], x2a[:], PA[:])
            return e.tensor_add(yb[:], x2b[:], PB[:])
        st("vector", _yout, reads=["x2a", "x2b", "PA", "PB"], writes=["ya", "yb"])
        dma("sync", ys.ap()[t0 * 128:(t0 + 1) * 128, :], ya[:], reads=["ya"],
            writes=["ys" + str(t0)], tag=tg + "st0")
        dma("sync", ys.ap()[t1 * 128:(t1 + 1) * 128, :], yb[:], reads=["yb"],
            writes=["ys" + str(t1)], tag=tg + "st1")
        if s + 1 < NST:
            dma("sync", x2a[:], x2_dr.ap()[t0 + 2], reads=["x2_dr" + str(t0 + 2)],
                writes=["x2a"], tag="C%d" % (s + 1))
            dma("sync", x2b[:], x2_dr.ap()[t1 + 2], reads=["x2_dr" + str(t1 + 2)],
                writes=["x2b"], tag="C%d" % (s + 1))

    b.emit()
    return nc



_BF = None

_IND4 = np.zeros((4, 512), np.float32)
for _j in range(4):
    _IND4[_j, _j * 128:(_j + 1) * 128] = 1.0

def _host_inputs(inputs, T):
    f32 = np.float32
    x = np.asarray(inputs["L0_attn_hidden_states"], dtype=f32)
    B, L, _ = x.shape
    xflat = np.ascontiguousarray(x.reshape(B * L, D))
    wq = np.asarray(inputs["L0_attn_wq"], f32)
    wk = np.asarray(inputs["L0_attn_wk"], f32)
    wv = np.asarray(inputs["L0_attn_wv"], f32)
    wg = np.asarray(inputs["L0_attn_wg"], f32)
    wo = np.asarray(inputs["L0_attn_wo"], f32)
    wgk1 = np.asarray(inputs["L0_attn_wgk1"], f32)
    wgk2 = np.asarray(inputs["L0_attn_wgk2"], f32)
    wqT = np.ascontiguousarray(wq.T)
    wkT = np.ascontiguousarray(wk.T)
    wvT = np.ascontiguousarray(wv.T)
    wgT = np.ascontiguousarray(wg.T)
    woT = np.ascontiguousarray(wo.T)
    wgk1T = np.ascontiguousarray(wgk1.T)
    wgk2T = np.ascontiguousarray(wgk2.T)
    wgk1T_f = np.ascontiguousarray(
        wgk1T.reshape(8, 128, R).transpose(1, 0, 2).reshape(128, 8 * R))
    def fmcols(v, nch):
        return np.ascontiguousarray(np.asarray(v, f32).reshape(nch, 128).T)
    nq_c = fmcols(inputs["L0_attn_nq"], 8)
    nk_c = fmcols(inputs["L0_attn_nk"], 8)
    nv_c = fmcols(inputs["L0_attn_nv"], 8)
    ng_c = fmcols(inputs["L0_attn_ng"], 8)
    ngk1_c = fmcols(inputs["L0_attn_ngk1"], 8)
    no_c = fmcols(inputs["L0_attn_no"], 16)
    ngk2_r = np.ascontiguousarray(
        np.broadcast_to(np.asarray(inputs["L0_attn_ngk2"], f32), (128, 16)))
    bgk2_row = np.ascontiguousarray(
        np.asarray(inputs["L0_attn_bgk2"], f32).reshape(8, 128).T)
    W_exp = np.asarray(inputs["L0_ffn_W_exp"], f32)
    W_con = np.asarray(inputs["L0_ffn_W_con"], f32)
    WexpT = np.ascontiguousarray(W_exp.T)
    upw = np.asarray(inputs["L0_ffn_up_weights"], f32)
    dnw = np.asarray(inputs["L0_ffn_down_weights"], f32)
    b_exp = np.asarray(inputs["L0_ffn_b_exp"], f32)
    up_bias = np.asarray(inputs["L0_ffn_up_bias"], f32)
    b_con = np.asarray(inputs["L0_ffn_b_con"], f32)
    dn_b = np.asarray(inputs["L0_ffn_down_bias"], f32)

    ident = np.eye(128, dtype=f32)
    jj, ii = np.meshgrid(np.arange(128), np.arange(128), indexing="ij")
    trimask = (jj <= ii).astype(f32)
    nelem = np.array([wq.size, wk.size, wv.size, wg.size, wo.size,
                      wgk1.size * NCORES, wgk2.size * NCORES, 1.0], f32)
    ninv = np.ascontiguousarray(np.broadcast_to(1.0 / nelem, (128, 8)))
    bexp_c = np.ascontiguousarray(b_exp.reshape(32, 128).T)

    def low_pack(W):
        out = np.zeros((NCORES, 7 * 4 * 4 * 64), f32)
        for st_ in range(7):
            sig = 2 ** st_
            Wv = W[st_].reshape(-1, sig, 2, 2)
            for c in range(NCORES):
                for j in range(4):
                    u = 4 * c + j
                    for w in range(4):
                        r_, c_ = divmod(w, 2)
                        vec = np.empty(64, f32)
                        for k in range(64):
                            vec[k] = Wv[u * (64 // sig) + k // sig, k % sig, r_, c_]
                        out[c, ((st_ * 4 + j) * 4 + w) * 64:((st_ * 4 + j) * 4 + w + 1) * 64] = vec
        return out

    def high_pack(W):
        out = np.zeros((128, 5 * 16 * 4), f32)
        for sidx in range(5):
            s_ = sidx + 7
            sig = 2 ** s_
            pairs = [u for u in range(32) if (u >> sidx) & 1 == 0]
            for pi, ua in enumerate(pairs):
                for t in range(128):
                    i = ua * 128 + t
                    k = (i >> (s_ + 1)) * sig + (i & (sig - 1))
                    for w in range(4):
                        r_, c_ = divmod(w, 2)
                        out[t, (sidx * 16 + pi) * 4 + w] = 0
                        out[t, (sidx * 16 + pi) * 4 + w] = W[s_][k, r_, c_]
        return out

    lowU = low_pack(upw)
    lowD = low_pack(dnw)
    uph = high_pack(upw)
    dnh = high_pack(dnw)

    in_maps = []
    for c in range(NCORES):
        base = 4 * (c // 4)
        am = np.zeros(8, f32)
        am[base:c] = 1.0
        woT_sl = woT[c * 256:(c + 1) * 256].reshape(2, 128, D).transpose(1, 0, 2).reshape(128, 2 * D)
        im = {
            "xs": np.ascontiguousarray(xflat[c * T:(c + 1) * T]),
            "ident": ident, "trimask": trimask,
            "wqT_sl": np.ascontiguousarray(wqT[c * 128:(c + 1) * 128]),
            "wkT_sl": np.ascontiguousarray(wkT[c * 128:(c + 1) * 128]),
            "wvT_sl": np.ascontiguousarray(wvT[c * 128:(c + 1) * 128]),
            "wgT_sl": np.ascontiguousarray(wgT[c * 128:(c + 1) * 128]),
            "woT_sl": np.ascontiguousarray(woT_sl),
            "wgk1T_f": wgk1T_f, "wgk2T_f": wgk2T,
            "nq_c": nq_c, "nk_c": nk_c, "nv_c": nv_c, "ng_c": ng_c,
            "ngk1_c": ngk1_c, "no_c": no_c, "ngk2_r": ngk2_r,
            "bgk2_row": bgk2_row, "ninv_row": ninv,
            "lowU_sl": lowU[c][None, :], "lowD_sl": lowD[c][None, :],
            "uph_w": uph, "dnh_w": dnh,
            "wexpT_sl": np.ascontiguousarray(WexpT[:, c * 128:(c + 1) * 128]),
            "wcon_sl": np.ascontiguousarray(W_con[:, c * 128:(c + 1) * 128]),
            "bexp_c": bexp_c,
            "upb_c": np.ascontiguousarray(up_bias.reshape(32, 128).T),
            "bcon_c": np.ascontiguousarray(b_con.reshape(8, 128).T),
            "dnb_c": np.ascontiguousarray(dn_b.reshape(8, 128).T),
            "ind4_in": _IND4,
            "amask": np.ascontiguousarray(np.broadcast_to(am, (128, 8))),
            "am1": np.ascontiguousarray(np.broadcast_to(1.0 - am, (128, 8))),
            "bmask": np.ascontiguousarray(np.broadcast_to(am, (128, 8))),
        }
        in_maps.append(im)
    return in_maps


def kernel(**inputs):
    x = np.asarray(inputs["L0_attn_hidden_states"])
    B, L, _ = x.shape
    T = B * L // NCORES
    in_maps = _host_inputs(inputs, T)
    nc = build_program(T)
    res = run_bass_kernel_spmd(nc, in_maps, core_ids=list(range(NCORES)))
    out = np.concatenate([res.results[c]["ys"] for c in range(NCORES)], axis=0)
    return out.reshape(B, L, D).astype(np.float32)


# version 38
# speedup vs baseline: 1.0313x; 1.0313x over previous
import math
import re
import contextlib
import numpy as np

import concourse.bass as bass
import concourse.mybir as mybir
from concourse.bass_utils import run_bass_kernel_spmd

F32 = mybir.dt.float32
BF16 = mybir.dt.bfloat16
I32 = mybir.dt.int32
AF = mybir.ActivationFunctionType
ALU = mybir.AluOpType
AX = mybir.AxisListType

NCORES = 8
D = 1024
DK = 1024
DV = 2048
H = 4
DKH = 256
DVH = 512
R = 16
DH = 4096
C = 128
DEBUG_DUMPS = False



class Step:
    def __init__(self, eng, fn, waits, incs, tag):
        self.eng, self.fn, self.waits, self.incs, self.tag = eng, fn, waits, incs, tag


class Builder:

    ENG_SEM = {"tensor": "te", "vector": "ve", "scalar": "se",
               "gpsimd": "gc", "sync": "sy"}

    CONST_BUFS = {"identt", "maskt", "ncols", "no_ct", "ngk2_t", "ninv_t",
                  "amask_t", "am1_t", "bmask_t", "uph_t", "dnh_t", "bex_t",
                  "upb_ct", "bcon_ct", "dnb_ct", "onesr"}

    def __init__(self, nc, T):
        self.nc = nc
        self.T = T
        self.NT = T // C
        self.steps = []
        self.sems = {}
        self._es = contextlib.ExitStack()
        for name in ("te", "ve", "se", "gc", "sy", "gd", "cc", "sd"):
            self.sems[name] = self._es.enter_context(nc.semaphore("sem_" + name))
        self.counts = {k: 0 for k in self.sems}
        self.last_write = {}
        self.readers = {}
        self.tensors = {}
        self.alias_debt = {}

    def _dma_sem(self, buf, eng="sync"):
        key = re.sub(r"[0-9_]+$", "", buf)
        if key in self.CONST_BUFS or buf in self.CONST_BUFS:
            key = "const"
        key = ("d_" if eng == "sync" else "dg_") + key
        if key not in self.sems:
            self.sems[key] = self._es.enter_context(self.nc.semaphore("sem_" + key))
            self.counts[key] = 0
        return key

    def sb(self, name, shape, dtype):
        t = self._es.enter_context(self.nc.sbuf_tensor(name, shape, dtype))
        self.tensors[name] = t
        return t

    def ps(self, name, shape):
        t = self._es.enter_context(self.nc.psum_tensor(name, shape, F32))
        self.tensors[name] = t
        return t

    def alias(self, name, oldnames):
        self.alias_debt.setdefault(name, []).extend(oldnames)

    def step(self, eng, fn, reads=(), writes=(), tag="", sem=None, inc=1):
        sem = sem or self.ENG_SEM[eng]
        writes = list(writes)
        for b_ in list(writes):
            if b_ in self.alias_debt:
                writes.extend(self.alias_debt.pop(b_))
        waits = {}
        def add_wait(ev):
            if ev is None:
                return
            s, v = ev
            if waits.get(s, 0) < v:
                waits[s] = v
        for b in reads:
            add_wait(self.last_write.get(b))
        for b in writes:
            add_wait(self.last_write.get(b))
            for ev in self.readers.get(b, ()):
                add_wait(ev)
        for s_ in list(waits):
            if s_.startswith(("d_", "dg_")):
                waits[s_] = self.counts[s_]
        self.counts[sem] += inc
        ev = (sem, self.counts[sem])
        for b in reads:
            self.readers.setdefault(b, []).append(ev)
        for b in writes:
            self.last_write[b] = ev
            self.readers[b] = []
        self.steps.append(Step(eng, fn, sorted(waits.items()), [(sem, inc)], tag))
        return ev

    def dma(self, eng, out, in_, reads=(), writes=(), tag=""):
        sem = self._dma_sem(writes[0] if writes else "misc", eng)
        return self.step(eng, lambda e: e.dma_start(out, in_),
                         reads=reads, writes=writes, tag=tag, sem=sem, inc=16)

    def barrier_event(self):
        return dict(self.counts)

    def emit(self):
        nc = self.nc
        per_eng = {e: [] for e in self.ENG_SEM}
        for st in self.steps:
            per_eng[st.eng].append(st)
        marks = {e: {} for e in self.ENG_SEM}

        with nc.Block() as block:
            def make_body(eng_name):
                def run(eng):
                    wm = marks[eng_name]
                    for st in per_eng[eng_name]:
                        for s, v in st.waits:
                            if wm.get(s, 0) >= v:
                                continue
                            eng.wait_ge(self.sems[s], v)
                            wm[s] = v
                        inst = st.fn(eng)
                        (s, k), = st.incs
                        if inst is None:
                            eng.sem_inc(self.sems[s], k)
                        else:
                            inst.then_inc(self.sems[s], k)
                return run
            block.tensor(make_body("tensor"))
            block.vector(make_body("vector"))
            block.scalar(make_body("scalar"))
            block.gpsimd(make_body("gpsimd"))
            block.sync(make_body("sync"))
        self._es.close()


def build_program(T):
    NT = T // C
    nc = bass.Bass("TRN2", target_bir_lowering=False)
    dt = nc.dram_tensor

    xs = dt("xs", [T, D], F32, kind="ExternalInput")
    ident = dt("ident", [128, 128], F32, kind="ExternalInput")
    trimask = dt("trimask", [128, 128], F32, kind="ExternalInput")
    wqT_sl = dt("wqT_sl", [128, DK], F32, kind="ExternalInput")
    wkT_sl = dt("wkT_sl", [128, DK], F32, kind="ExternalInput")
    wvT_sl = dt("wvT_sl", [128, DV], F32, kind="ExternalInput")
    wgT_sl = dt("wgT_sl", [128, DV], F32, kind="ExternalInput")
    woT_sl = dt("woT_sl", [128, 2 * D], F32, kind="ExternalInput")
    wgk1T_f = dt("wgk1T_f", [128, 8 * R], F32, kind="ExternalInput")
    wgk2T_f = dt("wgk2T_f", [16, DK], F32, kind="ExternalInput")
    nq_c = dt("nq_c", [128, 8], F32, kind="ExternalInput")
    nk_c = dt("nk_c", [128, 8], F32, kind="ExternalInput")
    nv_c = dt("nv_c", [128, 8], F32, kind="ExternalInput")
    ng_c = dt("ng_c", [128, 8], F32, kind="ExternalInput")
    ngk1_c = dt("ngk1_c", [128, 8], F32, kind="ExternalInput")
    no_c = dt("no_c", [128, 16], F32, kind="ExternalInput")
    ngk2_r = dt("ngk2_r", [128, 16], F32, kind="ExternalInput")
    bgk2_row = dt("bgk2_row", [128, 8], F32, kind="ExternalInput")
    ninv_row = dt("ninv_row", [128, 8], F32, kind="ExternalInput")
    lowU_sl = dt("lowU_sl", [1, 7 * 4 * 256], F32, kind="ExternalInput")
    lowD_sl = dt("lowD_sl", [1, 7 * 4 * 256], F32, kind="ExternalInput")
    uph_w = dt("uph_w", [128, 5 * 16 * 4], F32, kind="ExternalInput")
    dnh_w = dt("dnh_w", [128, 5 * 16 * 4], F32, kind="ExternalInput")
    wexpT_sl = dt("wexpT_sl", [4096, 128], F32, kind="ExternalInput")
    wcon_sl = dt("wcon_sl", [4096, 128], F32, kind="ExternalInput")
    bexp_c = dt("bexp_c", [128, 32], F32, kind="ExternalInput")
    upb_c = dt("upb_c", [128, 32], F32, kind="ExternalInput")
    bcon_c = dt("bcon_c", [128, 8], F32, kind="ExternalInput")
    dnb_c = dt("dnb_c", [128, 8], F32, kind="ExternalInput")
    ind4_in = dt("ind4_in", [4, 512], F32, kind="ExternalInput")
    amask = dt("amask", [128, 8], F32, kind="ExternalInput")
    am1 = dt("am1", [128, 8], F32, kind="ExternalInput")
    bmask = dt("bmask", [128, 8], F32, kind="ExternalInput")
    ys = dt("ys", [T, D], F32, kind="ExternalOutput")

    _K = "ExternalOutput" if DEBUG_DUMPS else "Internal"
    op_dr = dt("op_dr", [NT, 128, DV], BF16, kind=_K)
    sg_dr = dt("sg_dr", [NT, 128, DV], BF16, kind=_K)
    qb_dr = dt("qb_dr", [NT, 128, DK], BF16, kind=_K)
    x2_dr = dt("x2_dr", [NT, 128, D], F32, kind=_K)
    x2f_dr = dt("x2f_dr", [NT, 128, D], BF16, kind="Internal")
    e1_in = dt("e1_in", [1, 8], F32, kind="Internal")
    e1_out = dt("e1_out", [8, 8], F32, kind="Internal")
    e2_in = dt("e2_in", [128, 8192], mybir.dt.int8, kind="Internal")
    e2_out = dt("e2_out", [1024, 8192], mybir.dt.int8, kind="Internal")
    e3_in = dt("e3_in", [128, 1024], F32, kind="Internal")
    e3_out = dt("e3_out", [1024, 1024], F32, kind="Internal")
    e4_in = dt("e4_in", [128, 8192], BF16, kind="Internal")
    e4_out = dt("e4_out", [1024, 8192], BF16, kind="Internal")
    e5_in = dt("e5_in", [128, 4104], BF16, kind="Internal")
    e5_out = dt("e5_out", [1024, 4104], BF16, kind="Internal")
    btot_dr = dt("btot_dr", [128, 32], F32, kind="Internal")
    dbg_dr = dt("dbg_dr", [8, 128, DV], F32, kind=_K)
    b2_dr = dt("b2_dr", [1, D], F32, kind="Internal")

    DBG = DEBUG_DUMPS
    b = Builder(nc, T)
    sb = b.sb
    st = b.step
    dma = b.dma

    identt = sb("identt", [128, 128], F32)
    maskt = sb("maskt", [128, 128], BF16)
    onesr = sb("onesr", [1, 256], F32)
    epsc = sb("epsc", [128, 2], F32)
    ncols = sb("ncols", [128, 72], F32)
    no_ct = sb("no_ct", [128, 16], F32)
    ngk2_t = sb("ngk2_t", [128, 16], F32)
    ninv_t = sb("ninv_t", [128, 8], F32)
    mx_cols = sb("mx_cols", [128, 8], F32)
    th_cols = sb("th_cols", [128, 8], F32)
    nth_cols = sb("nth_cols", [128, 8], F32)
    m127_cols = sb("m127_cols", [128, 8], F32)
    ccol = sb("ccol", [128, 1], F32)
    c1col = sb("c1col", [128, 1], F32)
    colsA = sb("colsA", [128, 32], F32)
    colsB = sb("colsB", [128, 48], F32)
    colsQ = sb("colsQ", [128, NT], F32)
    eoff_t = sb("eoff_t", [128, 8], F32)
    offb_t = sb("offb_t", [128, 8], F32)
    dprod = sb("dprod", [128, 8], F32)
    gcur = sb("gcur", [128, 8], F32)
    wrk8 = sb("wrk8", [128, 8], F32)
    amask_t = sb("amask_t", [128, 8], F32)
    am1_t = sb("am1_t", [128, 8], F32)
    bmask_t = sb("bmask_t", [128, 8], F32)
    stats = sb("stats", [128, 12], F32)
    btsc = sb("btsc", [128, 2], F32)
    msum = sb("msum", [128, 8], F32)
    y1t = sb("y1t", [128, 16], F32)
    sq16 = sb("sq16", [128, 16], F32)
    t2g = sb("t2g", [128, 16], F32)
    rq2 = sb("rq2", [128, 16], I32)
    u4 = sb("u4", [128, 4 * 64], F32)
    tsc = u4
    pmx = u4[:, 0:128]
    y1sb = u4[0:16, 0:128]
    srow = u4[0:1, 0:256]
    e1sb = u4[0:8, 192:200]
    prow = u4[0:1, 200:216]
    dmod = wrk8
    S_f = sb("S_f", [128, 4096], F32)
    rint = sb("rint", [128, 1024], mybir.dt.int16)
    xq_a = sb("xq_a", [128, D], BF16)
    xq_b = sb("xq_b", [128, D], BF16)
    q_fm = sb("q_fm", [128, DK], BF16)
    k_fm = sb("k_fm", [128, DK], BF16)
    qt_h = sb("qt_h", [128, 256], BF16)
    kt_h = sb("kt_h", [128, 256], BF16)
    qb_h = sb("qb_h", [128, 256], BF16)
    wgk1_ar = sb("wgk1_ar", [128, 8 * R], BF16)
    xq2_tm = sb("xq2_tm", [128, 16], BF16)
    xq2_fm = sb("xq2_fm", [16, 128], BF16)

    upb_ct = sb("upb_ct", [128, 32], F32)
    bcon_ct = sb("bcon_ct", [128, 8], F32)
    dnb_ct = sb("dnb_ct", [128, 8], F32)

    SLAB1N = 68608
    slab1 = sb("slab1_mem", [128, SLAB1N], BF16)
    def s1(off, n):
        return slab1[:, off:off + n]
    wq_ar = s1(0, 8192)
    wk_ar = s1(8192, 8192)
    wv_ar = s1(16384, 16384)
    wg_ar = s1(32768, 16384)
    S_bf = s1(49152, 4096)
    p1b = s1(53248, 2048)
    p2b = s1(55296, 2048)
    op_t = s1(53248, 2048)
    sg_tm = s1(55296, 2048)
    h2g0 = s1(67584, 512)
    h2g1 = s1(68096, 512)
    xq_c = s1(59392, 1024)
    xq_d = s1(60416, 1024)
    EsA = s1(61440, 768)
    EsB = s1(62208, 768)
    khtm4 = s1(62976, 1024)
    am4 = s1(64000, 512)
    qh4 = s1(64512, 1024)
    v_tm = s1(57344, 2048)
    qstg = s1(59392, 2048)
    wo_ar = s1(0, 16384)
    Sin_bf = s1(16384, 4096)
    W1_ar = s1(0, 32768)
    W2_ar = s1(32768, 32768)
    xst = s1(65536, 2048)
    wgk2_ar = slab1[0:16, 67584:68608]
    xh2 = q_fm
    xh2fm = k_fm
    b.alias("op_t", ["p1b"])
    b.alias("sg_tm", ["p2b"])
    b.alias("wo_ar", ["wq_ar", "wk_ar"])
    b.alias("Sin_bf", ["wv_ar"])
    b.alias("W1_ar", ["wq_ar", "wk_ar", "wv_ar", "wo_ar", "Sin_bf"])
    b.alias("W2_ar", ["wg_ar", "S_bf0", "S_bf1", "S_bf2", "S_bf3", "op_t", "sg_tm",
                      "v_tm", "qstg", "p1b", "p2b", "xq_c", "xq_d",
                      "EsA", "EsB", "khtm4", "am4", "qh4"])
    b.alias("xq_c", ["qstg"])
    b.alias("xq_d", ["qstg"])

    SLAB2N = 10240
    slab2 = sb("slab2_mem", [128, SLAB2N], F32)
    def s2(off, n):
        return slab2[:, off:off + n]
    wsl_f = s2(0, 2048)
    gk1_f = slab2[:, 2048:2048 + 128]
    bex_t = slab2[:, 640:672]
    bfold = slab2[:, 672:704]
    pfold = s2(2048, 4096)
    bltU = s2(6144, 4096)
    lowr = slab2[0:1, 1024:2048]
    b.alias("lowr", ["wsl_f"])
    lowr2 = slab2[0:1, 0:1024]
    b.alias("lowr2", ["wsl_f"])
    b.alias("fbU0", ["gk1_f"])
    b.alias("uph_t", ["wsl_f"])
    b.alias("dnh_t", ["wsl_f"])
    b.alias("bex_t", ["wsl_f"])
    b.alias("bfold", ["wsl_f"])
    xt = s2(0, 1024)
    xh_tm = s2(1024, 1024)
    xh_fm = s2(2048, 1024)
    beta = s2(3072, 1024)
    sgm = s2(4096, 1024)
    lsg = sgm
    El_t = s2(5120, 1024)
    tp = s2(6144, 1024)
    usc = tp
    Eq_t = s2(7168, 256)
    Ek_t = s2(7424, 256)
    EK_t = s2(7680, 256)
    KHp = slab2[:, 7168:7424]
    ind4 = slab2[0:4, 7936:8448]
    b.alias("ind4", ["bltU"])
    pmA = slab2[:, 8448:8576]
    pmB = slab2[:, 8576:8704]
    b.alias("pmA", ["bltU"])
    b.alias("pmB", ["bltU"])
    uph_t = slab2[:, 0:320]
    dnh_t = slab2[:, 320:640]
    bgk2_t = slab2[0:1, 9216:9216 + 1024]
    b.alias("bgk2_c", ["msum"])
    b.alias("xt", ["wsl_f", "lowr", "lowr2", "uph_t", "dnh_t", "bex_t", "bfold"])
    b.alias("xh_tm", ["wsl_f", "lowr"])
    b.alias("xh_fm", ["pf0"])
    b.alias("pf0", ["fbU0", "fbU1", "fbU2", "fbU3",
                    "fbD0", "fbD1", "fbD2", "fbD3"])
    b.alias("beta", ["pf1"])
    b.alias("sgm", ["pf2"])
    b.alias("El_t", ["pf3"])
    b.alias("tp", ["bltU"])
    b.alias("Eq_t", ["bltU"])
    b.alias("Ek_t", ["bltU"])
    b.alias("EK_t", ["bltU"])
    b.alias("bgk2_t", ["bltU"])
    Sin_f = s2(0, 4096)
    Ur_f = s2(4096, 4104)
    og = s2(0, 2048)
    ogfm = s2(2048, 2048)
    x2t = s2(4096, 1024)
    xtB = s2(5120, 1024)
    xtBodd = s2(7424, 1024)
    b.alias("xtBodd", ["ind4"])
    x2a = s2(0, 1024)
    x2b = s2(1024, 1024)
    ya = s2(2048, 1024)
    yb = s2(3072, 1024)
    btot = slab2[0:1, 4096:4096 + 4096]
    b2r = slab2[0:1, 8192:8192 + 1024]

    PA = b.ps("PA", [128, 1024])
    PB = b.ps("PB", [128, 1024])
    PC = b.ps("PC", [128, 512])
    PD = b.ps("PD", [128, 512])
    PE = b.ps("PE", [128, 512])
    PF = b.ps("PF", [128, 512])

    dma("sync", identt[:], ident.ap(), writes=["identt"])
    dma("gpsimd", maskt[:], trimask.ap(), writes=["maskt"])
    for i_, src in enumerate([nq_c, nk_c, nv_c, ng_c, ngk1_c]):
        dma("sync", ncols[:, i_ * 8:(i_ + 1) * 8], src.ap(), writes=["ncols"])
    dma("sync", no_ct[:], no_c.ap(), writes=["no_ct"])
    dma("sync", ngk2_t[:], ngk2_r.ap(), writes=["ngk2_t"])
    dma("sync", ninv_t[:], ninv_row.ap(), writes=["ninv_t"])
    dma("sync", amask_t[:], amask.ap(), writes=["amask_t"])
    dma("sync", am1_t[:], am1.ap(), writes=["am1_t"])
    dma("sync", bmask_t[:], bmask.ap(), writes=["bmask_t"])
    dma("sync", upb_ct[:], upb_c.ap(), writes=["upb_ct"])
    dma("sync", bcon_ct[:], bcon_c.ap(), writes=["bcon_ct"])
    dma("sync", dnb_ct[:], dnb_c.ap(), writes=["dnb_ct"])

    st("vector", lambda e: e.scalar_tensor_tensor(
        ncols[:, 40:72], ncols[:, 0:32], -1.0, ncols[:, 0:32], ALU.mult, ALU.max),
       reads=["ncols"], writes=["ncols"])
    st("vector", lambda e: e.memset(onesr[:], 1.0), writes=["onesr"])
    def _eps(e):
        e.memset(epsc[:, 0:1], 1e-5)
        return e.memset(epsc[:, 1:2], 1e-8)
    st("vector", _eps, writes=["epsc"])
    st("vector", lambda e: e.memset(colsB[:, 39:40], 1.0), writes=["colsB"])
    st("vector", lambda e: e.memset(S_f[:], 0.0),
       writes=["S_f0", "S_f1", "S_f2", "S_f3"])
    st("vector", lambda e: e.memset(dprod[:], 1.0), writes=["dprod"])
    st("vector", lambda e: e.memset(gcur[:], 1.0), writes=["gcur"])
    st("vector", lambda e: e.memset(offb_t[:], 0.0), writes=["offb_t"])

    slmap = [(wqT_sl, DK, 0), (wkT_sl, DK, 1), (wvT_sl, DV, 2),
             (wgT_sl, DV, 3), (woT_sl, 2 * D, 4)]
    for src, w, slot in slmap:
        if w <= 2048:
            dma("sync", wsl_f[:, 0:w], src.ap(), writes=["wsl_f"], tag="p1ld%d" % slot)
            st("vector", (lambda w=w, slot=slot: (lambda e: e.reduce_sum(
                colsA[:, slot:slot + 1], wsl_f[:, 0:w], axis=AX.X,
                apply_absolute_value=True)))(),
               reads=["wsl_f"], writes=["colsA"])
    dma("sync", gk1_f[:], wgk1T_f.ap(), writes=["gk1_f"])
    st("vector", lambda e: e.reduce_sum(colsA[:, 5:6], gk1_f[:], axis=AX.X,
                                        apply_absolute_value=True),
       reads=["gk1_f"], writes=["colsA"])
    dma("sync", wsl_f[0:16, 0:DK], wgk2T_f.ap(), writes=["wsl_f"], tag="p1ldgk2")
    st("vector", lambda e: e.reduce_sum(colsA[0:16, 6:7], wsl_f[0:16, 0:DK], axis=AX.X,
                                        apply_absolute_value=True),
       reads=["wsl_f"], writes=["colsA"])

    def _summms(e):
        for slot in range(6):
            e.matmul(PC[0:1, slot:slot + 1], colsB[:, 39:40],
                     colsA[:, slot:slot + 1], start=True, stop=True)
        e.matmul(PC[0:1, 6:7], colsB[0:16, 39:40], colsA[0:16, 6:7],
                 start=True, stop=True)
        return e.matmul(PC[0:1, 7:8], colsB[0:16, 39:40], colsA[0:16, 6:7],
                        start=True, stop=True)
    st("tensor", _summms, reads=["colsA", "colsB"], writes=["PC"])
    st("vector", lambda e: e.tensor_copy(prow[:, 0:8], PC[0:1, 0:8]),
       reads=["PC"], writes=["u4"])
    dma("sync", e1_in.ap(), prow[:, 0:8], reads=["u4"], writes=["e1_in_d"])
    st("gpsimd", lambda e: e.collective_compute(
        "AllGather", ALU.bypass, ins=[e1_in.ap()], outs=[e1_out.ap()],
        replica_groups=[list(range(NCORES))]),
        reads=["e1_in_d"], writes=["e1_out_d"], sem="cc")
    dma("sync", e1sb[:], e1_out.ap(), reads=["e1_out_d"], writes=["u4"])
    st("tensor", lambda e: e.matmul(PC[0:1, 8:16], colsB[0:8, 39:40], e1sb[:],
                                    start=True, stop=True),
       reads=["u4", "colsB"], writes=["PC"])
    st("vector", lambda e: e.tensor_copy(prow[:, 8:16], PC[0:1, 8:16]),
       reads=["PC"], writes=["u4"])
    st("tensor", lambda e: e.matmul(PD[:, 0:8], onesr[0:1, 0:128], prow[:, 8:16],
                                    start=True, stop=True),
       reads=["u4", "onesr"], writes=["PD"])
    st("vector", lambda e: e.tensor_copy(msum[:], PD[:, 0:8]),
       reads=["PD"], writes=["msum"])
    st("vector", lambda e: e.tensor_mul(mx_cols[:], msum[:], ninv_t[:]),
       reads=["msum", "ninv_t"], writes=["mx_cols"])
    st("vector", lambda e: e.tensor_scalar_max(mx_cols[:], mx_cols[:], 1e-5),
       reads=["mx_cols"], writes=["mx_cols"])
    st("vector", lambda e: e.tensor_scalar_mul(th_cols[:], mx_cols[:], 0.5),
       reads=["mx_cols"], writes=["th_cols"])
    st("vector", lambda e: e.tensor_scalar_mul(nth_cols[:], th_cols[:], -1.0),
       reads=["th_cols"], writes=["nth_cols"])
    st("vector", lambda e: e.tensor_scalar_mul(m127_cols[:], mx_cols[:], 1.0 / 127.0),
       reads=["mx_cols"], writes=["m127_cols"])
    st("vector", lambda e: e.memset(ccol[:], 1.0 / 16.0), writes=["ccol"])

    e2off = {0: 0, 1: 1024, 2: 2048, 3: 4096, 4: 6144}
    for src, w, slot in slmap:
        dma("sync", wsl_f[:, 0:w], src.ap(), writes=["wsl_f"], tag="p2ld%d" % slot)
        st("vector", (lambda w=w, slot=slot: (lambda e: e.tensor_single_scalar(
            p1b[:, 0:w], wsl_f[:, 0:w], th_cols[:, slot:slot + 1], ALU.is_ge)))(),
           reads=["wsl_f", "th_cols"], writes=["p1b"])
        st("vector", (lambda w=w, slot=slot: (lambda e: e.tensor_single_scalar(
            p2b[:, 0:w], wsl_f[:, 0:w], nth_cols[:, slot:slot + 1], ALU.is_le)))(),
           reads=["wsl_f", "nth_cols"], writes=["p2b"])
        st("vector", (lambda w=w: (lambda e: e.tensor_sub(
            qstg[:, 0:w], p1b[:, 0:w], p2b[:, 0:w])))(),
           reads=["p1b", "p2b"], writes=["qstg"])
        dma("gpsimd", e2_in.ap()[:, e2off[slot]:e2off[slot] + w], qstg[:, 0:w],
            reads=["qstg"], writes=["e2_in_d"], tag="p2st%d" % slot)
    st("vector", lambda e: e.tensor_single_scalar(p1b[:, 0:128], gk1_f[:],
                                                  th_cols[:, 5:6], ALU.is_ge),
       reads=["gk1_f", "th_cols"], writes=["p1b"])
    st("vector", lambda e: e.tensor_single_scalar(p2b[:, 0:128], gk1_f[:],
                                                  nth_cols[:, 5:6], ALU.is_le),
       reads=["gk1_f", "nth_cols"], writes=["p2b"])
    st("vector", lambda e: e.tensor_sub(wgk1_ar[:], p1b[:, 0:128], p2b[:, 0:128]),
       reads=["p1b", "p2b"], writes=["wgk1_ar"])
    dma("sync", wsl_f[0:16, 0:DK], wgk2T_f.ap(), writes=["wsl_f"], tag="p2ldgk2")
    st("vector", lambda e: e.tensor_single_scalar(p1b[0:16, 0:DK], wsl_f[0:16, 0:DK],
                                                  th_cols[0:16, 6:7], ALU.is_ge),
       reads=["wsl_f", "th_cols"], writes=["p1b"])
    st("vector", lambda e: e.tensor_single_scalar(p2b[0:16, 0:DK], wsl_f[0:16, 0:DK],
                                                  nth_cols[0:16, 6:7], ALU.is_le),
       reads=["wsl_f", "nth_cols"], writes=["p2b"])
    st("vector", lambda e: e.tensor_sub(wgk2_ar[:], p1b[0:16, 0:DK], p2b[0:16, 0:DK]),
       reads=["p1b", "p2b"], writes=["wgk2_ar"])

    st("gpsimd", lambda e: e.collective_compute(
        "AllGather", ALU.bypass, ins=[e2_in.ap()], outs=[e2_out.ap()],
        replica_groups=[list(range(NCORES))]),
        reads=["e2_in_d"], writes=["e2_out_d"], sem="cc")

    dma("sync", uph_t, uph_w.ap(), writes=["uph_t"], tag="lduph")
    dma("sync", dnh_t, dnh_w.ap(), writes=["dnh_t"], tag="lddnh")

    for bfi, (low_dram, stoff) in enumerate([(lowU_sl, 0), (lowD_sl, 512)]):
        fbn = lambda j, bfi=bfi: "fb%s%d" % ("UD"[bfi], j)
        def _init(e, stoff=stoff):
            last = None
            for j in range(4):
                last = e.tensor_copy(pfold[:, stoff + j * 128: stoff + (j + 1) * 128],
                                     identt[:])
            return last
        st("vector", _init, reads=["identt"], writes=[fbn(j) for j in range(4)])
        for s in range(7):
            sig = 2 ** s
            lr, lrn = (lowr, "lowr") if s % 2 == 0 else (lowr2, "lowr2")
            dma("sync", lr[:], low_dram.ap()[:, s * 1024:(s + 1) * 1024],
                writes=[lrn], tag="lowld%d_%d" % (bfi, s))
            for j in range(4):
                base = (j * 4) * 64
                Pj, Pjn = (PC, "PC") if j % 2 == 0 else (PD, "PD")
                Sj, Sjn = (PE, "PE") if j % 2 == 0 else (PF, "PF")
                def _wmm(e, base=base, lr=lr, Pj=Pj):
                    last = None
                    for w in range(4):
                        last = e.matmul(Pj[:, w * 64:(w + 1) * 64],
                                        onesr[0:1, 0:128],
                                        lr[0:1, base + w * 64: base + (w + 1) * 64],
                                        start=True, stop=True)
                    return last
                st("tensor", _wmm, reads=[lrn, "onesr"], writes=[Pjn])
                blk = pfold[:, stoff + j * 128: stoff + (j + 1) * 128]
                bv = blk.rearrange("p (u two t) -> p u two t", two=2, t=sig)
                av, bvv = bv[:, :, 0, :], bv[:, :, 1, :]
                def _umul(e, av=av, bvv=bvv, sig=sig, Pj=Pj, Sj=Sj):
                    e.tensor_mul(u4[:, 0:64].rearrange("p (u t) -> p u t", t=sig), av,
                                 Pj[:, 0:64].rearrange("p (u t) -> p u t", t=sig))
                    e.tensor_mul(Sj[:, 0:64].rearrange("p (u t) -> p u t", t=sig), bvv,
                                 Pj[:, 64:128].rearrange("p (u t) -> p u t", t=sig))
                    e.tensor_mul(u4[:, 128:192].rearrange("p (u t) -> p u t", t=sig), av,
                                 Pj[:, 128:192].rearrange("p (u t) -> p u t", t=sig))
                    return e.tensor_mul(Sj[:, 64:128].rearrange("p (u t) -> p u t", t=sig), bvv,
                                        Pj[:, 192:256].rearrange("p (u t) -> p u t", t=sig))
                st("vector", _umul, reads=[fbn(j), Pjn], writes=["u4", Sjn])
                def _upd(e, av=av, bvv=bvv, sig=sig, Sj=Sj):
                    e.tensor_add(av, u4[:, 0:64].rearrange("p (u t) -> p u t", t=sig),
                                 Sj[:, 0:64].rearrange("p (u t) -> p u t", t=sig))
                    return e.tensor_add(bvv, u4[:, 128:192].rearrange("p (u t) -> p u t", t=sig),
                                        Sj[:, 64:128].rearrange("p (u t) -> p u t", t=sig))
                st("vector", _upd, reads=["u4", Sjn], writes=[fbn(j)])
    dma("sync", e3_in.ap(), pfold[:, 0:1024],
        reads=["fbU%d" % j for j in range(4)] + ["fbD%d" % j for j in range(4)],
        writes=["e3_in_d"])
    st("gpsimd", lambda e: e.collective_compute(
        "AllGather", ALU.bypass, ins=[e3_in.ap()], outs=[e3_out.ap()],
        replica_groups=[list(range(NCORES))]),
        reads=["e3_in_d"], writes=["e3_out_d"], sem="cc")
    for r_ in range(8):
        dma("sync", bltU.rearrange("p (r j t) -> p r j t", r=8, j=4)[:, r_],
            e3_out.ap()[r_ * 128:(r_ + 1) * 128, 0:512].rearrange("p (j t) -> p j t", j=4),
            reads=["e3_out_d"], writes=["bltU", "lowr"], tag="unpBltU%d" % r_)

    def high_pairs(sidx):
        sig = 2 ** sidx
        return [(u, u + sig) for u in range(32) if (u >> sidx) & 1 == 0]

    def wcol(tbl, sidx, pi, w):
        idx = (sidx * 16 + pi) * 4 + w
        return tbl[:, idx:idx + 1]

    def fold_high(tbl, tbln, reverse, swap):
        order = range(4, -1, -1) if reverse else range(5)
        for sidx in order:
            for pi, (ua, ub) in enumerate(high_pairs(sidx)):
                A = pfold[:, ua * 128:(ua + 1) * 128]
                Bc = pfold[:, ub * 128:(ub + 1) * 128]
                w00 = wcol(tbl, sidx, pi, 0)
                w01 = wcol(tbl, sidx, pi, 2 if swap else 1)
                w10 = wcol(tbl, sidx, pi, 1 if swap else 2)
                w11 = wcol(tbl, sidx, pi, 3)
                ts_, tsn = (tsc, "u4") if pi % 2 == 0 else (PF[:, 0:256], "PF")
                pn = ["pf%d" % (ua // 8), "pf%d" % (ub // 8)]
                st("vector", (lambda A=A, Bc=Bc, w01=w01, w10=w10, ts_=ts_: (lambda e: (
                    e.tensor_single_scalar(ts_[:, 0:128], Bc, w01, ALU.mult),
                    e.tensor_single_scalar(ts_[:, 128:256], A, w10, ALU.mult))[-1]))(),
                   reads=pn + [tbln], writes=[tsn])
                st("vector", (lambda A=A, Bc=Bc, w00=w00, w11=w11, ts_=ts_: (lambda e: (
                    e.scalar_tensor_tensor(A, A, w00, ts_[:, 0:128], ALU.mult, ALU.add),
                    e.scalar_tensor_tensor(Bc, Bc, w11, ts_[:, 128:256], ALU.mult, ALU.add))[-1]))(),
                   reads=[tsn, tbln], writes=pn)

    dma("sync", pfold.rearrange("p (u d) -> p u d", u=32),
        wexpT_sl.ap().rearrange("(u p) d -> p u d", p=128),
        writes=["pf0", "pf1", "pf2", "pf3"], tag="ldwexp")
    for unp_e, (nm_, ar_, off_, w_) in zip(
            ("gpsimd", "gpsimd", "gpsimd", "gpsimd"),
            [("wq_ar", wq_ar, 0, DK), ("wk_ar", wk_ar, 1024, DK),
             ("wv_ar", wv_ar, 2048, DV), ("wg_ar", wg_ar, 4096, DV)]):
        dma(unp_e,
            ar_.rearrange("p (c w) -> p c w", c=8),
            e2_out.ap().rearrange("(c p) w -> p c w", p=128)[:, :, off_:off_ + w_],
            reads=["e2_out_d"], writes=[nm_], tag="unp" + nm_)
    for rnd in range(4):
        def _pmm(e, rnd=rnd):
            last = None
            for j in range(8):
                u = rnd * 8 + j
                last = e.matmul(PA[:, j * 128:(j + 1) * 128],
                                bltU[:, u * 128:(u + 1) * 128],
                                pfold[:, u * 128:(u + 1) * 128],
                                start=True, stop=True)
            return last
        st("tensor", _pmm, reads=["bltU", "pf%d" % rnd], writes=["PA"], tag="w1p%d" % rnd)
        st("scalar", (lambda rnd=rnd: (lambda e: e.activation(
            pfold[:, rnd * 1024:(rnd + 1) * 1024], PA[:], AF.Copy)))(),
           reads=["PA"], writes=["pf%d" % rnd], tag="w1pe%d" % rnd)
    fold_high(uph_t, "uph_t", reverse=False, swap=False)
    for rnd in range(4):
        def _w1t(e, rnd=rnd):
            last = None
            for j in range(8):
                u = rnd * 8 + j
                last = e.transpose(PA[:, j * 128:(j + 1) * 128],
                                   pfold[:, u * 128:(u + 1) * 128], identt[:])
            return last
        st("tensor", _w1t, reads=["pf%d" % rnd, "identt"], writes=["PA"], tag="w1t%d" % rnd)
        st("scalar", (lambda rnd=rnd: (lambda e: e.activation(
            qstg[:, 0:1024], PA[:], AF.Copy)))(),
           reads=["PA"], writes=["qstg"])
        dma("sync", e4_in.ap()[:, rnd * 1024:(rnd + 1) * 1024], qstg[:, 0:1024],
            reads=["qstg"], writes=["e4_in_d"], tag="w1st%d" % rnd)

    dma("sync", bex_t, bexp_c.ap(), writes=["bex_t"], tag="ldbex")
    def _bmm(e):
        last = None
        for u in range(32):
            last = e.matmul(PC[:, u:u + 1], bltU[:, u * 128:(u + 1) * 128],
                            bex_t[:, u:u + 1], start=True, stop=True)
        return last
    st("tensor", _bmm, reads=["bltU", "bex_t"], writes=["PC"])
    st("vector", lambda e: e.tensor_copy(bfold, PC[:, 0:32]),
       reads=["PC"], writes=["bfold"])
    for sidx in range(5):
        sig = 2 ** sidx
        bvw = bfold.rearrange("p (q two r) -> p q two r", two=2, r=sig)
        av, bv = bvw[:, :, 0, :], bvw[:, :, 1, :]
        wv = uph_t[:, sidx * 64:(sidx + 1) * 64].rearrange(
            "p (q r w) -> p w q r", r=sig, w=4)
        w00, w01, w10, w11 = wv[:, 0], wv[:, 1], wv[:, 2], wv[:, 3]
        u3 = lambda a, b: u4[:, a:b].rearrange("p (q r) -> p q r", r=sig)
        def _bs1(e, av=av, bv=bv, w01=w01, w10=w10, u3=u3):
            e.tensor_mul(u3(0, 16), bv, w01)
            return e.tensor_mul(u3(16, 32), av, w10)
        st("vector", _bs1, reads=["bfold", "uph_t"], writes=["u4"])
        def _bs2(e, av=av, bv=bv, w00=w00, w11=w11, u3=u3):
            e.tensor_mul(u3(32, 48), av, w00)
            return e.tensor_mul(u3(48, 64), bv, w11)
        st("vector", _bs2, reads=["bfold", "uph_t"], writes=["u4"])
        def _bs3(e, av=av, bv=bv, u3=u3):
            e.tensor_add(av, u3(32, 48), u3(0, 16))
            return e.tensor_add(bv, u3(48, 64), u3(16, 32))
        st("vector", _bs3, reads=["u4"], writes=["bfold"])
    st("vector", lambda e: e.tensor_add(bfold, bfold, upb_ct[:]),
       reads=["bfold", "upb_ct"], writes=["bfold"])
    dma("sync", btot_dr.ap(), bfold, reads=["bfold"], writes=["btot_d"])
    st("vector", lambda e: e.tensor_add(bfold[:, 0:8], bcon_ct[:], dnb_ct[:]),
       reads=["bcon_ct", "dnb_ct"], writes=["bfold"])
    st("tensor", lambda e: e.transpose(PD[0:8, 0:128], bfold[:, 0:8], identt[:]),
       reads=["bfold", "identt"], writes=["PD"])
    st("vector", lambda e: e.tensor_copy(pmx[0:8, :], PD[0:8, 0:128]),
       reads=["PD"], writes=["u4"])
    dma("sync", b2_dr.ap().rearrange("o (u p) -> o u p", u=8), pmx[0:8, :],
        reads=["u4"], writes=["b2_d"])

    for r_ in range(8):
        dma("sync", bltU.rearrange("p (r j t) -> p r j t", r=8, j=4)[:, r_],
            e3_out.ap()[r_ * 128:(r_ + 1) * 128, 512:1024].rearrange("p (j t) -> p j t", j=4),
            reads=["e3_out_d"], writes=["bltU"], tag="unpBltD%d" % r_)
    for rnd in range(4):
        def _trs(e, rnd=rnd):
            last = None
            for j in range(8):
                u = rnd * 8 + j
                last = e.transpose(PA[:, j * 128:(j + 1) * 128],
                                   bltU[:, u * 128:(u + 1) * 128], identt[:])
            return last
        st("tensor", _trs, reads=["bltU", "identt"], writes=["PA"], tag="bltT%d" % rnd)
        st("scalar", (lambda rnd=rnd: (lambda e: e.activation(
            bltU[:, rnd * 1024:(rnd + 1) * 1024], PA[:], AF.Copy)))(),
           reads=["PA"], writes=["bltU"])
    dma("sync", pfold.rearrange("p (u o) -> p u o", u=32),
        wcon_sl.ap().rearrange("(u p) o -> p u o", p=128),
        writes=["pf0", "pf1", "pf2", "pf3"], tag="ldwcon")
    fold_high(dnh_t, "dnh_t", reverse=True, swap=True)
    for rnd in range(4):
        def _w2mm(e, rnd=rnd):
            last = None
            for j in range(8):
                u = rnd * 8 + j
                last = e.matmul(PA[:, j * 128:(j + 1) * 128],
                                bltU[:, u * 128:(u + 1) * 128],
                                pfold[:, u * 128:(u + 1) * 128],
                                start=True, stop=True)
            return last
        st("tensor", _w2mm, reads=["bltU", "pf%d" % rnd], writes=["PA"], tag="w2m%d" % rnd)
        st("scalar", (lambda rnd=rnd: (lambda e: e.activation(
            qstg[:, 0:1024], PA[:], AF.Copy)))(),
           reads=["PA"], writes=["qstg"])
        dma("sync", e4_in.ap()[:, 4096 + rnd * 1024: 4096 + (rnd + 1) * 1024],
            qstg[:, 0:1024], reads=["qstg"], writes=["e4_in_d"], tag="w2st%d" % rnd)


    st("gpsimd", lambda e: e.memset(S_bf[:], 0.0),
       writes=["S_bf0", "S_bf1", "S_bf2", "S_bf3"])
    dma("sync", ind4[:], ind4_in.ap(), writes=["ind4"], tag="ldind4")

    betav = beta[:].rearrange("p (c t) -> p c t", c=8)
    bmidv = betav[:, :, 63]
    bendv = betav[:, :, 127]

    def quant_fm(nslot, mslot, xq_dst, xq_name, tag, dq_dst, dq_name):
        def _tmul(e):
            last = None
            for c in range(8):
                last = e.tensor_single_scalar(
                    tp[:, c * 128:(c + 1) * 128], xh_fm[:, c * 128:(c + 1) * 128],
                    ncols[:, nslot * 8 + c: nslot * 8 + c + 1], ALU.mult)
            return last
        st("vector", _tmul, reads=["xh_fm", "ncols"], writes=["tp"], tag=tag + "_t")
        st("vector", lambda e: e.reduce_max(
            pmx[:], tp[:, 0:1024].rearrange("p (c t) -> p t c", c=8),
            axis=AX.X, apply_absolute_value=True),
           reads=["tp"], writes=["u4"], tag=tag + "_pm")
        st("tensor", lambda e: e.transpose(PD[:, 0:128], pmx[:], identt[:]),
           reads=["u4", "identt"], writes=["PD"], tag=tag + "_pmT")
        st("vector", lambda e: e.reduce_max(colsA[:, 5:6], PD[:, 0:128], axis=AX.X),
           reads=["PD"], writes=["colsA"])
        st("vector", lambda e: e.tensor_scalar_max(colsA[:, 6:7], colsA[:, 5:6], 1e-5),
           reads=["colsA"], writes=["colsA"])
        st("vector", lambda e: e.reciprocal(colsA[:, 7:8], colsA[:, 6:7]),
           reads=["colsA"], writes=["colsA"])
        st("vector", lambda e: e.tensor_scalar_mul(colsA[:, 8:9], colsA[:, 7:8], 127.0),
           reads=["colsA"], writes=["colsA"])
        st("vector", (lambda mslot=mslot, dq_dst=dq_dst: (lambda e: e.tensor_mul(
            dq_dst, colsA[:, 6:7], m127_cols[:, mslot:mslot + 1])))(),
           reads=["colsA", "m127_cols"], writes=[dq_name])
        st("tensor", lambda e: e.transpose(PD[0:1, 128:256], colsA[:, 8:9], identt[:]),
           reads=["colsA", "identt"], writes=["PD"])
        st("vector", lambda e: e.tensor_copy(srow[0:1, 0:128], PD[0:1, 128:256]),
           reads=["PD"], writes=["u4"])
        st("tensor", lambda e: e.matmul(PD[:, 256:384], onesr[0:1, 0:128],
                                        srow[0:1, 0:128], start=True, stop=True),
           reads=["u4", "onesr"], writes=["PD"])
        def _usc(e):
            last = None
            for c in range(8):
                last = e.tensor_mul(usc[:, c * 128:(c + 1) * 128],
                                    tp[:, c * 128:(c + 1) * 128], PD[:, 256:384])
            return last
        st("vector", _usc, reads=["tp", "PD"], writes=["tp"])
        st("vector", lambda e: e.tensor_scalar_add(rint[:, 0:1024], usc[:, 0:1024], 256.0),
           reads=["tp"], writes=["rint"])
        st("vector", lambda e: e.scalar_tensor_tensor(
            xq_dst[:, 0:1024], rint[:, 0:1024], -255.5, usc[:, 0:1024],
            ALU.add, ALU.is_le),
           reads=["rint", "tp"], writes=[xq_name])
        st("vector", lambda e: e.scalar_tensor_tensor(
            xq_dst[:, 0:1024], rint[:, 0:1024], -256.0, xq_dst[:, 0:1024],
            ALU.add, ALU.add),
           reads=["rint", xq_name], writes=[xq_name], tag=tag + "_dq")

    def _projfm(e, war, xqt, P):
        last = None
        for m in range(8):
            for kk in range(8):
                last = e.matmul(P[:, m * 128:(m + 1) * 128],
                                war[:, kk * 1024 + m * 128: kk * 1024 + (m + 1) * 128],
                                xqt[:, kk * 128:(kk + 1) * 128],
                                start=(kk == 0), stop=(kk == 7))
        return last

    def _projtm(e, war, xqt):
        last = None
        pslices = [PA[:, 0:512], PA[:, 512:1024], PB[:, 0:512], PB[:, 512:1024]]
        for kk in range(8):
            for n in range(4):
                last = e.matmul(pslices[n],
                                xqt[:, kk * 128:(kk + 1) * 128],
                                war[:, kk * 2048 + n * 512: kk * 2048 + (n + 1) * 512],
                                start=(kk == 0), stop=(kk == 7))
        return last

    SITES = [("q", 0, 0, xq_a, "xq_a"),
             ("k", 1, 1, xq_b, "xq_b"),
             ("v", 2, 2, xq_c, "xq_c"),
             ("g", 3, 3, xq_d, "xq_d")]

    def S0(ti):
        tg = "A%d" % ti
        dma("sync", xt[:], xs.ap()[ti * 128:(ti + 1) * 128, :], writes=["xt"], tag=tg)
        def _bns(e):
            e.bn_stats(stats[:, 0:6], xt[:, 0:512])
            return e.bn_stats(stats[:, 6:12], xt[:, 512:1024])
        st("vector", _bns, reads=["xt"], writes=["stats"])
        st("vector", lambda e: e.bn_aggr(colsA[:, 0:2], stats[:]),
           reads=["stats"], writes=["colsA"])
        st("scalar", lambda e: e.activation(colsA[:, 2:3], colsA[:, 1:2], AF.Ln,
                                            bias=epsc[:, 0:1]),
           reads=["colsA", "epsc"], writes=["colsA"])
        st("scalar", lambda e: e.activation(colsA[:, 3:4], colsA[:, 2:3], AF.Exp,
                                            scale=-0.5),
           reads=["colsA"], writes=["colsA"])
        st("vector", lambda e: e.tensor_scalar(colsA[:, 4:5], colsA[:, 0:1],
                                               colsA[:, 3:4], -1.0, ALU.mult, ALU.mult),
           reads=["colsA"], writes=["colsA"])
        st("vector", lambda e: e.tensor_scalar(xh_tm[:], xt[:], colsA[:, 3:4],
                                               colsA[:, 4:5], ALU.mult, ALU.add),
           reads=["xt", "colsA"], writes=["xh_tm"])

    def S1(ti):
        tg = "A%d" % ti
        def _xT(e):
            last = None
            for c in range(8):
                last = e.transpose(PB[:, c * 128:(c + 1) * 128],
                                   xh_tm[:, c * 128:(c + 1) * 128], identt[:])
            return last
        st("tensor", _xT, reads=["xh_tm", "identt"], writes=["PB"])
        st("scalar", lambda e: e.activation(xh_fm[:], PB[:], AF.Copy),
           reads=["PB"], writes=["xh_fm"])
        if ti == 0 and DBG:
            dma("sync", dbg_dr.ap()[0][:, 0:1024], xh_tm[:], reads=["xh_tm"], writes=["dbg0"])
            dma("sync", dbg_dr.ap()[1][:, 0:1024], xh_fm[:], reads=["xh_fm"], writes=["dbg1"])

    def S2_chunks(ti):
        tg = "A%d" % ti
        chunks = []
        def _stage1(j):
            snm, nslot, mslot, xq, xqn = SITES[j]
            tb, tbn = (tp, "tp") if j % 2 == 0 else (sgm, "sgm")
            def _tmul(e, nslot=nslot, tb=tb):
                last = None
                for c in range(8):
                    last = e.tensor_single_scalar(
                        tb[:, c * 128:(c + 1) * 128], xh_fm[:, c * 128:(c + 1) * 128],
                        ncols[:, nslot * 8 + c: nslot * 8 + c + 1], ALU.mult)
                return last
            st("vector", _tmul, reads=["xh_fm", "ncols"], writes=[tbn], tag=tg + snm + "_t")
        def _stage1pm(j):
            snm, nslot, mslot, xq, xqn = SITES[j]
            tb, tbn = (tp, "tp") if j % 2 == 0 else (sgm, "sgm")
            pm, pmn = (pmA, "pmA") if j < 2 else (pmB, "pmB")
            st("vector", (lambda tb=tb, pm=pm: (lambda e: e.reduce_max(
                pm, tb[:, 0:1024].rearrange("p (c t) -> p t c", c=8),
                axis=AX.X, apply_absolute_value=True)))(),
               reads=[tbn], writes=[pmn], tag=tg + snm + "_pm")
            st("tensor", (lambda pm=pm, j=j: (lambda e: e.transpose(
                PC[:, j * 128:(j + 1) * 128], pm, identt[:])))(),
               reads=[pmn, "identt"], writes=["PC"], tag=tg + snm + "_pmT")
        chunks.append(lambda: (_stage1(0), _stage1pm(0)))
        chunks.append(lambda: (_stage1(2), _stage1pm(2)))
        chunks.append(lambda: (_stage1(1), _stage1pm(1), _stage1(3), _stage1pm(3)))
        def _stage2():
            for j in range(4):
                st("vector", (lambda j=j: (lambda e: e.reduce_max(
                    colsA[:, 8 + j:9 + j], PC[:, j * 128:(j + 1) * 128], axis=AX.X)))(),
                   reads=["PC"], writes=["colsA"])
            st("vector", lambda e: e.tensor_scalar_max(colsA[:, 12:16], colsA[:, 8:12], 1e-5),
               reads=["colsA"], writes=["colsA"])
            st("vector", lambda e: e.reciprocal(colsA[:, 16:20], colsA[:, 12:16]),
               reads=["colsA"], writes=["colsA"])
            st("vector", lambda e: e.tensor_scalar_mul(colsA[:, 20:24], colsA[:, 16:20], 127.0),
               reads=["colsA"], writes=["colsA"])
            dqds = [(colsQ[:, ti:ti + 1], "colsQ"), (colsB[:, 41:42], "dqk"),
                    (colsB[:, 42:43], "dqv"), (colsB[:, 43:44], "dqg")]
            for j, (snm, nslot, mslot, xq, xqn) in enumerate(SITES):
                dqd, dqn = dqds[j]
                st("vector", (lambda j=j, dqd=dqd, mslot=mslot: (lambda e: e.tensor_mul(
                    dqd, colsA[:, 12 + j:13 + j], m127_cols[:, mslot:mslot + 1])))(),
                   reads=["colsA", "m127_cols"], writes=[dqn])
            st("vector", lambda e: e.tensor_mul(colsB[:, 46:47], colsB[:, 41:42],
                                                colsB[:, 42:43]),
               reads=["dqk", "dqv"], writes=["dqkv"])
            st("tensor", lambda e: e.transpose(PC[0:4, 0:128], colsA[:, 20:24], identt[:]),
               reads=["colsA", "identt"], writes=["PC"])
            st("vector", lambda e: e.tensor_copy(u4[0:4, 0:128], PC[0:4, 0:128]),
               reads=["PC"], writes=["u4"])
            def _bc4(e):
                last = None
                for j in range(4):
                    last = e.matmul(PC[:, j * 128:(j + 1) * 128],
                                    ind4[0:4, j * 128:(j + 1) * 128], u4[0:4, 0:128],
                                    start=True, stop=True)
                return last
            st("tensor", _bc4, reads=["u4", "ind4"], writes=["PC"])
        chunks.append(_stage2)
        def _stage4(j):
            snm, nslot, mslot, xq, xqn = SITES[j]
            def _usc(e, nslot=nslot, j=j):
                last = None
                for c in range(8):
                    last = e.scalar_tensor_tensor(
                        tp[:, c * 128:(c + 1) * 128], xh_fm[:, c * 128:(c + 1) * 128],
                        ncols[:, nslot * 8 + c: nslot * 8 + c + 1],
                        PC[:, j * 128:(j + 1) * 128], ALU.mult, ALU.mult)
                return last
            st("vector", _usc, reads=["xh_fm", "ncols", "PC"], writes=["tp"],
               tag=tg + snm + "_u")
            st("vector", lambda e: e.tensor_scalar_add(
                rint[:, 0:1024], tp[:, 0:1024], 256.0),
               reads=["tp"], writes=["rint"])
            st("vector", (lambda xq=xq: (lambda e: e.scalar_tensor_tensor(
                xq[:, 0:1024], rint[:, 0:1024], -255.5, tp[:, 0:1024],
                ALU.add, ALU.is_le)))(),
               reads=["rint", "tp"], writes=[xqn])
            st("vector", (lambda xq=xq: (lambda e: e.scalar_tensor_tensor(
                xq[:, 0:1024], rint[:, 0:1024], -256.0, xq[:, 0:1024],
                ALU.add, ALU.add)))(),
               reads=["rint", xqn], writes=[xqn], tag=tg + snm + "_r")
        for _j in range(4):
            chunks.append((lambda _j=_j: (lambda: _stage4(_j)))())
        return chunks

    def S3q(ti):
        tg = "A%d" % ti
        if ti == 0 and DBG:
            dma("gpsimd", dbg_dr.ap()[2][:, 0:1024], xq_a[:], reads=["xq_a"], writes=["dbg2"])
        st("tensor", lambda e: _projfm(e, wq_ar, xq_a, PA),
           reads=["wq_ar", "xq_a"], writes=["PA"], tag=tg + "qmm")
        st("scalar", lambda e: e.activation(q_fm[:], PA[:], AF.Copy),
           reads=["PA"], writes=["q_fm"])
        if ti == 0 and DBG:
            dma("gpsimd", dbg_dr.ap()[3][:, 0:1024], q_fm[:], reads=["q_fm"], writes=["dbg3"])

    def S3k(ti):
        tg = "A%d" % ti
        st("tensor", lambda e: _projfm(e, wk_ar, xq_b, PB),
           reads=["wk_ar", "xq_b"], writes=["PB"], tag=tg + "kmm")
        st("scalar", lambda e: e.activation(k_fm[:], PB[:], AF.Copy),
           reads=["PB"], writes=["k_fm"])

    def S3v(ti):
        tg = "A%d" % ti
        st("tensor", lambda e: _projtm(e, wv_ar, xq_c),
           reads=["wv_ar", "xq_c"], writes=["PA", "PB"], tag=tg + "vmm")
        def _vev(e):
            e.activation(v_tm[:, 0:1024], PA[:], AF.Copy, scale=colsB[:, 46:47])
            return e.activation(v_tm[:, 1024:2048], PB[:], AF.Copy,
                                scale=colsB[:, 46:47])
        st("scalar", _vev, reads=["PA", "PB", "dqkv"], writes=["v_tm"])
        if ti == 0 and DBG:
            dma("gpsimd", dbg_dr.ap()[4], v_tm[:], reads=["v_tm"], writes=["dbg4"])

    def S3g(ti):
        tg = "A%d" % ti
        st("tensor", lambda e: _projtm(e, wg_ar, xq_d),
           reads=["wg_ar", "xq_d"], writes=["PA", "PB"], tag=tg + "gmm")
        def _gev(e):
            e.activation(sg_tm[:, 0:1024], PA[:], AF.Silu, scale=colsB[:, 43:44])
            return e.activation(sg_tm[:, 1024:2048], PB[:], AF.Silu,
                                scale=colsB[:, 43:44])
        st("scalar", _gev, reads=["PA", "PB", "dqg"], writes=["sg_tm"])
        dma("sync", sg_dr.ap()[ti], sg_tm[:], reads=["sg_tm"], writes=["sg_dr" + str(ti)], tag=tg + "sgst")

    def S3(ti):
        S3q(ti); S3k(ti); S3v(ti); S3g(ti)

    def S4a(ti):
        tg = "A%d" % ti
        quant_fm(4, 5, xq_a, "xq_a", tg + "g1", colsB[:, 44:45], "dq1")

    def S4b(ti, filler=None):
        tg = "A%d" % ti
        def _y1mm(e):
            last = None
            for kk in range(8):
                last = e.matmul(PD[0:16, 0:128],
                                wgk1_ar[:, kk * 16:(kk + 1) * 16],
                                xq_a[:, kk * 128:(kk + 1) * 128],
                                start=(kk == 0), stop=(kk == 7))
            return last
        st("tensor", _y1mm, reads=["wgk1_ar", "xq_a"], writes=["PD"], tag=tg + "y1")
        st("vector", lambda e: e.tensor_copy(y1sb[:], PD[0:16, 0:128]),
           reads=["PD"], writes=["u4"])
        st("tensor", lambda e: e.transpose(PD[:, 128:144], y1sb[:], identt[0:16, 0:16]),
           reads=["u4", "identt"], writes=["PD"])
        st("vector", lambda e: e.tensor_single_scalar(y1t[:], PD[:, 128:144],
                                                      colsB[:, 44:45], ALU.mult),
           reads=["PD", "dq1"], writes=["y1t"])
        st("vector", lambda e: e.tensor_mul(sq16[:], y1t[:], y1t[:]),
           reads=["y1t"], writes=["sq16"])
        st("vector", lambda e: e.reduce_sum(colsA[:, 10:11], sq16[:], axis=AX.X),
           reads=["sq16"], writes=["colsA"])
        st("vector", lambda e: e.tensor_scalar_mul(colsA[:, 11:12], colsA[:, 10:11], 1.0 / 16.0),
           reads=["colsA"], writes=["colsA"])
        st("scalar", lambda e: e.activation(colsA[:, 12:13], colsA[:, 11:12], AF.Ln,
                                            bias=epsc[:, 1:2]),
           reads=["colsA", "epsc"], writes=["colsA"])
        st("scalar", lambda e: e.activation(colsA[:, 13:14], colsA[:, 12:13], AF.Exp,
                                            scale=-0.5),
           reads=["colsA"], writes=["colsA"])
        st("vector", lambda e: e.tensor_mul(t2g[:], y1t[:], ngk2_t[:]),
           reads=["y1t", "ngk2_t"], writes=["t2g"])
        st("vector", lambda e: e.reduce_max(colsA[:, 14:15], t2g[:], axis=AX.X,
                                            apply_absolute_value=True),
           reads=["t2g"], writes=["colsA"])
        st("vector", lambda e: e.tensor_mul(colsA[:, 15:16], colsA[:, 14:15], colsA[:, 13:14]),
           reads=["colsA"], writes=["colsA"])
        st("vector", lambda e: e.tensor_scalar_max(colsB[:, 32:33], colsA[:, 15:16], 1e-5),
           reads=["colsA"], writes=["colsB"])
        st("vector", lambda e: e.reciprocal(colsB[:, 33:34], colsB[:, 32:33]),
           reads=["colsB"], writes=["colsB"])
        st("vector", lambda e: e.tensor_scalar_mul(colsB[:, 34:35], colsB[:, 33:34], 127.0),
           reads=["colsB"], writes=["colsB"])
        st("vector", lambda e: e.tensor_mul(colsB[:, 35:36], colsA[:, 13:14], colsB[:, 34:35]),
           reads=["colsA", "colsB"], writes=["colsB"])
        st("vector", lambda e: e.tensor_mul(colsB[:, 36:37], colsB[:, 32:33], m127_cols[:, 6:7]),
           reads=["colsB", "m127_cols"], writes=["colsB"])
        st("vector", lambda e: e.tensor_single_scalar(sq16[:], t2g[:], colsB[:, 35:36],
                                                      ALU.mult),
           reads=["t2g", "colsB"], writes=["sq16"])
        st("vector", lambda e: e.tensor_scalar_add(rq2[:], sq16[:], 256.0),
           reads=["sq16"], writes=["rq2"])
        st("vector", lambda e: e.scalar_tensor_tensor(t2g[:], rq2[:], -255.5, sq16[:],
                                                      ALU.add, ALU.is_le),
           reads=["rq2", "sq16"], writes=["t2g"])
        st("vector", lambda e: e.scalar_tensor_tensor(sq16[:], rq2[:], -256.0, t2g[:],
                                                      ALU.add, ALU.add),
           reads=["rq2", "t2g"], writes=["sq16"])
        st("vector", lambda e: e.tensor_single_scalar(sq16[:], sq16[:], colsB[:, 36:37],
                                                      ALU.mult),
           reads=["sq16", "colsB"], writes=["sq16"])
        st("tensor", lambda e: e.transpose(PD[0:16, 384:512], sq16[:], identt[:]),
           reads=["sq16", "identt"], writes=["PD"])
        st("vector", lambda e: e.tensor_copy(xq2_fm[:], PD[0:16, 384:512]),
           reads=["PD"], writes=["xq2_fm"])
        def _zmm(e):
            last = None
            for m in range(8):
                last = e.matmul(PA[:, m * 128:(m + 1) * 128],
                                wgk2_ar[:, m * 128:(m + 1) * 128], xq2_fm[:],
                                start=True, stop=True)
            return last
        st("tensor", _zmm, reads=["wgk2_ar", "xq2_fm"],
           writes=["PA"], tag=tg + "z")
        if filler is not None:
            filler()
        def _sig(e):
            last = None
            for m in range(8):
                last = e.activation(sgm[:, m * 128:(m + 1) * 128],
                                    PA[:, m * 128:(m + 1) * 128], AF.Sigmoid,
                                    bias=msum[:, m:m + 1])
            return last
        st("scalar", _sig, reads=["PA", "bgk2_c"], writes=["sgm"])
        st("scalar", lambda e: e.activation(lsg[:], sgm[:], AF.Ln),
           reads=["sgm"], writes=["sgm"])
        def _cums(e):
            last = None
            for c in range(8):
                last = e.tensor_tensor_scan(beta[:, c * 128:(c + 1) * 128],
                                            lsg[:, c * 128:(c + 1) * 128],
                                            lsg[:, c * 128:(c + 1) * 128],
                                            0.0, ALU.add, ALU.bypass)
            return last
        st("vector", _cums, reads=["sgm", "ones_t"], writes=["beta"], tag=tg + "cum")
        if ti == 0 and DBG:
            dma("sync", dbg_dr.ap()[5][:, 0:1024], beta[:], reads=["beta"], writes=["dbg5"])
        def _bcols(e):
            e.tensor_scalar_mul(colsB[:, 0:8], bmidv, -1.0 / 16.0)
            e.tensor_scalar_mul(colsB[:, 8:16], bmidv, 1.0 / 16.0)
            return e.tensor_scalar_mul(colsB[:, 16:24], bendv, 1.0 / 16.0)
        st("vector", _bcols, reads=["beta"], writes=["colsB"])
        st("scalar", lambda e: e.activation(colsB[:, 24:32], bendv, AF.Exp, scale=1.0 / 16.0),
           reads=["beta"], writes=["colsB_lam"])
        st("scalar", lambda e: e.activation(eoff_t[:], offb_t[:], AF.Exp, scale=1.0 / 16.0),
           reads=["offb_t"], writes=["eoff_t"])
        st("vector", lambda e: e.tensor_add(offb_t[:], offb_t[:], bendv),
           reads=["offb_t", "beta"], writes=["offb_t"])
        st("vector", lambda e: e.tensor_mul(dprod[:], dprod[:], colsB[:, 24:32]),
           reads=["dprod", "colsB_lam"], writes=["dprod"])
        st("scalar", lambda e: e.activation(El_t[:], beta[:], AF.Exp, scale=1.0 / 16.0),
           reads=["beta"], writes=["El_t"])

    def S5(ti):
        tg = "A%d" % ti
        for h in range(H):
            c0, c1 = 2 * h, 2 * h + 1
            hsl = slice(c0 * 128, (c1 + 1) * 128)
            Es, esn = (EsA, "EsA") if h % 2 == 0 else (EsB, "EsB")
            Eq_s, Ek_s, EK_s = Es[:, 0:256], Es[:, 256:512], Es[:, 512:768]
            def _es(e, c0=c0, c1=c1, Eq_s=Eq_s, Ek_s=Ek_s, EK_s=EK_s):
                last = None
                for (dst, sc, boff) in ((Eq_s, 1.0 / 16.0, 0), (Ek_s, -1.0 / 16.0, 8),
                                        (EK_s, -1.0 / 16.0, 16)):
                    e.activation(dst[:, 0:128], beta[:, c0 * 128:(c0 + 1) * 128],
                                 AF.Exp, scale=sc, bias=colsB[:, boff + c0:boff + c0 + 1])
                    last = e.activation(dst[:, 128:256], beta[:, c1 * 128:(c1 + 1) * 128],
                                        AF.Exp, scale=sc, bias=colsB[:, boff + c1:boff + c1 + 1])
                return last
            st("scalar", _es, reads=["beta", "colsB"], writes=[esn], tag=tg + "es%d" % h)
            def _mults(e, hsl=hsl, h=h, Eq_s=Eq_s, Ek_s=Ek_s):
                e.tensor_mul(qt_h[:], q_fm[:, hsl], Eq_s)
                return e.tensor_mul(kt_h[:], k_fm[:, hsl], Ek_s)
            st("vector", _mults, reads=["q_fm", "k_fm", esn],
               writes=["qt_h", "kt_h"], tag=tg + "mu%d" % h)
            def _mults2(e, hsl=hsl, h=h, EK_s=EK_s):
                e.tensor_mul(KHp[:], k_fm[:, hsl], EK_s)
                return e.tensor_mul(qh4[:, h * 256:(h + 1) * 256], q_fm[:, hsl],
                                    El_t[:, hsl])
            st("vector", _mults2, reads=["q_fm", "k_fm", esn, "El_t"],
               writes=["KHp", "qh4"], tag=tg + "mu2_%d" % h)
            def _qbar(e, c0=c0, c1=c1, h=h):
                e.tensor_single_scalar(qb_h[:, 0:128], qh4[:, h * 256:h * 256 + 128],
                                       eoff_t[:, c0:c0 + 1], ALU.mult)
                return e.tensor_single_scalar(qb_h[:, 128:256],
                                              qh4[:, h * 256 + 128:h * 256 + 256],
                                              eoff_t[:, c1:c1 + 1], ALU.mult)
            st("vector", _qbar, reads=["qh4", "eoff_t"], writes=["qb_h"])
            dma("sync", qb_dr.ap()[ti][:, c0 * 128:(c1 + 1) * 128], qb_h[:],
                reads=["qb_h"], writes=["qb_dr%d_%d" % (ti, h)], tag=tg + "qbst%d" % h)
            def _amm(e):
                e.matmul(PD[:, 0:128], kt_h[:, 0:128], qt_h[:, 0:128], start=True, stop=False)
                return e.matmul(PD[:, 0:128], kt_h[:, 128:256], qt_h[:, 128:256],
                                start=False, stop=True)
            st("tensor", _amm, reads=["kt_h", "qt_h"], writes=["PD"], tag=tg + "A%d" % h)
            st("vector", (lambda h=h: (lambda e: e.tensor_mul(
                am4[:, h * 128:(h + 1) * 128], PD[:, 0:128], maskt[:])))(),
               reads=["PD", "maskt"], writes=["am4"])
            def _kT(e):
                e.transpose(PD[:, 128:256], KHp[:, 0:128], identt[:])
                return e.transpose(PD[:, 256:384], KHp[:, 128:256], identt[:])
            st("tensor", _kT, reads=["KHp", "identt"], writes=["PD"])
            st("vector", (lambda h=h: (lambda e: e.tensor_copy(
                khtm4[:, h * 256:(h + 1) * 256], PD[:, 128:384])))(),
               reads=["PD"], writes=["khtm4"])

    def S6(ti, heads):
        tg = "A%d" % ti
        for h in heads:
            c0, c1 = 2 * h, 2 * h + 1
            sfn, sbn = "S_f%d" % h, "S_bf%d" % h
            def _umm(e, h=h):
                e.matmul(PD[:], khtm4[:, h * 256:h * 256 + 128],
                         v_tm[:, h * 512:(h + 1) * 512], start=True, stop=True)
                return e.matmul(PE[:], khtm4[:, h * 256 + 128:h * 256 + 256],
                                v_tm[:, h * 512:(h + 1) * 512], start=True, stop=True)
            st("tensor", _umm, reads=["khtm4", "v_tm"], writes=["PD", "PE"],
               tag=tg + "U%d" % h)
            def _omm(e, h=h, c0=c0, c1=c1):
                e.matmul(PF[:], am4[:, h * 128:(h + 1) * 128],
                         v_tm[:, h * 512:(h + 1) * 512], start=True, stop=False)
                e.matmul(PF[:], qh4[:, h * 256:h * 256 + 128],
                         S_bf[:, c0 * 512:(c0 + 1) * 512], start=False, stop=False)
                return e.matmul(PF[:], qh4[:, h * 256 + 128:h * 256 + 256],
                                S_bf[:, c1 * 512:(c1 + 1) * 512], start=False, stop=True)
            st("tensor", _omm, reads=["am4", "v_tm", "qh4", sbn], writes=["PF"],
               tag=tg + "o%d" % h)
            st("scalar", (lambda h=h, ti=ti: (lambda e: e.activation(
                op_t[:, h * 512:(h + 1) * 512], PF[:], AF.Copy,
                scale=colsQ[:, ti:ti + 1])))(),
               reads=["PF", "colsQ"], writes=["op_t"], tag=tg + "oe%d" % h)
            def _supd(e, c0=c0, c1=c1):
                e.scalar_tensor_tensor(S_f[:, c0 * 512:(c0 + 1) * 512],
                                       S_f[:, c0 * 512:(c0 + 1) * 512],
                                       colsB[:, 24 + c0:24 + c0 + 1], PD[:],
                                       ALU.mult, ALU.add)
                return e.scalar_tensor_tensor(S_f[:, c1 * 512:(c1 + 1) * 512],
                                              S_f[:, c1 * 512:(c1 + 1) * 512],
                                              colsB[:, 24 + c1:24 + c1 + 1], PE[:],
                                              ALU.mult, ALU.add)
            st("vector", _supd, reads=["PD", "PE", "colsB_lam", sfn], writes=[sfn],
               tag=tg + "S%d" % h)
            st("scalar", (lambda c0=c0: (lambda e: e.activation(
                S_bf[:, c0 * 512:(c0 + 2) * 512], S_f[:, c0 * 512:(c0 + 2) * 512],
                AF.Copy)))(),
               reads=[sfn], writes=[sbn], tag=tg + "Sbf%d" % h)

    S0(0); S1(0)
    for ch in S2_chunks(0):
        ch()
    S3(0)
    st("gpsimd", lambda e: e.collective_compute(
        "AllGather", ALU.bypass, ins=[e4_in.ap()], outs=[e4_out.ap()],
        replica_groups=[list(range(NCORES))]),
        reads=["e4_in_d"], writes=["e4_out_d"], sem="cc")
    dma("sync", msum[:], bgk2_row.ap(), writes=["bgk2_c"], tag="ldbgk2")
    S4a(0)
    if NT > 1:
        S0(1); S1(1)
    S4b(0)
    pend = {}
    for ti in range(NT):
        tg = "A%d" % ti
        S5(ti)
        if ti + 1 < NT:
            chunks = pend.pop(ti + 1, None)
            if chunks is None:
                chunks = S2_chunks(ti + 1)
                chunks[0]()
                chunks[1]()
            chunks[2]()
            chunks[3]()
            S6(ti, (0,))
            chunks[4]()
            S6(ti, (1,))
            S3q(ti + 1)
            chunks[5]()
            S6(ti, (2,))
            S3k(ti + 1)
            chunks[6]()
            S6(ti, (3,))
            chunks[7]()
            S3v(ti + 1)
            S3g(ti + 1)
        else:
            S6(ti, (0, 1, 2, 3))
        dma("sync", op_dr.ap()[ti], op_t[:], reads=["op_t"], writes=["op_dr" + str(ti)], tag=tg + "opst")
        if ti + 1 < NT:
            S4a(ti + 1)
            def _filler(ti=ti):
                if ti + 2 < NT:
                    S0(ti + 2)
                    S1(ti + 2)
                    pend[ti + 2] = S2_chunks(ti + 2)
                    pend[ti + 2][0]()
                    pend[ti + 2][1]()
            S4b(ti + 1, _filler)

    b.alias("Sin_f", ["xt", "xh_tm", "xh_fm", "beta"])
    b.alias("Ur_a", ["sgm", "El_t", "tp"])
    b.alias("Ur_b", ["sgm", "El_t", "tp"])
    dma("gpsimd", e5_in.ap()[:, 0:4096], S_f[:],
        reads=["S_f0", "S_f1", "S_f2", "S_f3"], writes=["e5_in_d"])
    st("vector", lambda e: e.memset(Sin_f[:], 0.0), writes=["Sin_f"])
    dma("gpsimd", e5_in.ap()[:, 4096:4104], dprod[:], reads=["dprod"], writes=["e5_in_d"])
    st("gpsimd", lambda e: e.collective_compute(
        "AllGather", ALU.bypass, ins=[e5_in.ap()], outs=[e5_out.ap()],
        replica_groups=[list(range(NCORES))]),
        reads=["e5_in_d"], writes=["e5_out_d"], sem="cc")
    for r in (6, 5, 4, 2, 1, 0):
        dma("gpsimd", Ur_f[:, 0:2048], e5_out.ap()[r * 128:(r + 1) * 128, 0:2048],
            reads=["e5_out_d"], writes=["Ur_a"], tag="DUlda%d" % r)
        dma("gpsimd", Ur_f[:, 2048:4104], e5_out.ap()[r * 128:(r + 1) * 128, 2048:4104],
            reads=["e5_out_d"], writes=["Ur_b"], tag="DUldb%d" % r)
        st("vector", (lambda r=r: (lambda e: e.tensor_single_scalar(
            wrk8[:], gcur[:], bmask_t[:, r:r + 1], ALU.mult)))(),
           reads=["gcur", "bmask_t"], writes=["wrk8"])
        def _acca(e):
            last = None
            for hc in range(4):
                last = e.scalar_tensor_tensor(
                    Sin_f[:, hc * 512:(hc + 1) * 512],
                    Ur_f[:, hc * 512:(hc + 1) * 512],
                    wrk8[:, hc:hc + 1],
                    Sin_f[:, hc * 512:(hc + 1) * 512],
                    ALU.mult, ALU.add)
            return last
        st("vector", _acca, reads=["Ur_a", "wrk8", "Sin_f"], writes=["Sin_f"],
           tag="DUacca%d" % r)
        def _accb(e):
            last = None
            for hc in range(4, 8):
                last = e.scalar_tensor_tensor(
                    Sin_f[:, hc * 512:(hc + 1) * 512],
                    Ur_f[:, hc * 512:(hc + 1) * 512],
                    wrk8[:, hc:hc + 1],
                    Sin_f[:, hc * 512:(hc + 1) * 512],
                    ALU.mult, ALU.add)
            return last
        st("vector", _accb, reads=["Ur_b", "wrk8", "Sin_f"], writes=["Sin_f"],
           tag="DUaccb%d" % r)
        st("vector", (lambda r=r: (lambda e: e.tensor_scalar(
            dmod[:], Ur_f[:, 4096:4104], amask_t[:, r:r + 1], am1_t[:, r:r + 1],
            ALU.mult, ALU.add)))(),
           reads=["Ur_b", "amask_t", "am1_t"], writes=["wrk8"])
        st("vector", lambda e: e.tensor_mul(gcur[:], gcur[:], dmod[:]),
           reads=["gcur", "wrk8"], writes=["gcur"])
    st("gpsimd", lambda e: e.tensor_copy(Sin_bf[:], Sin_f[:]),
       reads=["Sin_f"], writes=["Sin_bf"])

    dma("sync", op_t[:], op_dr.ap()[0], reads=["op_dr0"], writes=["op_t"], tag="B0")
    dma("sync", sg_tm[:], sg_dr.ap()[0], reads=["sg_dr0"], writes=["sg_tm"], tag="B0")
    dma("sync", xq_a[:], qb_dr.ap()[0],
        reads=["qb_dr0_0", "qb_dr0_1", "qb_dr0_2", "qb_dr0_3"],
        writes=["xq_a"], tag="B0")
    for r_ in range(8):
        dma("gpsimd", wo_ar.rearrange("p (r s w) -> p r s w", r=8, s=2)[:, r_],
            e2_out.ap()[r_ * 128:(r_ + 1) * 128, 6144:8192].rearrange("p (s w) -> p s w", s=2),
            reads=["e2_out_d"], writes=["wo_ar"], tag="unpwo%d" % r_)

    b.alias("og", ["Sin_f"])
    b.alias("xtB", ["Ur_a", "Ur_b", "sgm", "El_t"])
    b.alias("ogfm", ["Sin_f"])
    b.alias("x2t", ["Ur_a"])
    b.alias("tp", ["Ur_b"])

    def BL(ti):
        tg = "B%d" % ti
        xb, xbn = (xtB, "xtB") if ti % 2 == 0 else (xtBodd, "xtBodd")
        if ti > 0:
            dma("sync", op_t[:], op_dr.ap()[ti], reads=["op_dr" + str(ti)], writes=["op_t"], tag=tg)
            dma("sync", sg_tm[:], sg_dr.ap()[ti], reads=["sg_dr" + str(ti)], writes=["sg_tm"], tag=tg)
            dma("sync", xq_a[:], qb_dr.ap()[ti],
                reads=["qb_dr%d_0" % ti, "qb_dr%d_1" % ti, "qb_dr%d_2" % ti, "qb_dr%d_3" % ti],
                writes=["xq_a"], tag=tg)
        dma("sync", xb[:], xs.ap()[ti * 128:(ti + 1) * 128, :], writes=[xbn], tag=tg + "x")

    def BC(ti):
        tg = "B%d" % ti
        def _corr(e):
            psl = [PA[:, 0:512], PA[:, 512:1024], PB[:, 0:512], PB[:, 512:1024]]
            last = None
            for h in range(H):
                for cc in range(2):
                    c = 2 * h + cc
                    last = e.matmul(psl[h], xq_a[:, c * 128:(c + 1) * 128],
                                    Sin_bf[:, c * 512:(c + 1) * 512],
                                    start=(cc == 0), stop=(cc == 1))
            return last
        st("tensor", _corr, reads=["xq_a", "Sin_bf"], writes=["PA", "PB"], tag=tg + "corr")

    def B1(ti):
        tg = "B%d" % ti
        def _osum(e, ti=ti):
            e.scalar_tensor_tensor(og[:, 0:1024], PA[:], colsQ[:, ti:ti + 1],
                                   op_t[:, 0:1024], ALU.mult, ALU.add)
            return e.scalar_tensor_tensor(og[:, 1024:2048], PB[:], colsQ[:, ti:ti + 1],
                                          op_t[:, 1024:2048], ALU.mult, ALU.add)
        st("vector", _osum, reads=["op_t", "PA", "PB", "colsQ"], writes=["og"])
        st("vector", lambda e: e.scalar_tensor_tensor(og[:], og[:], ccol[:, 0:1],
                                                      sg_tm[:], ALU.mult, ALU.mult),
           reads=["og", "ccol", "sg_tm"], writes=["og"])
        def _bnso(e):
            last = None
            for c4 in range(4):
                last = e.bn_stats(stats[:, 0:6] if c4 == 0 else stats[:, 6:12] if c4 == 1
                                  else colsB[:, 0:6] if c4 == 2 else colsB[:, 6:12],
                                  og[:, c4 * 512:(c4 + 1) * 512])
            return last
        st("vector", _bnso, reads=["og"], writes=["stats", "colsB"])
        def _bnao(e):
            e.bn_aggr(colsA[:, 0:2], stats[:])
            return e.bn_aggr(colsA[:, 2:4], colsB[:, 0:12])
        st("vector", _bnao, reads=["stats", "colsB"], writes=["colsA"])
        def _msq(e):
            e.tensor_mul(colsA[:, 4:5], colsA[:, 0:1], colsA[:, 0:1])
            return e.tensor_mul(colsA[:, 5:6], colsA[:, 2:3], colsA[:, 2:3])
        st("vector", _msq, reads=["colsA"], writes=["colsA"])
        def _msq2(e):
            e.tensor_add(colsA[:, 6:7], colsA[:, 4:5], colsA[:, 1:2])
            return e.tensor_add(colsA[:, 7:8], colsA[:, 5:6], colsA[:, 3:4])
        st("vector", _msq2, reads=["colsA"], writes=["colsA"])
        st("vector", lambda e: e.tensor_add(colsA[:, 8:9], colsA[:, 6:7], colsA[:, 7:8]),
           reads=["colsA"], writes=["colsA"])
        st("vector", lambda e: e.tensor_scalar_mul(colsA[:, 9:10], colsA[:, 8:9], 0.5),
           reads=["colsA"], writes=["colsA"])
        st("scalar", lambda e: e.activation(colsA[:, 10:11], colsA[:, 9:10], AF.Ln,
                                            bias=epsc[:, 1:2]),
           reads=["colsA", "epsc"], writes=["colsA"])
        st("scalar", lambda e: e.activation(colsA[:, 11:12], colsA[:, 10:11], AF.Exp,
                                            scale=-0.5),
           reads=["colsA"], writes=["colsA"])

    def B2(ti):
        tg = "B%d" % ti
        def _ogT(e):
            last = None
            for c in range(16):
                dst = PA[:, (c % 8) * 128:((c % 8) + 1) * 128] if c < 8 else \
                      PB[:, ((c - 8)) * 128:((c - 8) + 1) * 128]
                last = e.transpose(dst, og[:, c * 128:(c + 1) * 128], identt[:])
            return last
        st("tensor", _ogT, reads=["og", "identt"], writes=["PA", "PB"], tag=tg + "ogT")
        def _ogev(e):
            e.activation(ogfm[:, 0:1024], PA[:], AF.Copy)
            return e.activation(ogfm[:, 1024:2048], PB[:], AF.Copy)
        st("scalar", _ogev, reads=["PA", "PB"], writes=["ogfm"])

    def B3a(ti):
        tg = "B%d" % ti
        for half, (tb, tbn) in enumerate(((tp, "tp"), (x2t, "x2t"))):
            def _tmo(e, half=half, tb=tb):
                last = None
                for c in range(8):
                    cc = half * 8 + c
                    last = e.tensor_single_scalar(tb[:, c * 128:(c + 1) * 128],
                                                  ogfm[:, cc * 128:(cc + 1) * 128],
                                                  no_ct[:, cc:cc + 1], ALU.mult)
                return last
            st("vector", _tmo, reads=["ogfm", "no_ct"], writes=[tbn], tag=tg + "to%d" % half)
            st("vector", (lambda half=half, tb=tb: (lambda e: e.reduce_max(
                pmx[:] if half == 0 else u4[:, 128:256],
                tb[:, 0:1024].rearrange("p (c t) -> p t c", c=8),
                axis=AX.X, apply_absolute_value=True)))(),
               reads=[tbn], writes=["u4"], tag=tg + "pm%d" % half)
        st("vector", lambda e: e.tensor_max(pmx[:], pmx[:], u4[:, 128:256]),
           reads=["u4"], writes=["u4"])
        st("tensor", lambda e: e.transpose(PC[:, 0:128], pmx[:], identt[:]),
           reads=["u4", "identt"], writes=["PC"])
        st("vector", lambda e: e.reduce_max(colsA[:, 12:13], PC[:, 0:128], axis=AX.X),
           reads=["PC"], writes=["colsA"])
        st("vector", lambda e: e.tensor_mul(colsA[:, 13:14], colsA[:, 12:13], colsA[:, 11:12]),
           reads=["colsA"], writes=["colsA"])
        st("vector", lambda e: e.tensor_scalar_max(colsA[:, 14:15], colsA[:, 13:14], 1e-5),
           reads=["colsA"], writes=["colsA"])
        st("vector", lambda e: e.reciprocal(colsA[:, 15:16], colsA[:, 14:15]),
           reads=["colsA"], writes=["colsA"])
        st("vector", lambda e: e.tensor_scalar_mul(colsB[:, 32:33], colsA[:, 15:16], 127.0),
           reads=["colsA"], writes=["colsB"])
        st("vector", lambda e: e.tensor_mul(colsB[:, 33:34], colsA[:, 11:12], colsB[:, 32:33]),
           reads=["colsA", "colsB"], writes=["colsB"])
        st("vector", lambda e: e.tensor_mul(colsB[:, 34:35], colsA[:, 14:15], m127_cols[:, 4:5]),
           reads=["colsA", "m127_cols"], writes=["dqo"])

    def B3bc(ti):
        tg = "B%d" % ti
        st("tensor", lambda e: e.transpose(PC[0:1, 128:256], colsB[:, 33:34], identt[:]),
           reads=["colsB", "identt"], writes=["PC"])
        st("vector", lambda e: e.tensor_copy(srow[0:1, 0:128], PC[0:1, 128:256]),
           reads=["PC"], writes=["u4"])
        st("tensor", lambda e: e.matmul(PC[:, 256:384], onesr[0:1, 0:128],
                                        srow[0:1, 0:128], start=True, stop=True),
           reads=["u4", "onesr"], writes=["PC"])

    def B3bh(ti, half):
        tg = "B%d" % ti
        tb, tbn = (tp, "tp") if half == 0 else (x2t, "x2t")
        def _usco(e, tb=tb):
            last = None
            for c in range(8):
                last = e.tensor_mul(tb[:, c * 128:(c + 1) * 128],
                                    tb[:, c * 128:(c + 1) * 128], PC[:, 256:384])
            return last
        st("vector", _usco, reads=[tbn, "PC"], writes=[tbn])
        st("vector", (lambda tb=tb: (lambda e: e.tensor_scalar_add(
            rint[:], tb[:, 0:1024], 256.0)))(),
           reads=[tbn], writes=["rint"], tag=tg + "ri%d" % half)
        st("vector", (lambda half=half, tb=tb: (lambda e: e.scalar_tensor_tensor(
            v_tm[:, half * 1024:(half + 1) * 1024], rint[:, 0:1024], -255.5,
            tb[:, 0:1024], ALU.add, ALU.is_le)))(),
           reads=["rint", tbn], writes=["v_tm"])
        st("vector", (lambda half=half: (lambda e: e.scalar_tensor_tensor(
            v_tm[:, half * 1024:(half + 1) * 1024], rint[:, 0:1024], -256.0,
            v_tm[:, half * 1024:(half + 1) * 1024], ALU.add, ALU.add)))(),
           reads=["rint", "v_tm"], writes=["v_tm"], tag=tg + "xqo%d" % half)

    def B4p1(ti):
        tg = "B%d" % ti
        for nh, (P, pn) in enumerate(((PE, "PE"), (PF, "PF"))):
            def _womma(e, nh=nh, P=P):
                last = None
                for kk in range(8):
                    last = e.matmul(P[:], v_tm[:, kk * 128:(kk + 1) * 128],
                                    wo_ar[:, kk * 1024 + nh * 512: kk * 1024 + (nh + 1) * 512],
                                    start=(kk == 0), stop=False)
                return last
            st("tensor", _womma, reads=["v_tm", "wo_ar"], writes=[pn], tag=tg + "woa%d" % nh)

    def B4p2(ti):
        tg = "B%d" % ti
        xb, xbn = (xtB, "xtB") if ti % 2 == 0 else (xtBodd, "xtBodd")
        for nh, (P, pn) in enumerate(((PE, "PE"), (PF, "PF"))):
            def _wommb(e, nh=nh, P=P):
                last = None
                for kk in range(8, 16):
                    last = e.matmul(P[:], v_tm[:, kk * 128:(kk + 1) * 128],
                                    wo_ar[:, kk * 1024 + nh * 512: kk * 1024 + (nh + 1) * 512],
                                    start=False, stop=(kk == 15))
                return last
            st("tensor", _wommb, reads=["v_tm", "wo_ar"], writes=[pn], tag=tg + "wob%d" % nh)
            st("vector", (lambda nh=nh, P=P, xb=xb: (lambda e: e.scalar_tensor_tensor(
                x2t[:, nh * 512:(nh + 1) * 512], P[:], colsB[:, 34:35],
                xb[:, nh * 512:(nh + 1) * 512], ALU.mult, ALU.add)))(),
               reads=[xbn, pn, "dqo"], writes=["x2t"], tag=tg + "x2%d" % nh)
        dma("sync", x2_dr.ap()[ti], x2t[:], reads=["x2t"], writes=["x2_dr" + str(ti)], tag=tg + "x2")

    def B5(ti):
        tg = "B%d" % ti
        def _bns2(e):
            e.bn_stats(stats[:, 0:6], x2t[:, 0:512])
            return e.bn_stats(stats[:, 6:12], x2t[:, 512:1024])
        st("vector", _bns2, reads=["x2t"], writes=["stats"])
        st("vector", lambda e: e.bn_aggr(colsA[:, 0:2], stats[:]),
           reads=["stats"], writes=["colsA"])
        st("scalar", lambda e: e.activation(colsA[:, 2:3], colsA[:, 1:2], AF.Ln,
                                            bias=epsc[:, 0:1]),
           reads=["colsA", "epsc"], writes=["colsA"])
        st("scalar", lambda e: e.activation(colsA[:, 3:4], colsA[:, 2:3], AF.Exp,
                                            scale=-0.5),
           reads=["colsA"], writes=["colsA"])
        st("vector", lambda e: e.tensor_scalar(colsA[:, 4:5], colsA[:, 0:1],
                                               colsA[:, 3:4], -1.0, ALU.mult, ALU.mult),
           reads=["colsA"], writes=["colsA"])
        st("vector", lambda e: e.tensor_scalar(og[:, 0:1024], x2t[:], colsA[:, 3:4],
                                               colsA[:, 4:5], ALU.mult, ALU.add),
           reads=["x2t", "colsA"], writes=["og"])
        def _x2T(e):
            last = None
            for c in range(8):
                dst = PC[:, c * 128:(c + 1) * 128] if c < 4 else \
                      PD[:, (c - 4) * 128:(c - 3) * 128]
                last = e.transpose(dst, og[:, c * 128:(c + 1) * 128], identt[:])
            return last
        st("tensor", _x2T, reads=["og", "identt"], writes=["PC", "PD"], tag=tg + "x2T")
        def _x2ev(e):
            e.activation(xh2fm[:, 0:512], PC[:], AF.Copy)
            return e.activation(xh2fm[:, 512:1024], PD[:], AF.Copy)
        st("scalar", _x2ev, reads=["PC", "PD"], writes=["k_fm"])
        dma("sync", x2f_dr.ap()[ti], xh2fm[:], reads=["k_fm"],
            writes=["x2f_dr" + str(ti)], tag=tg + "x2f")

    BL(0)
    BC(0)
    B1(0)
    B2(0)
    for ti in range(NT):
        B3a(ti)
        if ti + 1 < NT:
            BL(ti + 1)
            BC(ti + 1)
            B1(ti + 1)
            B2(ti + 1)
        B3bc(ti)
        B3bh(ti, 0)
        B4p1(ti)
        B3bh(ti, 1)
        B4p2(ti)
        B5(ti)

    b.alias("x2a", ["og"])
    b.alias("x2b", ["og"])
    b.alias("ya", ["ogfm"])
    b.alias("yb", ["ogfm"])
    b.alias("btot_c", ["upb_ct"])
    b.alias("b2r", ["Eq_t", "Ek_t", "EK_t", "ind4", "pmA", "pmB"])
    dma("sync", upb_ct[:], btot_dr.ap(), reads=["btot_d"], writes=["btot_c"], tag="ldbtot")
    dma("sync", b2r, b2_dr.ap(), reads=["b2_d"], writes=["b2r"], tag="ldb2")
    dma("sync", W1_ar.rearrange("p (k h) -> p k h", k=8),
        e4_out.ap()[:, 0:4096].rearrange("(k p) h -> p k h", p=128),
        reads=["e4_out_d"], writes=["W1_ar"])
    for r_ in range(8):
        dma("sync",
            W2_ar.rearrange("p (u r oo) -> p u r oo", u=32, r=8)[:, :, r_],
            e4_out.ap()[r_ * 128:(r_ + 1) * 128, 4096:8192].rearrange(
                "p (u oo) -> p u oo", u=32),
            reads=["e4_out_d"], writes=["W2_ar"], tag="unpW2_%d" % r_)

    NST = NT // 2
    b.alias("h2g0", ["wgk2_ar"])
    b.alias("h2g1", ["wgk2_ar"])
    def CL(s):
        tg = "C%d" % s
        t0, t1 = 2 * s, 2 * s + 1
        if s % 2 == 0:
            bufA, bufB, nmA, nmB = xst[:, 0:1024], xst[:, 1024:2048], "xstA", "xstB"
        else:
            bufA, bufB, nmA, nmB = xq_a, xq_b, "xq_a", "xq_b"
        for j, tj in enumerate((t0, t1)):
            dma("sync",
                bufA.rearrange("p (c t) -> p c t", c=4)[:, :, j * 128:(j + 1) * 128],
                x2f_dr.ap()[tj][:, 0:512].rearrange("p (c t) -> p c t", c=4),
                reads=["x2f_dr" + str(tj)], writes=[nmA], tag=tg + "xla%d" % j)
            dma("sync",
                bufB.rearrange("p (c t) -> p c t", c=4)[:, :, j * 128:(j + 1) * 128],
                x2f_dr.ap()[tj][:, 512:1024].rearrange("p (c t) -> p c t", c=4),
                reads=["x2f_dr" + str(tj)], writes=[nmB], tag=tg + "xlb%d" % j)

    CL(0)
    if NST > 1:
        CL(1)
    dma("sync", x2a[:], x2_dr.ap()[0], reads=["x2_dr0"], writes=["x2a"], tag="C0")
    dma("sync", x2b[:], x2_dr.ap()[1], reads=["x2_dr1"], writes=["x2b"], tag="C0")
    for s in range(NST):
        tg = "C%d" % s
        t0, t1 = 2 * s, 2 * s + 1
        if s % 2 == 0:
            bufA, bufB, nmA, nmB = xst[:, 0:1024], xst[:, 1024:2048], "xstA", "xstB"
        else:
            bufA, bufB, nmA, nmB = xq_a, xq_b, "xq_a", "xq_b"
        for hg in range(17):
            if hg < 16:
                hc0, hc1 = 2 * hg, 2 * hg + 1
                P1, p1n = (PE, "PE") if hg % 2 == 0 else (PF, "PF")
                hb, hbn = (h2g0, "h2g0") if hg % 2 == 0 else (h2g1, "h2g1")
                def _ffn1(e, hc0=hc0, hc1=hc1, P1=P1, bufA=bufA, bufB=bufB):
                    last = None
                    for hcc, hc in enumerate((hc0, hc1)):
                        for kk in range(8):
                            rhs = (bufA[:, kk * 256:(kk + 1) * 256] if kk < 4
                                   else bufB[:, (kk - 4) * 256:(kk - 3) * 256])
                            last = e.matmul(
                                P1[:, hcc * 256:(hcc + 1) * 256],
                                W1_ar[:, kk * 4096 + hc * 128: kk * 4096 + (hc + 1) * 128],
                                rhs, start=(kk == 0), stop=(kk == 7))
                    return last
                st("tensor", _ffn1, reads=["W1_ar", nmA, nmB],
                   writes=[p1n], tag=tg + "f1_%d" % hg)
                def _gel(e, hc0=hc0, hc1=hc1, P1=P1, hb=hb):
                    e.activation(hb[:, 0:256], P1[:, 0:256], AF.Gelu,
                                 bias=upb_ct[:, hc0:hc0 + 1])
                    return e.activation(hb[:, 256:512], P1[:, 256:512], AF.Gelu,
                                        bias=upb_ct[:, hc1:hc1 + 1])
                st("scalar", _gel, reads=[p1n, "btot_c"], writes=[hbn],
                   tag=tg + "gel%d" % hg)
            if hg >= 1:
                pg = hg - 1
                pc0, pc1 = 2 * pg, 2 * pg + 1
                phb, phbn = (h2g0, "h2g0") if pg % 2 == 0 else (h2g1, "h2g1")
                def _ffn2(e, pg=pg, pc0=pc0, pc1=pc1, phb=phb):
                    last = None
                    for hcc, hc in enumerate((pc0, pc1)):
                        for tj, P in ((0, PA), (1, PB)):
                            for n in range(2):
                                last = e.matmul(
                                    P[:, n * 512:(n + 1) * 512],
                                    phb[:, hcc * 256 + tj * 128: hcc * 256 + (tj + 1) * 128],
                                    W2_ar[:, hc * 1024 + n * 512: hc * 1024 + (n + 1) * 512],
                                    start=(pg == 0 and hcc == 0), stop=False)
                    return last
                st("tensor", _ffn2, reads=[phbn, "W2_ar"], writes=["PA", "PB"],
                   tag=tg + "f2_%d" % pg)
        if s + 2 < NST:
            CL(s + 2)
        def _bias2(e):
            last = None
            for P in (PA, PB):
                for n in range(2):
                    last = e.matmul(P[:, n * 512:(n + 1) * 512], onesr[0:1, 0:128],
                                    b2r[0:1, n * 512:(n + 1) * 512],
                                    start=False, stop=True)
            return last
        st("tensor", _bias2, reads=["b2r", "onesr"], writes=["PA", "PB"], tag=tg + "b2")
        def _yout(e):
            e.tensor_add(ya[:], x2a[:], PA[:])
            return e.tensor_add(yb[:], x2b[:], PB[:])
        st("vector", _yout, reads=["x2a", "x2b", "PA", "PB"], writes=["ya", "yb"])
        dma("sync", ys.ap()[t0 * 128:(t0 + 1) * 128, :], ya[:], reads=["ya"],
            writes=["ys" + str(t0)], tag=tg + "st0")
        dma("sync", ys.ap()[t1 * 128:(t1 + 1) * 128, :], yb[:], reads=["yb"],
            writes=["ys" + str(t1)], tag=tg + "st1")
        if s + 1 < NST:
            dma("sync", x2a[:], x2_dr.ap()[t0 + 2], reads=["x2_dr" + str(t0 + 2)],
                writes=["x2a"], tag="C%d" % (s + 1))
            dma("sync", x2b[:], x2_dr.ap()[t1 + 2], reads=["x2_dr" + str(t1 + 2)],
                writes=["x2b"], tag="C%d" % (s + 1))

    b.emit()
    return nc



_BF = None

_IND4 = np.zeros((4, 512), np.float32)
for _j in range(4):
    _IND4[_j, _j * 128:(_j + 1) * 128] = 1.0

def _host_inputs(inputs, T):
    f32 = np.float32
    x = np.asarray(inputs["L0_attn_hidden_states"], dtype=f32)
    B, L, _ = x.shape
    xflat = np.ascontiguousarray(x.reshape(B * L, D))
    wq = np.asarray(inputs["L0_attn_wq"], f32)
    wk = np.asarray(inputs["L0_attn_wk"], f32)
    wv = np.asarray(inputs["L0_attn_wv"], f32)
    wg = np.asarray(inputs["L0_attn_wg"], f32)
    wo = np.asarray(inputs["L0_attn_wo"], f32)
    wgk1 = np.asarray(inputs["L0_attn_wgk1"], f32)
    wgk2 = np.asarray(inputs["L0_attn_wgk2"], f32)
    wqT = np.ascontiguousarray(wq.T)
    wkT = np.ascontiguousarray(wk.T)
    wvT = np.ascontiguousarray(wv.T)
    wgT = np.ascontiguousarray(wg.T)
    woT = np.ascontiguousarray(wo.T)
    wgk1T = np.ascontiguousarray(wgk1.T)
    wgk2T = np.ascontiguousarray(wgk2.T)
    wgk1T_f = np.ascontiguousarray(
        wgk1T.reshape(8, 128, R).transpose(1, 0, 2).reshape(128, 8 * R))
    def fmcols(v, nch):
        return np.ascontiguousarray(np.asarray(v, f32).reshape(nch, 128).T)
    nq_c = fmcols(inputs["L0_attn_nq"], 8)
    nk_c = fmcols(inputs["L0_attn_nk"], 8)
    nv_c = fmcols(inputs["L0_attn_nv"], 8)
    ng_c = fmcols(inputs["L0_attn_ng"], 8)
    ngk1_c = fmcols(inputs["L0_attn_ngk1"], 8)
    no_c = fmcols(inputs["L0_attn_no"], 16)
    ngk2_r = np.ascontiguousarray(
        np.broadcast_to(np.asarray(inputs["L0_attn_ngk2"], f32), (128, 16)))
    bgk2_row = np.ascontiguousarray(
        np.asarray(inputs["L0_attn_bgk2"], f32).reshape(8, 128).T)
    W_exp = np.asarray(inputs["L0_ffn_W_exp"], f32)
    W_con = np.asarray(inputs["L0_ffn_W_con"], f32)
    WexpT = np.ascontiguousarray(W_exp.T)
    upw = np.asarray(inputs["L0_ffn_up_weights"], f32)
    dnw = np.asarray(inputs["L0_ffn_down_weights"], f32)
    b_exp = np.asarray(inputs["L0_ffn_b_exp"], f32)
    up_bias = np.asarray(inputs["L0_ffn_up_bias"], f32)
    b_con = np.asarray(inputs["L0_ffn_b_con"], f32)
    dn_b = np.asarray(inputs["L0_ffn_down_bias"], f32)

    ident = np.eye(128, dtype=f32)
    jj, ii = np.meshgrid(np.arange(128), np.arange(128), indexing="ij")
    trimask = (jj <= ii).astype(f32)
    nelem = np.array([wq.size, wk.size, wv.size, wg.size, wo.size,
                      wgk1.size * NCORES, wgk2.size * NCORES, 1.0], f32)
    ninv = np.ascontiguousarray(np.broadcast_to(1.0 / nelem, (128, 8)))
    bexp_c = np.ascontiguousarray(b_exp.reshape(32, 128).T)

    def low_pack(W):
        out = np.zeros((NCORES, 7 * 4 * 4 * 64), f32)
        for st_ in range(7):
            sig = 2 ** st_
            Wv = W[st_].reshape(-1, sig, 2, 2)
            for c in range(NCORES):
                for j in range(4):
                    u = 4 * c + j
                    for w in range(4):
                        r_, c_ = divmod(w, 2)
                        vec = np.empty(64, f32)
                        for k in range(64):
                            vec[k] = Wv[u * (64 // sig) + k // sig, k % sig, r_, c_]
                        out[c, ((st_ * 4 + j) * 4 + w) * 64:((st_ * 4 + j) * 4 + w + 1) * 64] = vec
        return out

    def high_pack(W):
        out = np.zeros((128, 5 * 16 * 4), f32)
        for sidx in range(5):
            s_ = sidx + 7
            sig = 2 ** s_
            pairs = [u for u in range(32) if (u >> sidx) & 1 == 0]
            for pi, ua in enumerate(pairs):
                for t in range(128):
                    i = ua * 128 + t
                    k = (i >> (s_ + 1)) * sig + (i & (sig - 1))
                    for w in range(4):
                        r_, c_ = divmod(w, 2)
                        out[t, (sidx * 16 + pi) * 4 + w] = 0
                        out[t, (sidx * 16 + pi) * 4 + w] = W[s_][k, r_, c_]
        return out

    lowU = low_pack(upw)
    lowD = low_pack(dnw)
    uph = high_pack(upw)
    dnh = high_pack(dnw)

    in_maps = []
    for c in range(NCORES):
        base = 4 * (c // 4)
        am = np.zeros(8, f32)
        am[base:c] = 1.0
        woT_sl = woT[c * 256:(c + 1) * 256].reshape(2, 128, D).transpose(1, 0, 2).reshape(128, 2 * D)
        im = {
            "xs": np.ascontiguousarray(xflat[c * T:(c + 1) * T]),
            "ident": ident, "trimask": trimask,
            "wqT_sl": np.ascontiguousarray(wqT[c * 128:(c + 1) * 128]),
            "wkT_sl": np.ascontiguousarray(wkT[c * 128:(c + 1) * 128]),
            "wvT_sl": np.ascontiguousarray(wvT[c * 128:(c + 1) * 128]),
            "wgT_sl": np.ascontiguousarray(wgT[c * 128:(c + 1) * 128]),
            "woT_sl": np.ascontiguousarray(woT_sl),
            "wgk1T_f": wgk1T_f, "wgk2T_f": wgk2T,
            "nq_c": nq_c, "nk_c": nk_c, "nv_c": nv_c, "ng_c": ng_c,
            "ngk1_c": ngk1_c, "no_c": no_c, "ngk2_r": ngk2_r,
            "bgk2_row": bgk2_row, "ninv_row": ninv,
            "lowU_sl": lowU[c][None, :], "lowD_sl": lowD[c][None, :],
            "uph_w": uph, "dnh_w": dnh,
            "wexpT_sl": np.ascontiguousarray(WexpT[:, c * 128:(c + 1) * 128]),
            "wcon_sl": np.ascontiguousarray(W_con[:, c * 128:(c + 1) * 128]),
            "bexp_c": bexp_c,
            "upb_c": np.ascontiguousarray(up_bias.reshape(32, 128).T),
            "bcon_c": np.ascontiguousarray(b_con.reshape(8, 128).T),
            "dnb_c": np.ascontiguousarray(dn_b.reshape(8, 128).T),
            "ind4_in": _IND4,
            "amask": np.ascontiguousarray(np.broadcast_to(am, (128, 8))),
            "am1": np.ascontiguousarray(np.broadcast_to(1.0 - am, (128, 8))),
            "bmask": np.ascontiguousarray(np.broadcast_to(am, (128, 8))),
        }
        in_maps.append(im)
    return in_maps


def kernel(**inputs):
    x = np.asarray(inputs["L0_attn_hidden_states"])
    B, L, _ = x.shape
    T = B * L // NCORES
    in_maps = _host_inputs(inputs, T)
    nc = build_program(T)
    res = run_bass_kernel_spmd(nc, in_maps, core_ids=list(range(NCORES)))
    out = np.concatenate([res.results[c]["ys"] for c in range(NCORES)], axis=0)
    return out.reshape(B, L, D).astype(np.float32)


# version 39
# speedup vs baseline: 1.0514x; 1.0195x over previous
import math
import re
import contextlib
import numpy as np

import concourse.bass as bass
import concourse.mybir as mybir
from concourse.bass_utils import run_bass_kernel_spmd

F32 = mybir.dt.float32
BF16 = mybir.dt.bfloat16
I32 = mybir.dt.int32
AF = mybir.ActivationFunctionType
ALU = mybir.AluOpType
AX = mybir.AxisListType

NCORES = 8
D = 1024
DK = 1024
DV = 2048
H = 4
DKH = 256
DVH = 512
R = 16
DH = 4096
C = 128
DEBUG_DUMPS = False



class Step:
    def __init__(self, eng, fn, waits, incs, tag):
        self.eng, self.fn, self.waits, self.incs, self.tag = eng, fn, waits, incs, tag


class Builder:

    ENG_SEM = {"tensor": "te", "vector": "ve", "scalar": "se",
               "gpsimd": "gc", "sync": "sy"}

    CONST_BUFS = {"identt", "maskt", "ncols", "no_ct", "ngk2_t", "ninv_t",
                  "amask_t", "am1_t", "bmask_t", "uph_t", "dnh_t", "bex_t",
                  "upb_ct", "bcon_ct", "dnb_ct", "onesr"}

    def __init__(self, nc, T):
        self.nc = nc
        self.T = T
        self.NT = T // C
        self.steps = []
        self.sems = {}
        self._es = contextlib.ExitStack()
        for name in ("te", "ve", "se", "gc", "sy", "gd", "cc", "sd"):
            self.sems[name] = self._es.enter_context(nc.semaphore("sem_" + name))
        self.counts = {k: 0 for k in self.sems}
        self.last_write = {}
        self.readers = {}
        self.tensors = {}
        self.alias_debt = {}

    DMA_PREFIX = {"sync": "d_", "gpsimd": "dg_", "scalar": "ds_"}

    def _dma_sem(self, buf, eng="sync"):
        key = re.sub(r"[0-9_]+$", "", buf)
        if key in self.CONST_BUFS or buf in self.CONST_BUFS:
            key = "const"
        key = self.DMA_PREFIX[eng] + key
        if key not in self.sems:
            self.sems[key] = self._es.enter_context(self.nc.semaphore("sem_" + key))
            self.counts[key] = 0
        return key

    def sb(self, name, shape, dtype):
        t = self._es.enter_context(self.nc.sbuf_tensor(name, shape, dtype))
        self.tensors[name] = t
        return t

    def ps(self, name, shape):
        t = self._es.enter_context(self.nc.psum_tensor(name, shape, F32))
        self.tensors[name] = t
        return t

    def alias(self, name, oldnames):
        self.alias_debt.setdefault(name, []).extend(oldnames)

    def step(self, eng, fn, reads=(), writes=(), tag="", sem=None, inc=1):
        sem = sem or self.ENG_SEM[eng]
        writes = list(writes)
        for b_ in list(writes):
            if b_ in self.alias_debt:
                writes.extend(self.alias_debt.pop(b_))
        waits = {}
        def add_wait(ev):
            if ev is None:
                return
            s, v = ev
            if waits.get(s, 0) < v:
                waits[s] = v
        for b in reads:
            add_wait(self.last_write.get(b))
        for b in writes:
            add_wait(self.last_write.get(b))
            for ev in self.readers.get(b, ()):
                add_wait(ev)
        for s_ in list(waits):
            if s_.startswith(("d_", "dg_", "ds_")):
                waits[s_] = self.counts[s_]
        self.counts[sem] += inc
        ev = (sem, self.counts[sem])
        for b in reads:
            self.readers.setdefault(b, []).append(ev)
        for b in writes:
            self.last_write[b] = ev
            self.readers[b] = []
        self.steps.append(Step(eng, fn, sorted(waits.items()), [(sem, inc)], tag))
        return ev

    def dma(self, eng, out, in_, reads=(), writes=(), tag=""):
        sem = self._dma_sem(writes[0] if writes else "misc", eng)
        return self.step(eng, lambda e: e.dma_start(out, in_),
                         reads=reads, writes=writes, tag=tag, sem=sem, inc=16)

    def barrier_event(self):
        return dict(self.counts)

    def emit(self):
        nc = self.nc
        per_eng = {e: [] for e in self.ENG_SEM}
        for st in self.steps:
            per_eng[st.eng].append(st)
        marks = {e: {} for e in self.ENG_SEM}

        with nc.Block() as block:
            def make_body(eng_name):
                def run(eng):
                    wm = marks[eng_name]
                    for st in per_eng[eng_name]:
                        for s, v in st.waits:
                            if wm.get(s, 0) >= v:
                                continue
                            eng.wait_ge(self.sems[s], v)
                            wm[s] = v
                        inst = st.fn(eng)
                        (s, k), = st.incs
                        if inst is None:
                            eng.sem_inc(self.sems[s], k)
                        else:
                            inst.then_inc(self.sems[s], k)
                return run
            block.tensor(make_body("tensor"))
            block.vector(make_body("vector"))
            block.scalar(make_body("scalar"))
            block.gpsimd(make_body("gpsimd"))
            block.sync(make_body("sync"))
        self._es.close()


def build_program(T):
    NT = T // C
    nc = bass.Bass("TRN2", target_bir_lowering=False)
    dt = nc.dram_tensor

    xs = dt("xs", [T, D], F32, kind="ExternalInput")
    ident = dt("ident", [128, 128], F32, kind="ExternalInput")
    trimask = dt("trimask", [128, 128], F32, kind="ExternalInput")
    wqT_sl = dt("wqT_sl", [128, DK], F32, kind="ExternalInput")
    wkT_sl = dt("wkT_sl", [128, DK], F32, kind="ExternalInput")
    wvT_sl = dt("wvT_sl", [128, DV], F32, kind="ExternalInput")
    wgT_sl = dt("wgT_sl", [128, DV], F32, kind="ExternalInput")
    woT_sl = dt("woT_sl", [128, 2 * D], F32, kind="ExternalInput")
    wgk1T_f = dt("wgk1T_f", [128, 8 * R], F32, kind="ExternalInput")
    wgk2T_f = dt("wgk2T_f", [16, DK], F32, kind="ExternalInput")
    nq_c = dt("nq_c", [128, 8], F32, kind="ExternalInput")
    nk_c = dt("nk_c", [128, 8], F32, kind="ExternalInput")
    nv_c = dt("nv_c", [128, 8], F32, kind="ExternalInput")
    ng_c = dt("ng_c", [128, 8], F32, kind="ExternalInput")
    ngk1_c = dt("ngk1_c", [128, 8], F32, kind="ExternalInput")
    no_c = dt("no_c", [128, 16], F32, kind="ExternalInput")
    ngk2_r = dt("ngk2_r", [128, 16], F32, kind="ExternalInput")
    bgk2_row = dt("bgk2_row", [128, 8], F32, kind="ExternalInput")
    ninv_row = dt("ninv_row", [128, 8], F32, kind="ExternalInput")
    lowU_sl = dt("lowU_sl", [1, 7 * 4 * 256], F32, kind="ExternalInput")
    lowD_sl = dt("lowD_sl", [1, 7 * 4 * 256], F32, kind="ExternalInput")
    uph_w = dt("uph_w", [128, 5 * 16 * 4], F32, kind="ExternalInput")
    dnh_w = dt("dnh_w", [128, 5 * 16 * 4], F32, kind="ExternalInput")
    wexpT_sl = dt("wexpT_sl", [4096, 128], F32, kind="ExternalInput")
    wcon_sl = dt("wcon_sl", [4096, 128], F32, kind="ExternalInput")
    bexp_c = dt("bexp_c", [128, 32], F32, kind="ExternalInput")
    upb_c = dt("upb_c", [128, 32], F32, kind="ExternalInput")
    bcon_c = dt("bcon_c", [128, 8], F32, kind="ExternalInput")
    dnb_c = dt("dnb_c", [128, 8], F32, kind="ExternalInput")
    ind4_in = dt("ind4_in", [4, 512], F32, kind="ExternalInput")
    amask = dt("amask", [128, 8], F32, kind="ExternalInput")
    am1 = dt("am1", [128, 8], F32, kind="ExternalInput")
    bmask = dt("bmask", [128, 8], F32, kind="ExternalInput")
    ys = dt("ys", [T, D], F32, kind="ExternalOutput")

    _K = "ExternalOutput" if DEBUG_DUMPS else "Internal"
    op_dr = dt("op_dr", [NT, 128, DV], BF16, kind=_K)
    sg_dr = dt("sg_dr", [NT, 128, DV], BF16, kind=_K)
    qb_dr = dt("qb_dr", [NT, 128, DK], BF16, kind=_K)
    x2_dr = dt("x2_dr", [NT, 128, D], F32, kind=_K)
    x2f_dr = dt("x2f_dr", [NT, 128, D], BF16, kind="Internal")
    e1_in = dt("e1_in", [1, 8], F32, kind="Internal")
    e1_out = dt("e1_out", [8, 8], F32, kind="Internal")
    e2_in = dt("e2_in", [128, 8192], mybir.dt.int8, kind="Internal")
    e2_out = dt("e2_out", [1024, 8192], mybir.dt.int8, kind="Internal")
    e3_in = dt("e3_in", [128, 1024], F32, kind="Internal")
    e3_out = dt("e3_out", [1024, 1024], F32, kind="Internal")
    e4_in = dt("e4_in", [128, 8192], BF16, kind="Internal")
    e4_out = dt("e4_out", [1024, 8192], BF16, kind="Internal")
    e5_in = dt("e5_in", [128, 4104], BF16, kind="Internal")
    e5_out = dt("e5_out", [1024, 4104], BF16, kind="Internal")
    btot_dr = dt("btot_dr", [128, 32], F32, kind="Internal")
    dbg_dr = dt("dbg_dr", [8, 128, DV], F32, kind=_K)
    b2_dr = dt("b2_dr", [1, D], F32, kind="Internal")

    DBG = DEBUG_DUMPS
    b = Builder(nc, T)
    sb = b.sb
    st = b.step
    dma = b.dma

    identt = sb("identt", [128, 128], F32)
    maskt = sb("maskt", [128, 128], BF16)
    onesr = sb("onesr", [1, 256], F32)
    epsc = sb("epsc", [128, 2], F32)
    ncols = sb("ncols", [128, 72], F32)
    no_ct = sb("no_ct", [128, 16], F32)
    ngk2_t = sb("ngk2_t", [128, 16], F32)
    ninv_t = sb("ninv_t", [128, 8], F32)
    mx_cols = sb("mx_cols", [128, 8], F32)
    th_cols = sb("th_cols", [128, 8], F32)
    nth_cols = sb("nth_cols", [128, 8], F32)
    m127_cols = sb("m127_cols", [128, 8], F32)
    ccol = sb("ccol", [128, 1], F32)
    c1col = sb("c1col", [128, 1], F32)
    colsA = sb("colsA", [128, 32], F32)
    colsB = sb("colsB", [128, 48], F32)
    colsQ = sb("colsQ", [128, NT], F32)
    eoff_t = sb("eoff_t", [128, 8], F32)
    offb_t = sb("offb_t", [128, 8], F32)
    dprod = sb("dprod", [128, 8], F32)
    gcur = sb("gcur", [128, 8], F32)
    wrk8 = sb("wrk8", [128, 8], F32)
    amask_t = sb("amask_t", [128, 8], F32)
    am1_t = sb("am1_t", [128, 8], F32)
    bmask_t = sb("bmask_t", [128, 8], F32)
    stats = sb("stats", [128, 12], F32)
    btsc = sb("btsc", [128, 2], F32)
    msum = sb("msum", [128, 8], F32)
    y1t = sb("y1t", [128, 16], F32)
    sq16 = sb("sq16", [128, 16], F32)
    t2g = sb("t2g", [128, 16], F32)
    rq2 = sb("rq2", [128, 16], I32)
    u4 = sb("u4", [128, 4 * 64], F32)
    tsc = u4
    pmx = u4[:, 0:128]
    y1sb = u4[0:16, 0:128]
    srow = u4[0:1, 0:256]
    e1sb = u4[0:8, 192:200]
    prow = u4[0:1, 200:216]
    dmod = wrk8
    S_f = sb("S_f", [128, 4096], F32)
    rint = sb("rint", [128, 1024], mybir.dt.int16)
    xq_a = sb("xq_a", [128, D], BF16)
    xq_b = sb("xq_b", [128, D], BF16)
    q_fm = sb("q_fm", [128, DK], BF16)
    k_fm = sb("k_fm", [128, DK], BF16)
    qt_h = sb("qt_h", [128, 256], BF16)
    kt_h = sb("kt_h", [128, 256], BF16)
    qb_h = sb("qb_h", [128, 256], BF16)
    wgk1_ar = sb("wgk1_ar", [128, 8 * R], BF16)
    xq2_tm = sb("xq2_tm", [128, 16], BF16)
    xq2_fm = sb("xq2_fm", [16, 128], BF16)

    upb_ct = sb("upb_ct", [128, 32], F32)
    bcon_ct = sb("bcon_ct", [128, 8], F32)
    dnb_ct = sb("dnb_ct", [128, 8], F32)

    SLAB1N = 68608
    slab1 = sb("slab1_mem", [128, SLAB1N], BF16)
    def s1(off, n):
        return slab1[:, off:off + n]
    wq_ar = s1(0, 8192)
    wk_ar = s1(8192, 8192)
    wv_ar = s1(16384, 16384)
    wg_ar = s1(32768, 16384)
    S_bf = s1(49152, 4096)
    p1b = s1(53248, 2048)
    p2b = s1(55296, 2048)
    op_t = s1(53248, 2048)
    sg_tm = s1(55296, 2048)
    h2g0 = s1(67584, 512)
    h2g1 = s1(68096, 512)
    xq_c = s1(59392, 1024)
    xq_d = s1(60416, 1024)
    EsA = s1(61440, 768)
    EsB = s1(62208, 768)
    khtm4 = s1(62976, 1024)
    am4 = s1(64000, 512)
    qh4 = s1(64512, 1024)
    v_tm = s1(57344, 2048)
    qstg = s1(59392, 2048)
    wo_ar = s1(0, 16384)
    Sin_bf = s1(16384, 4096)
    W1_ar = s1(0, 32768)
    W2_ar = s1(32768, 32768)
    xst = s1(65536, 2048)
    wgk2_ar = slab1[0:16, 67584:68608]
    xh2 = q_fm
    xh2fm = k_fm
    b.alias("op_t", ["p1b"])
    b.alias("sg_tm", ["p2b"])
    b.alias("wo_ar", ["wq_ar", "wk_ar"])
    b.alias("Sin_bf", ["wv_ar"])
    b.alias("W1_ar", ["wq_ar", "wk_ar", "wv_ar", "wo_ar", "Sin_bf"])
    b.alias("W2_ar", ["wg_ar", "S_bf0", "S_bf1", "S_bf2", "S_bf3", "op_t", "sg_tm",
                      "v_tm", "qstg", "p1b", "p2b", "xq_c", "xq_d",
                      "EsA", "EsB", "khtm4", "am4", "qh4"])
    b.alias("xq_c", ["qstg"])
    b.alias("xq_d", ["qstg"])

    SLAB2N = 10240
    slab2 = sb("slab2_mem", [128, SLAB2N], F32)
    def s2(off, n):
        return slab2[:, off:off + n]
    wsl_f = s2(0, 2048)
    gk1_f = slab2[:, 2048:2048 + 128]
    bex_t = slab2[:, 640:672]
    bfold = slab2[:, 672:704]
    pfold = s2(2048, 4096)
    bltU = s2(6144, 4096)
    lowr = slab2[0:1, 1024:2048]
    b.alias("lowr", ["wsl_f"])
    lowr2 = slab2[0:1, 0:1024]
    b.alias("lowr2", ["wsl_f"])
    b.alias("fbU0", ["gk1_f"])
    b.alias("uph_t", ["wsl_f"])
    b.alias("dnh_t", ["wsl_f"])
    b.alias("bex_t", ["wsl_f"])
    b.alias("bfold", ["wsl_f"])
    xt = s2(0, 1024)
    xh_tm = s2(1024, 1024)
    xh_fm = s2(2048, 1024)
    beta = s2(3072, 1024)
    sgm = s2(4096, 1024)
    lsg = sgm
    El_t = s2(5120, 1024)
    tp = s2(6144, 1024)
    usc = tp
    Eq_t = s2(7168, 256)
    Ek_t = s2(7424, 256)
    EK_t = s2(7680, 256)
    KHp = slab2[:, 7168:7424]
    ind4 = slab2[0:4, 7936:8448]
    b.alias("ind4", ["bltU"])
    pmA = slab2[:, 8448:8576]
    pmB = slab2[:, 8576:8704]
    b.alias("pmA", ["bltU"])
    b.alias("pmB", ["bltU"])
    uph_t = slab2[:, 0:320]
    dnh_t = slab2[:, 320:640]
    bgk2_t = slab2[0:1, 9216:9216 + 1024]
    b.alias("bgk2_c", ["msum"])
    b.alias("xt", ["wsl_f", "lowr", "lowr2", "uph_t", "dnh_t", "bex_t", "bfold"])
    b.alias("xh_tm", ["wsl_f", "lowr"])
    b.alias("xh_fm", ["pf0"])
    b.alias("pf0", ["fbU0", "fbU1", "fbU2", "fbU3",
                    "fbD0", "fbD1", "fbD2", "fbD3"])
    b.alias("beta", ["pf1"])
    b.alias("sgm", ["pf2"])
    b.alias("El_t", ["pf3"])
    b.alias("tp", ["bltU"])
    b.alias("Eq_t", ["bltU"])
    b.alias("Ek_t", ["bltU"])
    b.alias("EK_t", ["bltU"])
    b.alias("bgk2_t", ["bltU"])
    Sin_f = s2(0, 4096)
    Ur_f = s2(4096, 4104)
    og = s2(0, 2048)
    ogfm = s2(2048, 2048)
    x2t = s2(4096, 1024)
    xtB = s2(5120, 1024)
    xtBodd = s2(7424, 1024)
    b.alias("xtBodd", ["ind4"])
    x2a = s2(0, 1024)
    x2b = s2(1024, 1024)
    ya = s2(2048, 1024)
    yb = s2(3072, 1024)
    btot = slab2[0:1, 4096:4096 + 4096]
    b2r = slab2[0:1, 8192:8192 + 1024]

    PA = b.ps("PA", [128, 1024])
    PB = b.ps("PB", [128, 1024])
    PC = b.ps("PC", [128, 512])
    PD = b.ps("PD", [128, 512])
    PE = b.ps("PE", [128, 512])
    PF = b.ps("PF", [128, 512])

    dma("sync", identt[:], ident.ap(), writes=["identt"])
    dma("gpsimd", maskt[:], trimask.ap(), writes=["maskt"])
    for i_, src in enumerate([nq_c, nk_c, nv_c, ng_c, ngk1_c]):
        dma("sync", ncols[:, i_ * 8:(i_ + 1) * 8], src.ap(), writes=["ncols"])
    dma("sync", no_ct[:], no_c.ap(), writes=["no_ct"])
    dma("sync", ngk2_t[:], ngk2_r.ap(), writes=["ngk2_t"])
    dma("sync", ninv_t[:], ninv_row.ap(), writes=["ninv_t"])
    dma("sync", amask_t[:], amask.ap(), writes=["amask_t"])
    dma("sync", am1_t[:], am1.ap(), writes=["am1_t"])
    dma("sync", bmask_t[:], bmask.ap(), writes=["bmask_t"])
    dma("sync", upb_ct[:], upb_c.ap(), writes=["upb_ct"])
    dma("sync", bcon_ct[:], bcon_c.ap(), writes=["bcon_ct"])
    dma("sync", dnb_ct[:], dnb_c.ap(), writes=["dnb_ct"])

    st("vector", lambda e: e.scalar_tensor_tensor(
        ncols[:, 40:72], ncols[:, 0:32], -1.0, ncols[:, 0:32], ALU.mult, ALU.max),
       reads=["ncols"], writes=["ncols"])
    st("vector", lambda e: e.memset(onesr[:], 1.0), writes=["onesr"])
    def _eps(e):
        e.memset(epsc[:, 0:1], 1e-5)
        return e.memset(epsc[:, 1:2], 1e-8)
    st("vector", _eps, writes=["epsc"])
    st("vector", lambda e: e.memset(colsB[:, 39:40], 1.0), writes=["colsB"])
    st("vector", lambda e: e.memset(S_f[:], 0.0),
       writes=["S_f0", "S_f1", "S_f2", "S_f3"])
    st("vector", lambda e: e.memset(dprod[:], 1.0), writes=["dprod"])
    st("vector", lambda e: e.memset(gcur[:], 1.0), writes=["gcur"])
    st("vector", lambda e: e.memset(offb_t[:], 0.0), writes=["offb_t"])

    slmap = [(wqT_sl, DK, 0), (wkT_sl, DK, 1), (wvT_sl, DV, 2),
             (wgT_sl, DV, 3), (woT_sl, 2 * D, 4)]
    for src, w, slot in slmap:
        if w <= 2048:
            dma("sync", wsl_f[:, 0:w], src.ap(), writes=["wsl_f"], tag="p1ld%d" % slot)
            st("vector", (lambda w=w, slot=slot: (lambda e: e.reduce_sum(
                colsA[:, slot:slot + 1], wsl_f[:, 0:w], axis=AX.X,
                apply_absolute_value=True)))(),
               reads=["wsl_f"], writes=["colsA"])
    dma("sync", gk1_f[:], wgk1T_f.ap(), writes=["gk1_f"])
    st("vector", lambda e: e.reduce_sum(colsA[:, 5:6], gk1_f[:], axis=AX.X,
                                        apply_absolute_value=True),
       reads=["gk1_f"], writes=["colsA"])
    dma("sync", wsl_f[0:16, 0:DK], wgk2T_f.ap(), writes=["wsl_f"], tag="p1ldgk2")
    st("vector", lambda e: e.reduce_sum(colsA[0:16, 6:7], wsl_f[0:16, 0:DK], axis=AX.X,
                                        apply_absolute_value=True),
       reads=["wsl_f"], writes=["colsA"])

    def _summms(e):
        for slot in range(6):
            e.matmul(PC[0:1, slot:slot + 1], colsB[:, 39:40],
                     colsA[:, slot:slot + 1], start=True, stop=True)
        e.matmul(PC[0:1, 6:7], colsB[0:16, 39:40], colsA[0:16, 6:7],
                 start=True, stop=True)
        return e.matmul(PC[0:1, 7:8], colsB[0:16, 39:40], colsA[0:16, 6:7],
                        start=True, stop=True)
    st("tensor", _summms, reads=["colsA", "colsB"], writes=["PC"])
    st("vector", lambda e: e.tensor_copy(prow[:, 0:8], PC[0:1, 0:8]),
       reads=["PC"], writes=["u4"])
    dma("sync", e1_in.ap(), prow[:, 0:8], reads=["u4"], writes=["e1_in_d"])
    st("gpsimd", lambda e: e.collective_compute(
        "AllGather", ALU.bypass, ins=[e1_in.ap()], outs=[e1_out.ap()],
        replica_groups=[list(range(NCORES))]),
        reads=["e1_in_d"], writes=["e1_out_d"], sem="cc")
    dma("sync", e1sb[:], e1_out.ap(), reads=["e1_out_d"], writes=["u4"])
    st("tensor", lambda e: e.matmul(PC[0:1, 8:16], colsB[0:8, 39:40], e1sb[:],
                                    start=True, stop=True),
       reads=["u4", "colsB"], writes=["PC"])
    st("vector", lambda e: e.tensor_copy(prow[:, 8:16], PC[0:1, 8:16]),
       reads=["PC"], writes=["u4"])
    st("tensor", lambda e: e.matmul(PD[:, 0:8], onesr[0:1, 0:128], prow[:, 8:16],
                                    start=True, stop=True),
       reads=["u4", "onesr"], writes=["PD"])
    st("vector", lambda e: e.tensor_copy(msum[:], PD[:, 0:8]),
       reads=["PD"], writes=["msum"])
    st("vector", lambda e: e.tensor_mul(mx_cols[:], msum[:], ninv_t[:]),
       reads=["msum", "ninv_t"], writes=["mx_cols"])
    st("vector", lambda e: e.tensor_scalar_max(mx_cols[:], mx_cols[:], 1e-5),
       reads=["mx_cols"], writes=["mx_cols"])
    st("vector", lambda e: e.tensor_scalar_mul(th_cols[:], mx_cols[:], 0.5),
       reads=["mx_cols"], writes=["th_cols"])
    st("vector", lambda e: e.tensor_scalar_mul(nth_cols[:], th_cols[:], -1.0),
       reads=["th_cols"], writes=["nth_cols"])
    st("vector", lambda e: e.tensor_scalar_mul(m127_cols[:], mx_cols[:], 1.0 / 127.0),
       reads=["mx_cols"], writes=["m127_cols"])
    st("vector", lambda e: e.memset(ccol[:], 1.0 / 16.0), writes=["ccol"])

    e2off = {0: 0, 1: 1024, 2: 2048, 3: 4096, 4: 6144}
    for src, w, slot in slmap:
        dma("sync", wsl_f[:, 0:w], src.ap(), writes=["wsl_f"], tag="p2ld%d" % slot)
        st("vector", (lambda w=w, slot=slot: (lambda e: e.tensor_single_scalar(
            p1b[:, 0:w], wsl_f[:, 0:w], th_cols[:, slot:slot + 1], ALU.is_ge)))(),
           reads=["wsl_f", "th_cols"], writes=["p1b"])
        st("vector", (lambda w=w, slot=slot: (lambda e: e.tensor_single_scalar(
            p2b[:, 0:w], wsl_f[:, 0:w], nth_cols[:, slot:slot + 1], ALU.is_le)))(),
           reads=["wsl_f", "nth_cols"], writes=["p2b"])
        st("vector", (lambda w=w: (lambda e: e.tensor_sub(
            qstg[:, 0:w], p1b[:, 0:w], p2b[:, 0:w])))(),
           reads=["p1b", "p2b"], writes=["qstg"])
        dma("gpsimd", e2_in.ap()[:, e2off[slot]:e2off[slot] + w], qstg[:, 0:w],
            reads=["qstg"], writes=["e2_in_d"], tag="p2st%d" % slot)
    st("vector", lambda e: e.tensor_single_scalar(p1b[:, 0:128], gk1_f[:],
                                                  th_cols[:, 5:6], ALU.is_ge),
       reads=["gk1_f", "th_cols"], writes=["p1b"])
    st("vector", lambda e: e.tensor_single_scalar(p2b[:, 0:128], gk1_f[:],
                                                  nth_cols[:, 5:6], ALU.is_le),
       reads=["gk1_f", "nth_cols"], writes=["p2b"])
    st("vector", lambda e: e.tensor_sub(wgk1_ar[:], p1b[:, 0:128], p2b[:, 0:128]),
       reads=["p1b", "p2b"], writes=["wgk1_ar"])
    dma("sync", wsl_f[0:16, 0:DK], wgk2T_f.ap(), writes=["wsl_f"], tag="p2ldgk2")
    st("vector", lambda e: e.tensor_single_scalar(p1b[0:16, 0:DK], wsl_f[0:16, 0:DK],
                                                  th_cols[0:16, 6:7], ALU.is_ge),
       reads=["wsl_f", "th_cols"], writes=["p1b"])
    st("vector", lambda e: e.tensor_single_scalar(p2b[0:16, 0:DK], wsl_f[0:16, 0:DK],
                                                  nth_cols[0:16, 6:7], ALU.is_le),
       reads=["wsl_f", "nth_cols"], writes=["p2b"])
    st("vector", lambda e: e.tensor_sub(wgk2_ar[:], p1b[0:16, 0:DK], p2b[0:16, 0:DK]),
       reads=["p1b", "p2b"], writes=["wgk2_ar"])

    st("gpsimd", lambda e: e.collective_compute(
        "AllGather", ALU.bypass, ins=[e2_in.ap()], outs=[e2_out.ap()],
        replica_groups=[list(range(NCORES))]),
        reads=["e2_in_d"], writes=["e2_out_d"], sem="cc")

    dma("sync", uph_t, uph_w.ap(), writes=["uph_t"], tag="lduph")
    dma("sync", dnh_t, dnh_w.ap(), writes=["dnh_t"], tag="lddnh")

    for bfi, (low_dram, stoff) in enumerate([(lowU_sl, 0), (lowD_sl, 512)]):
        fbn = lambda j, bfi=bfi: "fb%s%d" % ("UD"[bfi], j)
        def _init(e, stoff=stoff):
            last = None
            for j in range(4):
                last = e.tensor_copy(pfold[:, stoff + j * 128: stoff + (j + 1) * 128],
                                     identt[:])
            return last
        st("vector", _init, reads=["identt"], writes=[fbn(j) for j in range(4)])
        for s in range(7):
            sig = 2 ** s
            lr, lrn = (lowr, "lowr") if s % 2 == 0 else (lowr2, "lowr2")
            dma("sync", lr[:], low_dram.ap()[:, s * 1024:(s + 1) * 1024],
                writes=[lrn], tag="lowld%d_%d" % (bfi, s))
            for j in range(4):
                base = (j * 4) * 64
                Pj, Pjn = (PC, "PC") if j % 2 == 0 else (PD, "PD")
                Sj, Sjn = (PE, "PE") if j % 2 == 0 else (PF, "PF")
                def _wmm(e, base=base, lr=lr, Pj=Pj):
                    last = None
                    for w in range(4):
                        last = e.matmul(Pj[:, w * 64:(w + 1) * 64],
                                        onesr[0:1, 0:128],
                                        lr[0:1, base + w * 64: base + (w + 1) * 64],
                                        start=True, stop=True)
                    return last
                st("tensor", _wmm, reads=[lrn, "onesr"], writes=[Pjn])
                blk = pfold[:, stoff + j * 128: stoff + (j + 1) * 128]
                bv = blk.rearrange("p (u two t) -> p u two t", two=2, t=sig)
                av, bvv = bv[:, :, 0, :], bv[:, :, 1, :]
                def _umul(e, av=av, bvv=bvv, sig=sig, Pj=Pj, Sj=Sj):
                    e.tensor_mul(u4[:, 0:64].rearrange("p (u t) -> p u t", t=sig), av,
                                 Pj[:, 0:64].rearrange("p (u t) -> p u t", t=sig))
                    e.tensor_mul(Sj[:, 0:64].rearrange("p (u t) -> p u t", t=sig), bvv,
                                 Pj[:, 64:128].rearrange("p (u t) -> p u t", t=sig))
                    e.tensor_mul(u4[:, 128:192].rearrange("p (u t) -> p u t", t=sig), av,
                                 Pj[:, 128:192].rearrange("p (u t) -> p u t", t=sig))
                    return e.tensor_mul(Sj[:, 64:128].rearrange("p (u t) -> p u t", t=sig), bvv,
                                        Pj[:, 192:256].rearrange("p (u t) -> p u t", t=sig))
                st("vector", _umul, reads=[fbn(j), Pjn], writes=["u4", Sjn])
                def _upd(e, av=av, bvv=bvv, sig=sig, Sj=Sj):
                    e.tensor_add(av, u4[:, 0:64].rearrange("p (u t) -> p u t", t=sig),
                                 Sj[:, 0:64].rearrange("p (u t) -> p u t", t=sig))
                    return e.tensor_add(bvv, u4[:, 128:192].rearrange("p (u t) -> p u t", t=sig),
                                        Sj[:, 64:128].rearrange("p (u t) -> p u t", t=sig))
                st("vector", _upd, reads=["u4", Sjn], writes=[fbn(j)])
    dma("sync", e3_in.ap(), pfold[:, 0:1024],
        reads=["fbU%d" % j for j in range(4)] + ["fbD%d" % j for j in range(4)],
        writes=["e3_in_d"])
    st("gpsimd", lambda e: e.collective_compute(
        "AllGather", ALU.bypass, ins=[e3_in.ap()], outs=[e3_out.ap()],
        replica_groups=[list(range(NCORES))]),
        reads=["e3_in_d"], writes=["e3_out_d"], sem="cc")
    for r_ in range(8):
        dma("sync", bltU.rearrange("p (r j t) -> p r j t", r=8, j=4)[:, r_],
            e3_out.ap()[r_ * 128:(r_ + 1) * 128, 0:512].rearrange("p (j t) -> p j t", j=4),
            reads=["e3_out_d"], writes=["bltU", "lowr"], tag="unpBltU%d" % r_)

    def high_pairs(sidx):
        sig = 2 ** sidx
        return [(u, u + sig) for u in range(32) if (u >> sidx) & 1 == 0]

    def wcol(tbl, sidx, pi, w):
        idx = (sidx * 16 + pi) * 4 + w
        return tbl[:, idx:idx + 1]

    def fold_high(tbl, tbln, reverse, swap):
        order = range(4, -1, -1) if reverse else range(5)
        for sidx in order:
            for pi, (ua, ub) in enumerate(high_pairs(sidx)):
                A = pfold[:, ua * 128:(ua + 1) * 128]
                Bc = pfold[:, ub * 128:(ub + 1) * 128]
                w00 = wcol(tbl, sidx, pi, 0)
                w01 = wcol(tbl, sidx, pi, 2 if swap else 1)
                w10 = wcol(tbl, sidx, pi, 1 if swap else 2)
                w11 = wcol(tbl, sidx, pi, 3)
                ts_, tsn = (tsc, "u4") if pi % 2 == 0 else (PF[:, 0:256], "PF")
                pn = ["pf%d" % (ua // 8), "pf%d" % (ub // 8)]
                st("vector", (lambda A=A, Bc=Bc, w01=w01, w10=w10, ts_=ts_: (lambda e: (
                    e.tensor_single_scalar(ts_[:, 0:128], Bc, w01, ALU.mult),
                    e.tensor_single_scalar(ts_[:, 128:256], A, w10, ALU.mult))[-1]))(),
                   reads=pn + [tbln], writes=[tsn])
                st("vector", (lambda A=A, Bc=Bc, w00=w00, w11=w11, ts_=ts_: (lambda e: (
                    e.scalar_tensor_tensor(A, A, w00, ts_[:, 0:128], ALU.mult, ALU.add),
                    e.scalar_tensor_tensor(Bc, Bc, w11, ts_[:, 128:256], ALU.mult, ALU.add))[-1]))(),
                   reads=[tsn, tbln], writes=pn)

    dma("sync", pfold.rearrange("p (u d) -> p u d", u=32),
        wexpT_sl.ap().rearrange("(u p) d -> p u d", p=128),
        writes=["pf0", "pf1", "pf2", "pf3"], tag="ldwexp")
    for unp_e, (nm_, ar_, off_, w_) in zip(
            ("gpsimd", "gpsimd", "gpsimd", "gpsimd"),
            [("wq_ar", wq_ar, 0, DK), ("wk_ar", wk_ar, 1024, DK),
             ("wv_ar", wv_ar, 2048, DV), ("wg_ar", wg_ar, 4096, DV)]):
        dma(unp_e,
            ar_.rearrange("p (c w) -> p c w", c=8),
            e2_out.ap().rearrange("(c p) w -> p c w", p=128)[:, :, off_:off_ + w_],
            reads=["e2_out_d"], writes=[nm_], tag="unp" + nm_)
    for rnd in range(4):
        def _pmm(e, rnd=rnd):
            last = None
            for j in range(8):
                u = rnd * 8 + j
                last = e.matmul(PA[:, j * 128:(j + 1) * 128],
                                bltU[:, u * 128:(u + 1) * 128],
                                pfold[:, u * 128:(u + 1) * 128],
                                start=True, stop=True)
            return last
        st("tensor", _pmm, reads=["bltU", "pf%d" % rnd], writes=["PA"], tag="w1p%d" % rnd)
        st("scalar", (lambda rnd=rnd: (lambda e: e.activation(
            pfold[:, rnd * 1024:(rnd + 1) * 1024], PA[:], AF.Copy)))(),
           reads=["PA"], writes=["pf%d" % rnd], tag="w1pe%d" % rnd)
    fold_high(uph_t, "uph_t", reverse=False, swap=False)
    for rnd in range(4):
        def _w1t(e, rnd=rnd):
            last = None
            for j in range(8):
                u = rnd * 8 + j
                last = e.transpose(PA[:, j * 128:(j + 1) * 128],
                                   pfold[:, u * 128:(u + 1) * 128], identt[:])
            return last
        st("tensor", _w1t, reads=["pf%d" % rnd, "identt"], writes=["PA"], tag="w1t%d" % rnd)
        st("scalar", (lambda rnd=rnd: (lambda e: e.activation(
            qstg[:, 0:1024], PA[:], AF.Copy)))(),
           reads=["PA"], writes=["qstg"])
        dma("sync", e4_in.ap()[:, rnd * 1024:(rnd + 1) * 1024], qstg[:, 0:1024],
            reads=["qstg"], writes=["e4_in_d"], tag="w1st%d" % rnd)

    dma("sync", bex_t, bexp_c.ap(), writes=["bex_t"], tag="ldbex")
    def _bmm(e):
        last = None
        for u in range(32):
            last = e.matmul(PC[:, u:u + 1], bltU[:, u * 128:(u + 1) * 128],
                            bex_t[:, u:u + 1], start=True, stop=True)
        return last
    st("tensor", _bmm, reads=["bltU", "bex_t"], writes=["PC"])
    st("vector", lambda e: e.tensor_copy(bfold, PC[:, 0:32]),
       reads=["PC"], writes=["bfold"])
    for sidx in range(5):
        sig = 2 ** sidx
        bvw = bfold.rearrange("p (q two r) -> p q two r", two=2, r=sig)
        av, bv = bvw[:, :, 0, :], bvw[:, :, 1, :]
        wv = uph_t[:, sidx * 64:(sidx + 1) * 64].rearrange(
            "p (q r w) -> p w q r", r=sig, w=4)
        w00, w01, w10, w11 = wv[:, 0], wv[:, 1], wv[:, 2], wv[:, 3]
        u3 = lambda a, b: u4[:, a:b].rearrange("p (q r) -> p q r", r=sig)
        def _bs1(e, av=av, bv=bv, w01=w01, w10=w10, u3=u3):
            e.tensor_mul(u3(0, 16), bv, w01)
            return e.tensor_mul(u3(16, 32), av, w10)
        st("vector", _bs1, reads=["bfold", "uph_t"], writes=["u4"])
        def _bs2(e, av=av, bv=bv, w00=w00, w11=w11, u3=u3):
            e.tensor_mul(u3(32, 48), av, w00)
            return e.tensor_mul(u3(48, 64), bv, w11)
        st("vector", _bs2, reads=["bfold", "uph_t"], writes=["u4"])
        def _bs3(e, av=av, bv=bv, u3=u3):
            e.tensor_add(av, u3(32, 48), u3(0, 16))
            return e.tensor_add(bv, u3(48, 64), u3(16, 32))
        st("vector", _bs3, reads=["u4"], writes=["bfold"])
    st("vector", lambda e: e.tensor_add(bfold, bfold, upb_ct[:]),
       reads=["bfold", "upb_ct"], writes=["bfold"])
    dma("sync", btot_dr.ap(), bfold, reads=["bfold"], writes=["btot_d"])
    st("vector", lambda e: e.tensor_add(bfold[:, 0:8], bcon_ct[:], dnb_ct[:]),
       reads=["bcon_ct", "dnb_ct"], writes=["bfold"])
    st("tensor", lambda e: e.transpose(PD[0:8, 0:128], bfold[:, 0:8], identt[:]),
       reads=["bfold", "identt"], writes=["PD"])
    st("vector", lambda e: e.tensor_copy(pmx[0:8, :], PD[0:8, 0:128]),
       reads=["PD"], writes=["u4"])
    dma("sync", b2_dr.ap().rearrange("o (u p) -> o u p", u=8), pmx[0:8, :],
        reads=["u4"], writes=["b2_d"])

    for r_ in range(8):
        dma("sync", bltU.rearrange("p (r j t) -> p r j t", r=8, j=4)[:, r_],
            e3_out.ap()[r_ * 128:(r_ + 1) * 128, 512:1024].rearrange("p (j t) -> p j t", j=4),
            reads=["e3_out_d"], writes=["bltU"], tag="unpBltD%d" % r_)
    for rnd in range(4):
        def _trs(e, rnd=rnd):
            last = None
            for j in range(8):
                u = rnd * 8 + j
                last = e.transpose(PA[:, j * 128:(j + 1) * 128],
                                   bltU[:, u * 128:(u + 1) * 128], identt[:])
            return last
        st("tensor", _trs, reads=["bltU", "identt"], writes=["PA"], tag="bltT%d" % rnd)
        st("scalar", (lambda rnd=rnd: (lambda e: e.activation(
            bltU[:, rnd * 1024:(rnd + 1) * 1024], PA[:], AF.Copy)))(),
           reads=["PA"], writes=["bltU"])
    dma("sync", pfold.rearrange("p (u o) -> p u o", u=32),
        wcon_sl.ap().rearrange("(u p) o -> p u o", p=128),
        writes=["pf0", "pf1", "pf2", "pf3"], tag="ldwcon")
    fold_high(dnh_t, "dnh_t", reverse=True, swap=True)
    for rnd in range(4):
        def _w2mm(e, rnd=rnd):
            last = None
            for j in range(8):
                u = rnd * 8 + j
                last = e.matmul(PA[:, j * 128:(j + 1) * 128],
                                bltU[:, u * 128:(u + 1) * 128],
                                pfold[:, u * 128:(u + 1) * 128],
                                start=True, stop=True)
            return last
        st("tensor", _w2mm, reads=["bltU", "pf%d" % rnd], writes=["PA"], tag="w2m%d" % rnd)
        st("scalar", (lambda rnd=rnd: (lambda e: e.activation(
            qstg[:, 0:1024], PA[:], AF.Copy)))(),
           reads=["PA"], writes=["qstg"])
        dma("sync", e4_in.ap()[:, 4096 + rnd * 1024: 4096 + (rnd + 1) * 1024],
            qstg[:, 0:1024], reads=["qstg"], writes=["e4_in_d"], tag="w2st%d" % rnd)


    st("gpsimd", lambda e: e.memset(S_bf[:], 0.0),
       writes=["S_bf0", "S_bf1", "S_bf2", "S_bf3"])
    dma("sync", ind4[:], ind4_in.ap(), writes=["ind4"], tag="ldind4")

    betav = beta[:].rearrange("p (c t) -> p c t", c=8)
    bmidv = betav[:, :, 63]
    bendv = betav[:, :, 127]

    def quant_fm(nslot, mslot, xq_dst, xq_name, tag, dq_dst, dq_name):
        def _tmul(e):
            last = None
            for c in range(8):
                last = e.tensor_single_scalar(
                    tp[:, c * 128:(c + 1) * 128], xh_fm[:, c * 128:(c + 1) * 128],
                    ncols[:, nslot * 8 + c: nslot * 8 + c + 1], ALU.mult)
            return last
        st("vector", _tmul, reads=["xh_fm", "ncols"], writes=["tp"], tag=tag + "_t")
        st("vector", lambda e: e.reduce_max(
            pmx[:], tp[:, 0:1024].rearrange("p (c t) -> p t c", c=8),
            axis=AX.X, apply_absolute_value=True),
           reads=["tp"], writes=["u4"], tag=tag + "_pm")
        st("tensor", lambda e: e.transpose(PD[:, 0:128], pmx[:], identt[:]),
           reads=["u4", "identt"], writes=["PD"], tag=tag + "_pmT")
        st("vector", lambda e: e.reduce_max(colsA[:, 5:6], PD[:, 0:128], axis=AX.X),
           reads=["PD"], writes=["colsA"])
        st("vector", lambda e: e.tensor_scalar_max(colsA[:, 6:7], colsA[:, 5:6], 1e-5),
           reads=["colsA"], writes=["colsA"])
        st("vector", lambda e: e.reciprocal(colsA[:, 7:8], colsA[:, 6:7]),
           reads=["colsA"], writes=["colsA"])
        st("vector", lambda e: e.tensor_scalar_mul(colsA[:, 8:9], colsA[:, 7:8], 127.0),
           reads=["colsA"], writes=["colsA"])
        st("vector", (lambda mslot=mslot, dq_dst=dq_dst: (lambda e: e.tensor_mul(
            dq_dst, colsA[:, 6:7], m127_cols[:, mslot:mslot + 1])))(),
           reads=["colsA", "m127_cols"], writes=[dq_name])
        st("tensor", lambda e: e.transpose(PD[0:1, 128:256], colsA[:, 8:9], identt[:]),
           reads=["colsA", "identt"], writes=["PD"])
        st("vector", lambda e: e.tensor_copy(srow[0:1, 0:128], PD[0:1, 128:256]),
           reads=["PD"], writes=["u4"])
        st("tensor", lambda e: e.matmul(PD[:, 256:384], onesr[0:1, 0:128],
                                        srow[0:1, 0:128], start=True, stop=True),
           reads=["u4", "onesr"], writes=["PD"])
        def _usc(e):
            last = None
            for c in range(8):
                last = e.tensor_mul(usc[:, c * 128:(c + 1) * 128],
                                    tp[:, c * 128:(c + 1) * 128], PD[:, 256:384])
            return last
        st("vector", _usc, reads=["tp", "PD"], writes=["tp"])
        st("vector", lambda e: e.tensor_scalar_add(rint[:, 0:1024], usc[:, 0:1024], 256.0),
           reads=["tp"], writes=["rint"])
        st("vector", lambda e: e.scalar_tensor_tensor(
            xq_dst[:, 0:1024], rint[:, 0:1024], -255.5, usc[:, 0:1024],
            ALU.add, ALU.is_le),
           reads=["rint", "tp"], writes=[xq_name])
        st("vector", lambda e: e.scalar_tensor_tensor(
            xq_dst[:, 0:1024], rint[:, 0:1024], -256.0, xq_dst[:, 0:1024],
            ALU.add, ALU.add),
           reads=["rint", xq_name], writes=[xq_name], tag=tag + "_dq")

    def _projfm(e, war, xqt, P):
        last = None
        for m in range(8):
            for kk in range(8):
                last = e.matmul(P[:, m * 128:(m + 1) * 128],
                                war[:, kk * 1024 + m * 128: kk * 1024 + (m + 1) * 128],
                                xqt[:, kk * 128:(kk + 1) * 128],
                                start=(kk == 0), stop=(kk == 7))
        return last

    def _projtm(e, war, xqt):
        last = None
        pslices = [PA[:, 0:512], PA[:, 512:1024], PB[:, 0:512], PB[:, 512:1024]]
        for kk in range(8):
            for n in range(4):
                last = e.matmul(pslices[n],
                                xqt[:, kk * 128:(kk + 1) * 128],
                                war[:, kk * 2048 + n * 512: kk * 2048 + (n + 1) * 512],
                                start=(kk == 0), stop=(kk == 7))
        return last

    SITES = [("q", 0, 0, xq_a, "xq_a"),
             ("k", 1, 1, xq_b, "xq_b"),
             ("v", 2, 2, xq_c, "xq_c"),
             ("g", 3, 3, xq_d, "xq_d")]

    def S0(ti):
        tg = "A%d" % ti
        dma("sync", xt[:], xs.ap()[ti * 128:(ti + 1) * 128, :], writes=["xt"], tag=tg)
        def _bns(e):
            e.bn_stats(stats[:, 0:6], xt[:, 0:512])
            return e.bn_stats(stats[:, 6:12], xt[:, 512:1024])
        st("vector", _bns, reads=["xt"], writes=["stats"])
        st("vector", lambda e: e.bn_aggr(colsA[:, 0:2], stats[:]),
           reads=["stats"], writes=["colsA"])
        st("scalar", lambda e: e.activation(colsA[:, 2:3], colsA[:, 1:2], AF.Ln,
                                            bias=epsc[:, 0:1]),
           reads=["colsA", "epsc"], writes=["colsA"])
        st("scalar", lambda e: e.activation(colsA[:, 3:4], colsA[:, 2:3], AF.Exp,
                                            scale=-0.5),
           reads=["colsA"], writes=["colsA"])
        st("vector", lambda e: e.tensor_scalar(colsA[:, 4:5], colsA[:, 0:1],
                                               colsA[:, 3:4], -1.0, ALU.mult, ALU.mult),
           reads=["colsA"], writes=["colsA"])
        st("vector", lambda e: e.tensor_scalar(xh_tm[:], xt[:], colsA[:, 3:4],
                                               colsA[:, 4:5], ALU.mult, ALU.add),
           reads=["xt", "colsA"], writes=["xh_tm"])

    def S1(ti):
        tg = "A%d" % ti
        def _xT(e):
            last = None
            for c in range(8):
                last = e.transpose(PB[:, c * 128:(c + 1) * 128],
                                   xh_tm[:, c * 128:(c + 1) * 128], identt[:])
            return last
        st("tensor", _xT, reads=["xh_tm", "identt"], writes=["PB"])
        st("scalar", lambda e: e.activation(xh_fm[:], PB[:], AF.Copy),
           reads=["PB"], writes=["xh_fm"])
        if ti == 0 and DBG:
            dma("sync", dbg_dr.ap()[0][:, 0:1024], xh_tm[:], reads=["xh_tm"], writes=["dbg0"])
            dma("sync", dbg_dr.ap()[1][:, 0:1024], xh_fm[:], reads=["xh_fm"], writes=["dbg1"])

    def S2_chunks(ti):
        tg = "A%d" % ti
        chunks = []
        def _stage1(j):
            snm, nslot, mslot, xq, xqn = SITES[j]
            tb, tbn = (tp, "tp") if j % 2 == 0 else (sgm, "sgm")
            def _tmul(e, nslot=nslot, tb=tb):
                last = None
                for c in range(8):
                    last = e.tensor_single_scalar(
                        tb[:, c * 128:(c + 1) * 128], xh_fm[:, c * 128:(c + 1) * 128],
                        ncols[:, nslot * 8 + c: nslot * 8 + c + 1], ALU.mult)
                return last
            st("vector", _tmul, reads=["xh_fm", "ncols"], writes=[tbn], tag=tg + snm + "_t")
        def _stage1pm(j):
            snm, nslot, mslot, xq, xqn = SITES[j]
            tb, tbn = (tp, "tp") if j % 2 == 0 else (sgm, "sgm")
            pm, pmn = (pmA, "pmA") if j < 2 else (pmB, "pmB")
            st("vector", (lambda tb=tb, pm=pm: (lambda e: e.reduce_max(
                pm, tb[:, 0:1024].rearrange("p (c t) -> p t c", c=8),
                axis=AX.X, apply_absolute_value=True)))(),
               reads=[tbn], writes=[pmn], tag=tg + snm + "_pm")
            st("tensor", (lambda pm=pm, j=j: (lambda e: e.transpose(
                PC[:, j * 128:(j + 1) * 128], pm, identt[:])))(),
               reads=[pmn, "identt"], writes=["PC"], tag=tg + snm + "_pmT")
        chunks.append(lambda: (_stage1(0), _stage1pm(0)))
        chunks.append(lambda: (_stage1(2), _stage1pm(2)))
        chunks.append(lambda: (_stage1(1), _stage1pm(1), _stage1(3), _stage1pm(3)))
        def _stage2():
            for j in range(4):
                st("vector", (lambda j=j: (lambda e: e.reduce_max(
                    colsA[:, 8 + j:9 + j], PC[:, j * 128:(j + 1) * 128], axis=AX.X)))(),
                   reads=["PC"], writes=["colsA"])
            st("vector", lambda e: e.tensor_scalar_max(colsA[:, 12:16], colsA[:, 8:12], 1e-5),
               reads=["colsA"], writes=["colsA"])
            st("vector", lambda e: e.reciprocal(colsA[:, 16:20], colsA[:, 12:16]),
               reads=["colsA"], writes=["colsA"])
            st("vector", lambda e: e.tensor_scalar_mul(colsA[:, 20:24], colsA[:, 16:20], 127.0),
               reads=["colsA"], writes=["colsA"])
            dqds = [(colsQ[:, ti:ti + 1], "colsQ"), (colsB[:, 41:42], "dqk"),
                    (colsB[:, 42:43], "dqv"), (colsB[:, 43:44], "dqg")]
            for j, (snm, nslot, mslot, xq, xqn) in enumerate(SITES):
                dqd, dqn = dqds[j]
                st("vector", (lambda j=j, dqd=dqd, mslot=mslot: (lambda e: e.tensor_mul(
                    dqd, colsA[:, 12 + j:13 + j], m127_cols[:, mslot:mslot + 1])))(),
                   reads=["colsA", "m127_cols"], writes=[dqn])
            st("vector", lambda e: e.tensor_mul(colsB[:, 46:47], colsB[:, 41:42],
                                                colsB[:, 42:43]),
               reads=["dqk", "dqv"], writes=["dqkv"])
            st("tensor", lambda e: e.transpose(PC[0:4, 0:128], colsA[:, 20:24], identt[:]),
               reads=["colsA", "identt"], writes=["PC"])
            st("vector", lambda e: e.tensor_copy(u4[0:4, 0:128], PC[0:4, 0:128]),
               reads=["PC"], writes=["u4"])
            def _bc4(e):
                last = None
                for j in range(4):
                    last = e.matmul(PC[:, j * 128:(j + 1) * 128],
                                    ind4[0:4, j * 128:(j + 1) * 128], u4[0:4, 0:128],
                                    start=True, stop=True)
                return last
            st("tensor", _bc4, reads=["u4", "ind4"], writes=["PC"])
        chunks.append(_stage2)
        def _stage4(j):
            snm, nslot, mslot, xq, xqn = SITES[j]
            def _usc(e, nslot=nslot, j=j):
                last = None
                for c in range(8):
                    last = e.scalar_tensor_tensor(
                        tp[:, c * 128:(c + 1) * 128], xh_fm[:, c * 128:(c + 1) * 128],
                        ncols[:, nslot * 8 + c: nslot * 8 + c + 1],
                        PC[:, j * 128:(j + 1) * 128], ALU.mult, ALU.mult)
                return last
            st("vector", _usc, reads=["xh_fm", "ncols", "PC"], writes=["tp"],
               tag=tg + snm + "_u")
            st("vector", lambda e: e.tensor_scalar_add(
                rint[:, 0:1024], tp[:, 0:1024], 256.0),
               reads=["tp"], writes=["rint"])
            st("vector", (lambda xq=xq: (lambda e: e.scalar_tensor_tensor(
                xq[:, 0:1024], rint[:, 0:1024], -255.5, tp[:, 0:1024],
                ALU.add, ALU.is_le)))(),
               reads=["rint", "tp"], writes=[xqn])
            st("vector", (lambda xq=xq: (lambda e: e.scalar_tensor_tensor(
                xq[:, 0:1024], rint[:, 0:1024], -256.0, xq[:, 0:1024],
                ALU.add, ALU.add)))(),
               reads=["rint", xqn], writes=[xqn], tag=tg + snm + "_r")
        for _j in range(4):
            chunks.append((lambda _j=_j: (lambda: _stage4(_j)))())
        return chunks

    def S3q(ti):
        tg = "A%d" % ti
        if ti == 0 and DBG:
            dma("gpsimd", dbg_dr.ap()[2][:, 0:1024], xq_a[:], reads=["xq_a"], writes=["dbg2"])
        st("tensor", lambda e: _projfm(e, wq_ar, xq_a, PA),
           reads=["wq_ar", "xq_a"], writes=["PA"], tag=tg + "qmm")
        st("scalar", lambda e: e.activation(q_fm[:], PA[:], AF.Copy),
           reads=["PA"], writes=["q_fm"])
        if ti == 0 and DBG:
            dma("gpsimd", dbg_dr.ap()[3][:, 0:1024], q_fm[:], reads=["q_fm"], writes=["dbg3"])

    def S3k(ti):
        tg = "A%d" % ti
        st("tensor", lambda e: _projfm(e, wk_ar, xq_b, PB),
           reads=["wk_ar", "xq_b"], writes=["PB"], tag=tg + "kmm")
        st("scalar", lambda e: e.activation(k_fm[:], PB[:], AF.Copy),
           reads=["PB"], writes=["k_fm"])

    def S3v(ti):
        tg = "A%d" % ti
        st("tensor", lambda e: _projtm(e, wv_ar, xq_c),
           reads=["wv_ar", "xq_c"], writes=["PA", "PB"], tag=tg + "vmm")
        def _vev(e):
            e.activation(v_tm[:, 0:1024], PA[:], AF.Copy, scale=colsB[:, 46:47])
            return e.activation(v_tm[:, 1024:2048], PB[:], AF.Copy,
                                scale=colsB[:, 46:47])
        st("scalar", _vev, reads=["PA", "PB", "dqkv"], writes=["v_tm"])
        if ti == 0 and DBG:
            dma("gpsimd", dbg_dr.ap()[4], v_tm[:], reads=["v_tm"], writes=["dbg4"])

    def S3g(ti):
        tg = "A%d" % ti
        st("tensor", lambda e: _projtm(e, wg_ar, xq_d),
           reads=["wg_ar", "xq_d"], writes=["PA", "PB"], tag=tg + "gmm")
        def _gev(e):
            e.activation(sg_tm[:, 0:1024], PA[:], AF.Silu, scale=colsB[:, 43:44])
            return e.activation(sg_tm[:, 1024:2048], PB[:], AF.Silu,
                                scale=colsB[:, 43:44])
        st("scalar", _gev, reads=["PA", "PB", "dqg"], writes=["sg_tm"])
        dma("scalar", sg_dr.ap()[ti], sg_tm[:], reads=["sg_tm"], writes=["sg_dr" + str(ti)], tag=tg + "sgst")

    def S3(ti):
        S3q(ti); S3k(ti); S3v(ti); S3g(ti)

    def S4a(ti):
        tg = "A%d" % ti
        quant_fm(4, 5, xq_a, "xq_a", tg + "g1", colsB[:, 44:45], "dq1")

    def S4b(ti, filler=None):
        tg = "A%d" % ti
        def _y1mm(e):
            last = None
            for kk in range(8):
                last = e.matmul(PD[0:16, 0:128],
                                wgk1_ar[:, kk * 16:(kk + 1) * 16],
                                xq_a[:, kk * 128:(kk + 1) * 128],
                                start=(kk == 0), stop=(kk == 7))
            return last
        st("tensor", _y1mm, reads=["wgk1_ar", "xq_a"], writes=["PD"], tag=tg + "y1")
        st("vector", lambda e: e.tensor_copy(y1sb[:], PD[0:16, 0:128]),
           reads=["PD"], writes=["u4"])
        st("tensor", lambda e: e.transpose(PD[:, 128:144], y1sb[:], identt[0:16, 0:16]),
           reads=["u4", "identt"], writes=["PD"])
        st("vector", lambda e: e.tensor_single_scalar(y1t[:], PD[:, 128:144],
                                                      colsB[:, 44:45], ALU.mult),
           reads=["PD", "dq1"], writes=["y1t"])
        st("vector", lambda e: e.tensor_mul(sq16[:], y1t[:], y1t[:]),
           reads=["y1t"], writes=["sq16"])
        st("vector", lambda e: e.reduce_sum(colsA[:, 10:11], sq16[:], axis=AX.X),
           reads=["sq16"], writes=["colsA"])
        st("vector", lambda e: e.tensor_scalar_mul(colsA[:, 11:12], colsA[:, 10:11], 1.0 / 16.0),
           reads=["colsA"], writes=["colsA"])
        st("scalar", lambda e: e.activation(colsA[:, 12:13], colsA[:, 11:12], AF.Ln,
                                            bias=epsc[:, 1:2]),
           reads=["colsA", "epsc"], writes=["colsA"])
        st("scalar", lambda e: e.activation(colsA[:, 13:14], colsA[:, 12:13], AF.Exp,
                                            scale=-0.5),
           reads=["colsA"], writes=["colsA"])
        st("vector", lambda e: e.tensor_mul(t2g[:], y1t[:], ngk2_t[:]),
           reads=["y1t", "ngk2_t"], writes=["t2g"])
        st("vector", lambda e: e.reduce_max(colsA[:, 14:15], t2g[:], axis=AX.X,
                                            apply_absolute_value=True),
           reads=["t2g"], writes=["colsA"])
        st("vector", lambda e: e.tensor_mul(colsA[:, 15:16], colsA[:, 14:15], colsA[:, 13:14]),
           reads=["colsA"], writes=["colsA"])
        st("vector", lambda e: e.tensor_scalar_max(colsB[:, 32:33], colsA[:, 15:16], 1e-5),
           reads=["colsA"], writes=["colsB"])
        st("vector", lambda e: e.reciprocal(colsB[:, 33:34], colsB[:, 32:33]),
           reads=["colsB"], writes=["colsB"])
        st("vector", lambda e: e.tensor_scalar_mul(colsB[:, 34:35], colsB[:, 33:34], 127.0),
           reads=["colsB"], writes=["colsB"])
        st("vector", lambda e: e.tensor_mul(colsB[:, 35:36], colsA[:, 13:14], colsB[:, 34:35]),
           reads=["colsA", "colsB"], writes=["colsB"])
        st("vector", lambda e: e.tensor_mul(colsB[:, 36:37], colsB[:, 32:33], m127_cols[:, 6:7]),
           reads=["colsB", "m127_cols"], writes=["colsB"])
        st("vector", lambda e: e.tensor_single_scalar(sq16[:], t2g[:], colsB[:, 35:36],
                                                      ALU.mult),
           reads=["t2g", "colsB"], writes=["sq16"])
        st("vector", lambda e: e.tensor_scalar_add(rq2[:], sq16[:], 256.0),
           reads=["sq16"], writes=["rq2"])
        st("vector", lambda e: e.scalar_tensor_tensor(t2g[:], rq2[:], -255.5, sq16[:],
                                                      ALU.add, ALU.is_le),
           reads=["rq2", "sq16"], writes=["t2g"])
        st("vector", lambda e: e.scalar_tensor_tensor(sq16[:], rq2[:], -256.0, t2g[:],
                                                      ALU.add, ALU.add),
           reads=["rq2", "t2g"], writes=["sq16"])
        st("vector", lambda e: e.tensor_single_scalar(sq16[:], sq16[:], colsB[:, 36:37],
                                                      ALU.mult),
           reads=["sq16", "colsB"], writes=["sq16"])
        st("tensor", lambda e: e.transpose(PD[0:16, 384:512], sq16[:], identt[:]),
           reads=["sq16", "identt"], writes=["PD"])
        st("vector", lambda e: e.tensor_copy(xq2_fm[:], PD[0:16, 384:512]),
           reads=["PD"], writes=["xq2_fm"])
        def _zmm(e):
            last = None
            for m in range(8):
                last = e.matmul(PA[:, m * 128:(m + 1) * 128],
                                wgk2_ar[:, m * 128:(m + 1) * 128], xq2_fm[:],
                                start=True, stop=True)
            return last
        st("tensor", _zmm, reads=["wgk2_ar", "xq2_fm"],
           writes=["PA"], tag=tg + "z")
        if filler is not None:
            filler()
        def _sig(e):
            last = None
            for m in range(8):
                last = e.activation(sgm[:, m * 128:(m + 1) * 128],
                                    PA[:, m * 128:(m + 1) * 128], AF.Sigmoid,
                                    bias=msum[:, m:m + 1])
            return last
        st("scalar", _sig, reads=["PA", "bgk2_c"], writes=["sgm"])
        st("scalar", lambda e: e.activation(lsg[:], sgm[:], AF.Ln),
           reads=["sgm"], writes=["sgm"])
        def _cums(e):
            last = None
            for c in range(8):
                last = e.tensor_tensor_scan(beta[:, c * 128:(c + 1) * 128],
                                            lsg[:, c * 128:(c + 1) * 128],
                                            lsg[:, c * 128:(c + 1) * 128],
                                            0.0, ALU.add, ALU.bypass)
            return last
        st("vector", _cums, reads=["sgm", "ones_t"], writes=["beta"], tag=tg + "cum")
        if ti == 0 and DBG:
            dma("sync", dbg_dr.ap()[5][:, 0:1024], beta[:], reads=["beta"], writes=["dbg5"])
        def _bcols(e):
            e.tensor_scalar_mul(colsB[:, 0:8], bmidv, -1.0 / 16.0)
            e.tensor_scalar_mul(colsB[:, 8:16], bmidv, 1.0 / 16.0)
            return e.tensor_scalar_mul(colsB[:, 16:24], bendv, 1.0 / 16.0)
        st("vector", _bcols, reads=["beta"], writes=["colsB"])
        st("scalar", lambda e: e.activation(colsB[:, 24:32], bendv, AF.Exp, scale=1.0 / 16.0),
           reads=["beta"], writes=["colsB_lam"])
        st("scalar", lambda e: e.activation(eoff_t[:], offb_t[:], AF.Exp, scale=1.0 / 16.0),
           reads=["offb_t"], writes=["eoff_t"])
        st("vector", lambda e: e.tensor_add(offb_t[:], offb_t[:], bendv),
           reads=["offb_t", "beta"], writes=["offb_t"])
        st("vector", lambda e: e.tensor_mul(dprod[:], dprod[:], colsB[:, 24:32]),
           reads=["dprod", "colsB_lam"], writes=["dprod"])
        st("scalar", lambda e: e.activation(El_t[:], beta[:], AF.Exp, scale=1.0 / 16.0),
           reads=["beta"], writes=["El_t"])

    def S5(ti):
        tg = "A%d" % ti
        for h in range(H):
            c0, c1 = 2 * h, 2 * h + 1
            hsl = slice(c0 * 128, (c1 + 1) * 128)
            Es, esn = (EsA, "EsA") if h % 2 == 0 else (EsB, "EsB")
            Eq_s, Ek_s, EK_s = Es[:, 0:256], Es[:, 256:512], Es[:, 512:768]
            def _es(e, c0=c0, c1=c1, Eq_s=Eq_s, Ek_s=Ek_s, EK_s=EK_s):
                last = None
                for (dst, sc, boff) in ((Eq_s, 1.0 / 16.0, 0), (Ek_s, -1.0 / 16.0, 8),
                                        (EK_s, -1.0 / 16.0, 16)):
                    e.activation(dst[:, 0:128], beta[:, c0 * 128:(c0 + 1) * 128],
                                 AF.Exp, scale=sc, bias=colsB[:, boff + c0:boff + c0 + 1])
                    last = e.activation(dst[:, 128:256], beta[:, c1 * 128:(c1 + 1) * 128],
                                        AF.Exp, scale=sc, bias=colsB[:, boff + c1:boff + c1 + 1])
                return last
            st("scalar", _es, reads=["beta", "colsB"], writes=[esn], tag=tg + "es%d" % h)
            def _mults(e, hsl=hsl, h=h, Eq_s=Eq_s, Ek_s=Ek_s):
                e.tensor_mul(qt_h[:], q_fm[:, hsl], Eq_s)
                return e.tensor_mul(kt_h[:], k_fm[:, hsl], Ek_s)
            st("vector", _mults, reads=["q_fm", "k_fm", esn],
               writes=["qt_h", "kt_h"], tag=tg + "mu%d" % h)
            def _mults2(e, hsl=hsl, h=h, EK_s=EK_s):
                e.tensor_mul(KHp[:], k_fm[:, hsl], EK_s)
                return e.tensor_mul(qh4[:, h * 256:(h + 1) * 256], q_fm[:, hsl],
                                    El_t[:, hsl])
            st("vector", _mults2, reads=["q_fm", "k_fm", esn, "El_t"],
               writes=["KHp", "qh4"], tag=tg + "mu2_%d" % h)
            def _qbar(e, c0=c0, c1=c1, h=h):
                e.tensor_single_scalar(qb_h[:, 0:128], qh4[:, h * 256:h * 256 + 128],
                                       eoff_t[:, c0:c0 + 1], ALU.mult)
                return e.tensor_single_scalar(qb_h[:, 128:256],
                                              qh4[:, h * 256 + 128:h * 256 + 256],
                                              eoff_t[:, c1:c1 + 1], ALU.mult)
            st("vector", _qbar, reads=["qh4", "eoff_t"], writes=["qb_h"])
            dma("sync", qb_dr.ap()[ti][:, c0 * 128:(c1 + 1) * 128], qb_h[:],
                reads=["qb_h"], writes=["qb_dr%d_%d" % (ti, h)], tag=tg + "qbst%d" % h)
            def _amm(e):
                e.matmul(PD[:, 0:128], kt_h[:, 0:128], qt_h[:, 0:128], start=True, stop=False)
                return e.matmul(PD[:, 0:128], kt_h[:, 128:256], qt_h[:, 128:256],
                                start=False, stop=True)
            st("tensor", _amm, reads=["kt_h", "qt_h"], writes=["PD"], tag=tg + "A%d" % h)
            st("vector", (lambda h=h: (lambda e: e.tensor_mul(
                am4[:, h * 128:(h + 1) * 128], PD[:, 0:128], maskt[:])))(),
               reads=["PD", "maskt"], writes=["am4"])
            def _kT(e):
                e.transpose(PD[:, 128:256], KHp[:, 0:128], identt[:])
                return e.transpose(PD[:, 256:384], KHp[:, 128:256], identt[:])
            st("tensor", _kT, reads=["KHp", "identt"], writes=["PD"])
            st("vector", (lambda h=h: (lambda e: e.tensor_copy(
                khtm4[:, h * 256:(h + 1) * 256], PD[:, 128:384])))(),
               reads=["PD"], writes=["khtm4"])

    def S6(ti, heads):
        tg = "A%d" % ti
        for h in heads:
            c0, c1 = 2 * h, 2 * h + 1
            sfn, sbn = "S_f%d" % h, "S_bf%d" % h
            def _umm(e, h=h):
                e.matmul(PD[:], khtm4[:, h * 256:h * 256 + 128],
                         v_tm[:, h * 512:(h + 1) * 512], start=True, stop=True)
                return e.matmul(PE[:], khtm4[:, h * 256 + 128:h * 256 + 256],
                                v_tm[:, h * 512:(h + 1) * 512], start=True, stop=True)
            st("tensor", _umm, reads=["khtm4", "v_tm"], writes=["PD", "PE"],
               tag=tg + "U%d" % h)
            def _omm(e, h=h, c0=c0, c1=c1):
                e.matmul(PF[:], am4[:, h * 128:(h + 1) * 128],
                         v_tm[:, h * 512:(h + 1) * 512], start=True, stop=False)
                e.matmul(PF[:], qh4[:, h * 256:h * 256 + 128],
                         S_bf[:, c0 * 512:(c0 + 1) * 512], start=False, stop=False)
                return e.matmul(PF[:], qh4[:, h * 256 + 128:h * 256 + 256],
                                S_bf[:, c1 * 512:(c1 + 1) * 512], start=False, stop=True)
            st("tensor", _omm, reads=["am4", "v_tm", "qh4", sbn], writes=["PF"],
               tag=tg + "o%d" % h)
            st("scalar", (lambda h=h, ti=ti: (lambda e: e.activation(
                op_t[:, h * 512:(h + 1) * 512], PF[:], AF.Copy,
                scale=colsQ[:, ti:ti + 1])))(),
               reads=["PF", "colsQ"], writes=["op_t"], tag=tg + "oe%d" % h)
            def _supd(e, c0=c0, c1=c1):
                e.scalar_tensor_tensor(S_f[:, c0 * 512:(c0 + 1) * 512],
                                       S_f[:, c0 * 512:(c0 + 1) * 512],
                                       colsB[:, 24 + c0:24 + c0 + 1], PD[:],
                                       ALU.mult, ALU.add)
                return e.scalar_tensor_tensor(S_f[:, c1 * 512:(c1 + 1) * 512],
                                              S_f[:, c1 * 512:(c1 + 1) * 512],
                                              colsB[:, 24 + c1:24 + c1 + 1], PE[:],
                                              ALU.mult, ALU.add)
            st("vector", _supd, reads=["PD", "PE", "colsB_lam", sfn], writes=[sfn],
               tag=tg + "S%d" % h)
            st("scalar", (lambda c0=c0: (lambda e: e.activation(
                S_bf[:, c0 * 512:(c0 + 2) * 512], S_f[:, c0 * 512:(c0 + 2) * 512],
                AF.Copy)))(),
               reads=[sfn], writes=[sbn], tag=tg + "Sbf%d" % h)

    S0(0); S1(0)
    for ch in S2_chunks(0):
        ch()
    S3(0)
    st("gpsimd", lambda e: e.collective_compute(
        "AllGather", ALU.bypass, ins=[e4_in.ap()], outs=[e4_out.ap()],
        replica_groups=[list(range(NCORES))]),
        reads=["e4_in_d"], writes=["e4_out_d"], sem="cc")
    dma("sync", msum[:], bgk2_row.ap(), writes=["bgk2_c"], tag="ldbgk2")
    S4a(0)
    if NT > 1:
        S0(1); S1(1)
    S4b(0)
    pend = {}
    for ti in range(NT):
        tg = "A%d" % ti
        S5(ti)
        if ti + 1 < NT:
            chunks = pend.pop(ti + 1, None)
            if chunks is None:
                chunks = S2_chunks(ti + 1)
                chunks[0]()
                chunks[1]()
            chunks[2]()
            chunks[3]()
            S6(ti, (0,))
            chunks[4]()
            S6(ti, (1,))
            S3q(ti + 1)
            chunks[5]()
            S6(ti, (2,))
            S3k(ti + 1)
            chunks[6]()
            S6(ti, (3,))
            chunks[7]()
            S3v(ti + 1)
            S3g(ti + 1)
        else:
            S6(ti, (0, 1, 2, 3))
        dma("scalar", op_dr.ap()[ti], op_t[:], reads=["op_t"], writes=["op_dr" + str(ti)], tag=tg + "opst")
        if ti + 1 < NT:
            S4a(ti + 1)
            def _filler(ti=ti):
                if ti + 2 < NT:
                    S0(ti + 2)
                    S1(ti + 2)
                    pend[ti + 2] = S2_chunks(ti + 2)
                    pend[ti + 2][0]()
                    pend[ti + 2][1]()
            S4b(ti + 1, _filler)

    b.alias("Sin_f", ["xt", "xh_tm", "xh_fm", "beta"])
    b.alias("Ur_a", ["sgm", "El_t", "tp"])
    b.alias("Ur_b", ["sgm", "El_t", "tp"])
    dma("gpsimd", e5_in.ap()[:, 0:4096], S_f[:],
        reads=["S_f0", "S_f1", "S_f2", "S_f3"], writes=["e5_in_d"])
    st("vector", lambda e: e.memset(Sin_f[:], 0.0), writes=["Sin_f"])
    dma("gpsimd", e5_in.ap()[:, 4096:4104], dprod[:], reads=["dprod"], writes=["e5_in_d"])
    st("gpsimd", lambda e: e.collective_compute(
        "AllGather", ALU.bypass, ins=[e5_in.ap()], outs=[e5_out.ap()],
        replica_groups=[list(range(NCORES))]),
        reads=["e5_in_d"], writes=["e5_out_d"], sem="cc")
    for r in (6, 5, 4, 2, 1, 0):
        dma("gpsimd", Ur_f[:, 0:2048], e5_out.ap()[r * 128:(r + 1) * 128, 0:2048],
            reads=["e5_out_d"], writes=["Ur_a"], tag="DUlda%d" % r)
        dma("gpsimd", Ur_f[:, 2048:4104], e5_out.ap()[r * 128:(r + 1) * 128, 2048:4104],
            reads=["e5_out_d"], writes=["Ur_b"], tag="DUldb%d" % r)
        st("vector", (lambda r=r: (lambda e: e.tensor_single_scalar(
            wrk8[:], gcur[:], bmask_t[:, r:r + 1], ALU.mult)))(),
           reads=["gcur", "bmask_t"], writes=["wrk8"])
        def _acca(e):
            last = None
            for hc in range(4):
                last = e.scalar_tensor_tensor(
                    Sin_f[:, hc * 512:(hc + 1) * 512],
                    Ur_f[:, hc * 512:(hc + 1) * 512],
                    wrk8[:, hc:hc + 1],
                    Sin_f[:, hc * 512:(hc + 1) * 512],
                    ALU.mult, ALU.add)
            return last
        st("vector", _acca, reads=["Ur_a", "wrk8", "Sin_f"], writes=["Sin_f"],
           tag="DUacca%d" % r)
        def _accb(e):
            last = None
            for hc in range(4, 8):
                last = e.scalar_tensor_tensor(
                    Sin_f[:, hc * 512:(hc + 1) * 512],
                    Ur_f[:, hc * 512:(hc + 1) * 512],
                    wrk8[:, hc:hc + 1],
                    Sin_f[:, hc * 512:(hc + 1) * 512],
                    ALU.mult, ALU.add)
            return last
        st("vector", _accb, reads=["Ur_b", "wrk8", "Sin_f"], writes=["Sin_f"],
           tag="DUaccb%d" % r)
        st("vector", (lambda r=r: (lambda e: e.tensor_scalar(
            dmod[:], Ur_f[:, 4096:4104], amask_t[:, r:r + 1], am1_t[:, r:r + 1],
            ALU.mult, ALU.add)))(),
           reads=["Ur_b", "amask_t", "am1_t"], writes=["wrk8"])
        st("vector", lambda e: e.tensor_mul(gcur[:], gcur[:], dmod[:]),
           reads=["gcur", "wrk8"], writes=["gcur"])
    st("gpsimd", lambda e: e.tensor_copy(Sin_bf[:], Sin_f[:]),
       reads=["Sin_f"], writes=["Sin_bf"])

    dma("sync", op_t[:], op_dr.ap()[0], reads=["op_dr0"], writes=["op_t"], tag="B0")
    dma("sync", sg_tm[:], sg_dr.ap()[0], reads=["sg_dr0"], writes=["sg_tm"], tag="B0")
    dma("sync", xq_a[:], qb_dr.ap()[0],
        reads=["qb_dr0_0", "qb_dr0_1", "qb_dr0_2", "qb_dr0_3"],
        writes=["xq_a"], tag="B0")
    for r_ in range(8):
        dma("gpsimd", wo_ar.rearrange("p (r s w) -> p r s w", r=8, s=2)[:, r_],
            e2_out.ap()[r_ * 128:(r_ + 1) * 128, 6144:8192].rearrange("p (s w) -> p s w", s=2),
            reads=["e2_out_d"], writes=["wo_ar"], tag="unpwo%d" % r_)

    b.alias("og", ["Sin_f"])
    b.alias("xtB", ["Ur_a", "Ur_b", "sgm", "El_t"])
    b.alias("ogfm", ["Sin_f"])
    b.alias("x2t", ["Ur_a"])
    b.alias("tp", ["Ur_b"])

    def BL(ti):
        tg = "B%d" % ti
        xb, xbn = (xtB, "xtB") if ti % 2 == 0 else (xtBodd, "xtBodd")
        if ti > 0:
            dma("sync", op_t[:], op_dr.ap()[ti], reads=["op_dr" + str(ti)], writes=["op_t"], tag=tg)
            dma("sync", sg_tm[:], sg_dr.ap()[ti], reads=["sg_dr" + str(ti)], writes=["sg_tm"], tag=tg)
            dma("sync", xq_a[:], qb_dr.ap()[ti],
                reads=["qb_dr%d_0" % ti, "qb_dr%d_1" % ti, "qb_dr%d_2" % ti, "qb_dr%d_3" % ti],
                writes=["xq_a"], tag=tg)
        dma("sync", xb[:], xs.ap()[ti * 128:(ti + 1) * 128, :], writes=[xbn], tag=tg + "x")

    def BC(ti):
        tg = "B%d" % ti
        def _corr(e):
            psl = [PA[:, 0:512], PA[:, 512:1024], PB[:, 0:512], PB[:, 512:1024]]
            last = None
            for h in range(H):
                for cc in range(2):
                    c = 2 * h + cc
                    last = e.matmul(psl[h], xq_a[:, c * 128:(c + 1) * 128],
                                    Sin_bf[:, c * 512:(c + 1) * 512],
                                    start=(cc == 0), stop=(cc == 1))
            return last
        st("tensor", _corr, reads=["xq_a", "Sin_bf"], writes=["PA", "PB"], tag=tg + "corr")

    def B1(ti):
        tg = "B%d" % ti
        def _osum(e, ti=ti):
            e.scalar_tensor_tensor(og[:, 0:1024], PA[:], colsQ[:, ti:ti + 1],
                                   op_t[:, 0:1024], ALU.mult, ALU.add)
            return e.scalar_tensor_tensor(og[:, 1024:2048], PB[:], colsQ[:, ti:ti + 1],
                                          op_t[:, 1024:2048], ALU.mult, ALU.add)
        st("vector", _osum, reads=["op_t", "PA", "PB", "colsQ"], writes=["og"])
        st("vector", lambda e: e.scalar_tensor_tensor(og[:], og[:], ccol[:, 0:1],
                                                      sg_tm[:], ALU.mult, ALU.mult),
           reads=["og", "ccol", "sg_tm"], writes=["og"])
        def _bnso(e):
            last = None
            for c4 in range(4):
                last = e.bn_stats(stats[:, 0:6] if c4 == 0 else stats[:, 6:12] if c4 == 1
                                  else colsB[:, 0:6] if c4 == 2 else colsB[:, 6:12],
                                  og[:, c4 * 512:(c4 + 1) * 512])
            return last
        st("vector", _bnso, reads=["og"], writes=["stats", "colsB"])
        def _bnao(e):
            e.bn_aggr(colsA[:, 0:2], stats[:])
            return e.bn_aggr(colsA[:, 2:4], colsB[:, 0:12])
        st("vector", _bnao, reads=["stats", "colsB"], writes=["colsA"])
        def _msq(e):
            e.tensor_mul(colsA[:, 4:5], colsA[:, 0:1], colsA[:, 0:1])
            return e.tensor_mul(colsA[:, 5:6], colsA[:, 2:3], colsA[:, 2:3])
        st("vector", _msq, reads=["colsA"], writes=["colsA"])
        def _msq2(e):
            e.tensor_add(colsA[:, 6:7], colsA[:, 4:5], colsA[:, 1:2])
            return e.tensor_add(colsA[:, 7:8], colsA[:, 5:6], colsA[:, 3:4])
        st("vector", _msq2, reads=["colsA"], writes=["colsA"])
        st("vector", lambda e: e.tensor_add(colsA[:, 8:9], colsA[:, 6:7], colsA[:, 7:8]),
           reads=["colsA"], writes=["colsA"])
        st("vector", lambda e: e.tensor_scalar_mul(colsA[:, 9:10], colsA[:, 8:9], 0.5),
           reads=["colsA"], writes=["colsA"])
        st("scalar", lambda e: e.activation(colsA[:, 10:11], colsA[:, 9:10], AF.Ln,
                                            bias=epsc[:, 1:2]),
           reads=["colsA", "epsc"], writes=["colsA"])
        st("scalar", lambda e: e.activation(colsA[:, 11:12], colsA[:, 10:11], AF.Exp,
                                            scale=-0.5),
           reads=["colsA"], writes=["colsA"])

    def B2(ti):
        tg = "B%d" % ti
        def _ogT(e):
            last = None
            for c in range(16):
                dst = PA[:, (c % 8) * 128:((c % 8) + 1) * 128] if c < 8 else \
                      PB[:, ((c - 8)) * 128:((c - 8) + 1) * 128]
                last = e.transpose(dst, og[:, c * 128:(c + 1) * 128], identt[:])
            return last
        st("tensor", _ogT, reads=["og", "identt"], writes=["PA", "PB"], tag=tg + "ogT")
        def _ogev(e):
            e.activation(ogfm[:, 0:1024], PA[:], AF.Copy)
            return e.activation(ogfm[:, 1024:2048], PB[:], AF.Copy)
        st("scalar", _ogev, reads=["PA", "PB"], writes=["ogfm"])

    def B3a(ti):
        tg = "B%d" % ti
        for half, (tb, tbn) in enumerate(((tp, "tp"), (x2t, "x2t"))):
            def _tmo(e, half=half, tb=tb):
                last = None
                for c in range(8):
                    cc = half * 8 + c
                    last = e.tensor_single_scalar(tb[:, c * 128:(c + 1) * 128],
                                                  ogfm[:, cc * 128:(cc + 1) * 128],
                                                  no_ct[:, cc:cc + 1], ALU.mult)
                return last
            st("vector", _tmo, reads=["ogfm", "no_ct"], writes=[tbn], tag=tg + "to%d" % half)
            st("vector", (lambda half=half, tb=tb: (lambda e: e.reduce_max(
                pmx[:] if half == 0 else u4[:, 128:256],
                tb[:, 0:1024].rearrange("p (c t) -> p t c", c=8),
                axis=AX.X, apply_absolute_value=True)))(),
               reads=[tbn], writes=["u4"], tag=tg + "pm%d" % half)
        st("vector", lambda e: e.tensor_max(pmx[:], pmx[:], u4[:, 128:256]),
           reads=["u4"], writes=["u4"])
        st("tensor", lambda e: e.transpose(PC[:, 0:128], pmx[:], identt[:]),
           reads=["u4", "identt"], writes=["PC"])
        st("vector", lambda e: e.reduce_max(colsA[:, 12:13], PC[:, 0:128], axis=AX.X),
           reads=["PC"], writes=["colsA"])
        st("vector", lambda e: e.tensor_mul(colsA[:, 13:14], colsA[:, 12:13], colsA[:, 11:12]),
           reads=["colsA"], writes=["colsA"])
        st("vector", lambda e: e.tensor_scalar_max(colsA[:, 14:15], colsA[:, 13:14], 1e-5),
           reads=["colsA"], writes=["colsA"])
        st("vector", lambda e: e.reciprocal(colsA[:, 15:16], colsA[:, 14:15]),
           reads=["colsA"], writes=["colsA"])
        st("vector", lambda e: e.tensor_scalar_mul(colsB[:, 32:33], colsA[:, 15:16], 127.0),
           reads=["colsA"], writes=["colsB"])
        st("vector", lambda e: e.tensor_mul(colsB[:, 33:34], colsA[:, 11:12], colsB[:, 32:33]),
           reads=["colsA", "colsB"], writes=["colsB"])
        st("vector", lambda e: e.tensor_mul(colsB[:, 34:35], colsA[:, 14:15], m127_cols[:, 4:5]),
           reads=["colsA", "m127_cols"], writes=["dqo"])

    def B3bc(ti):
        tg = "B%d" % ti
        st("tensor", lambda e: e.transpose(PC[0:1, 128:256], colsB[:, 33:34], identt[:]),
           reads=["colsB", "identt"], writes=["PC"])
        st("vector", lambda e: e.tensor_copy(srow[0:1, 0:128], PC[0:1, 128:256]),
           reads=["PC"], writes=["u4"])
        st("tensor", lambda e: e.matmul(PC[:, 256:384], onesr[0:1, 0:128],
                                        srow[0:1, 0:128], start=True, stop=True),
           reads=["u4", "onesr"], writes=["PC"])

    def B3bh(ti, half):
        tg = "B%d" % ti
        tb, tbn = (tp, "tp") if half == 0 else (x2t, "x2t")
        def _usco(e, tb=tb):
            last = None
            for c in range(8):
                last = e.tensor_mul(tb[:, c * 128:(c + 1) * 128],
                                    tb[:, c * 128:(c + 1) * 128], PC[:, 256:384])
            return last
        st("vector", _usco, reads=[tbn, "PC"], writes=[tbn])
        st("vector", (lambda tb=tb: (lambda e: e.tensor_scalar_add(
            rint[:], tb[:, 0:1024], 256.0)))(),
           reads=[tbn], writes=["rint"], tag=tg + "ri%d" % half)
        st("vector", (lambda half=half, tb=tb: (lambda e: e.scalar_tensor_tensor(
            v_tm[:, half * 1024:(half + 1) * 1024], rint[:, 0:1024], -255.5,
            tb[:, 0:1024], ALU.add, ALU.is_le)))(),
           reads=["rint", tbn], writes=["v_tm"])
        st("vector", (lambda half=half: (lambda e: e.scalar_tensor_tensor(
            v_tm[:, half * 1024:(half + 1) * 1024], rint[:, 0:1024], -256.0,
            v_tm[:, half * 1024:(half + 1) * 1024], ALU.add, ALU.add)))(),
           reads=["rint", "v_tm"], writes=["v_tm"], tag=tg + "xqo%d" % half)

    def B4p1(ti):
        tg = "B%d" % ti
        for nh, (P, pn) in enumerate(((PE, "PE"), (PF, "PF"))):
            def _womma(e, nh=nh, P=P):
                last = None
                for kk in range(8):
                    last = e.matmul(P[:], v_tm[:, kk * 128:(kk + 1) * 128],
                                    wo_ar[:, kk * 1024 + nh * 512: kk * 1024 + (nh + 1) * 512],
                                    start=(kk == 0), stop=False)
                return last
            st("tensor", _womma, reads=["v_tm", "wo_ar"], writes=[pn], tag=tg + "woa%d" % nh)

    def B4p2(ti):
        tg = "B%d" % ti
        xb, xbn = (xtB, "xtB") if ti % 2 == 0 else (xtBodd, "xtBodd")
        for nh, (P, pn) in enumerate(((PE, "PE"), (PF, "PF"))):
            def _wommb(e, nh=nh, P=P):
                last = None
                for kk in range(8, 16):
                    last = e.matmul(P[:], v_tm[:, kk * 128:(kk + 1) * 128],
                                    wo_ar[:, kk * 1024 + nh * 512: kk * 1024 + (nh + 1) * 512],
                                    start=False, stop=(kk == 15))
                return last
            st("tensor", _wommb, reads=["v_tm", "wo_ar"], writes=[pn], tag=tg + "wob%d" % nh)
            st("vector", (lambda nh=nh, P=P, xb=xb: (lambda e: e.scalar_tensor_tensor(
                x2t[:, nh * 512:(nh + 1) * 512], P[:], colsB[:, 34:35],
                xb[:, nh * 512:(nh + 1) * 512], ALU.mult, ALU.add)))(),
               reads=[xbn, pn, "dqo"], writes=["x2t"], tag=tg + "x2%d" % nh)
        dma("sync", x2_dr.ap()[ti], x2t[:], reads=["x2t"], writes=["x2_dr" + str(ti)], tag=tg + "x2")

    def B5(ti):
        tg = "B%d" % ti
        def _bns2(e):
            e.bn_stats(stats[:, 0:6], x2t[:, 0:512])
            return e.bn_stats(stats[:, 6:12], x2t[:, 512:1024])
        st("vector", _bns2, reads=["x2t"], writes=["stats"])
        st("vector", lambda e: e.bn_aggr(colsA[:, 0:2], stats[:]),
           reads=["stats"], writes=["colsA"])
        st("scalar", lambda e: e.activation(colsA[:, 2:3], colsA[:, 1:2], AF.Ln,
                                            bias=epsc[:, 0:1]),
           reads=["colsA", "epsc"], writes=["colsA"])
        st("scalar", lambda e: e.activation(colsA[:, 3:4], colsA[:, 2:3], AF.Exp,
                                            scale=-0.5),
           reads=["colsA"], writes=["colsA"])
        st("vector", lambda e: e.tensor_scalar(colsA[:, 4:5], colsA[:, 0:1],
                                               colsA[:, 3:4], -1.0, ALU.mult, ALU.mult),
           reads=["colsA"], writes=["colsA"])
        st("vector", lambda e: e.tensor_scalar(og[:, 0:1024], x2t[:], colsA[:, 3:4],
                                               colsA[:, 4:5], ALU.mult, ALU.add),
           reads=["x2t", "colsA"], writes=["og"])
        def _x2T(e):
            last = None
            for c in range(8):
                dst = PC[:, c * 128:(c + 1) * 128] if c < 4 else \
                      PD[:, (c - 4) * 128:(c - 3) * 128]
                last = e.transpose(dst, og[:, c * 128:(c + 1) * 128], identt[:])
            return last
        st("tensor", _x2T, reads=["og", "identt"], writes=["PC", "PD"], tag=tg + "x2T")
        def _x2ev(e):
            e.activation(xh2fm[:, 0:512], PC[:], AF.Copy)
            return e.activation(xh2fm[:, 512:1024], PD[:], AF.Copy)
        st("scalar", _x2ev, reads=["PC", "PD"], writes=["k_fm"])
        dma("scalar", x2f_dr.ap()[ti], xh2fm[:], reads=["k_fm"],
            writes=["x2f_dr" + str(ti)], tag=tg + "x2f")

    BL(0)
    BC(0)
    B1(0)
    B2(0)
    for ti in range(NT):
        B3a(ti)
        if ti + 1 < NT:
            BL(ti + 1)
            BC(ti + 1)
            B1(ti + 1)
            B2(ti + 1)
        B3bc(ti)
        B3bh(ti, 0)
        B4p1(ti)
        B3bh(ti, 1)
        B4p2(ti)
        B5(ti)

    b.alias("x2a", ["og"])
    b.alias("x2b", ["og"])
    b.alias("ya", ["ogfm"])
    b.alias("yb", ["ogfm"])
    b.alias("btot_c", ["upb_ct"])
    b.alias("b2r", ["Eq_t", "Ek_t", "EK_t", "ind4", "pmA", "pmB"])
    dma("sync", upb_ct[:], btot_dr.ap(), reads=["btot_d"], writes=["btot_c"], tag="ldbtot")
    dma("sync", b2r, b2_dr.ap(), reads=["b2_d"], writes=["b2r"], tag="ldb2")
    dma("sync", W1_ar.rearrange("p (k h) -> p k h", k=8),
        e4_out.ap()[:, 0:4096].rearrange("(k p) h -> p k h", p=128),
        reads=["e4_out_d"], writes=["W1_ar"])
    for r_ in range(8):
        dma("sync",
            W2_ar.rearrange("p (u r oo) -> p u r oo", u=32, r=8)[:, :, r_],
            e4_out.ap()[r_ * 128:(r_ + 1) * 128, 4096:8192].rearrange(
                "p (u oo) -> p u oo", u=32),
            reads=["e4_out_d"], writes=["W2_ar"], tag="unpW2_%d" % r_)

    NST = NT // 2
    b.alias("h2g0", ["wgk2_ar"])
    b.alias("h2g1", ["wgk2_ar"])
    def CL(s):
        tg = "C%d" % s
        t0, t1 = 2 * s, 2 * s + 1
        if s % 2 == 0:
            bufA, bufB, nmA, nmB = xst[:, 0:1024], xst[:, 1024:2048], "xstA", "xstB"
        else:
            bufA, bufB, nmA, nmB = xq_a, xq_b, "xq_a", "xq_b"
        for j, tj in enumerate((t0, t1)):
            dma("sync",
                bufA.rearrange("p (c t) -> p c t", c=4)[:, :, j * 128:(j + 1) * 128],
                x2f_dr.ap()[tj][:, 0:512].rearrange("p (c t) -> p c t", c=4),
                reads=["x2f_dr" + str(tj)], writes=[nmA], tag=tg + "xla%d" % j)
            dma("sync",
                bufB.rearrange("p (c t) -> p c t", c=4)[:, :, j * 128:(j + 1) * 128],
                x2f_dr.ap()[tj][:, 512:1024].rearrange("p (c t) -> p c t", c=4),
                reads=["x2f_dr" + str(tj)], writes=[nmB], tag=tg + "xlb%d" % j)

    CL(0)
    if NST > 1:
        CL(1)
    dma("sync", x2a[:], x2_dr.ap()[0], reads=["x2_dr0"], writes=["x2a"], tag="C0")
    dma("sync", x2b[:], x2_dr.ap()[1], reads=["x2_dr1"], writes=["x2b"], tag="C0")
    for s in range(NST):
        tg = "C%d" % s
        t0, t1 = 2 * s, 2 * s + 1
        if s % 2 == 0:
            bufA, bufB, nmA, nmB = xst[:, 0:1024], xst[:, 1024:2048], "xstA", "xstB"
        else:
            bufA, bufB, nmA, nmB = xq_a, xq_b, "xq_a", "xq_b"
        for hg in range(17):
            if hg < 16:
                hc0, hc1 = 2 * hg, 2 * hg + 1
                P1, p1n = (PE, "PE") if hg % 2 == 0 else (PF, "PF")
                hb, hbn = (h2g0, "h2g0") if hg % 2 == 0 else (h2g1, "h2g1")
                def _ffn1(e, hc0=hc0, hc1=hc1, P1=P1, bufA=bufA, bufB=bufB):
                    last = None
                    for hcc, hc in enumerate((hc0, hc1)):
                        for kk in range(8):
                            rhs = (bufA[:, kk * 256:(kk + 1) * 256] if kk < 4
                                   else bufB[:, (kk - 4) * 256:(kk - 3) * 256])
                            last = e.matmul(
                                P1[:, hcc * 256:(hcc + 1) * 256],
                                W1_ar[:, kk * 4096 + hc * 128: kk * 4096 + (hc + 1) * 128],
                                rhs, start=(kk == 0), stop=(kk == 7))
                    return last
                st("tensor", _ffn1, reads=["W1_ar", nmA, nmB],
                   writes=[p1n], tag=tg + "f1_%d" % hg)
                def _gel(e, hc0=hc0, hc1=hc1, P1=P1, hb=hb):
                    e.activation(hb[:, 0:256], P1[:, 0:256], AF.Gelu,
                                 bias=upb_ct[:, hc0:hc0 + 1])
                    return e.activation(hb[:, 256:512], P1[:, 256:512], AF.Gelu,
                                        bias=upb_ct[:, hc1:hc1 + 1])
                st("scalar", _gel, reads=[p1n, "btot_c"], writes=[hbn],
                   tag=tg + "gel%d" % hg)
            if hg >= 1:
                pg = hg - 1
                pc0, pc1 = 2 * pg, 2 * pg + 1
                phb, phbn = (h2g0, "h2g0") if pg % 2 == 0 else (h2g1, "h2g1")
                def _ffn2(e, pg=pg, pc0=pc0, pc1=pc1, phb=phb):
                    last = None
                    for hcc, hc in enumerate((pc0, pc1)):
                        for tj, P in ((0, PA), (1, PB)):
                            for n in range(2):
                                last = e.matmul(
                                    P[:, n * 512:(n + 1) * 512],
                                    phb[:, hcc * 256 + tj * 128: hcc * 256 + (tj + 1) * 128],
                                    W2_ar[:, hc * 1024 + n * 512: hc * 1024 + (n + 1) * 512],
                                    start=(pg == 0 and hcc == 0), stop=False)
                    return last
                st("tensor", _ffn2, reads=[phbn, "W2_ar"], writes=["PA", "PB"],
                   tag=tg + "f2_%d" % pg)
        if s + 2 < NST:
            CL(s + 2)
        def _bias2(e):
            last = None
            for P in (PA, PB):
                for n in range(2):
                    last = e.matmul(P[:, n * 512:(n + 1) * 512], onesr[0:1, 0:128],
                                    b2r[0:1, n * 512:(n + 1) * 512],
                                    start=False, stop=True)
            return last
        st("tensor", _bias2, reads=["b2r", "onesr"], writes=["PA", "PB"], tag=tg + "b2")
        def _yout(e):
            e.tensor_add(ya[:], x2a[:], PA[:])
            return e.tensor_add(yb[:], x2b[:], PB[:])
        st("vector", _yout, reads=["x2a", "x2b", "PA", "PB"], writes=["ya", "yb"])
        dma("sync", ys.ap()[t0 * 128:(t0 + 1) * 128, :], ya[:], reads=["ya"],
            writes=["ys" + str(t0)], tag=tg + "st0")
        dma("sync", ys.ap()[t1 * 128:(t1 + 1) * 128, :], yb[:], reads=["yb"],
            writes=["ys" + str(t1)], tag=tg + "st1")
        if s + 1 < NST:
            dma("sync", x2a[:], x2_dr.ap()[t0 + 2], reads=["x2_dr" + str(t0 + 2)],
                writes=["x2a"], tag="C%d" % (s + 1))
            dma("sync", x2b[:], x2_dr.ap()[t1 + 2], reads=["x2_dr" + str(t1 + 2)],
                writes=["x2b"], tag="C%d" % (s + 1))

    b.emit()
    return nc



_BF = None

_IND4 = np.zeros((4, 512), np.float32)
for _j in range(4):
    _IND4[_j, _j * 128:(_j + 1) * 128] = 1.0

def _host_inputs(inputs, T):
    f32 = np.float32
    x = np.asarray(inputs["L0_attn_hidden_states"], dtype=f32)
    B, L, _ = x.shape
    xflat = np.ascontiguousarray(x.reshape(B * L, D))
    wq = np.asarray(inputs["L0_attn_wq"], f32)
    wk = np.asarray(inputs["L0_attn_wk"], f32)
    wv = np.asarray(inputs["L0_attn_wv"], f32)
    wg = np.asarray(inputs["L0_attn_wg"], f32)
    wo = np.asarray(inputs["L0_attn_wo"], f32)
    wgk1 = np.asarray(inputs["L0_attn_wgk1"], f32)
    wgk2 = np.asarray(inputs["L0_attn_wgk2"], f32)
    wqT = np.ascontiguousarray(wq.T)
    wkT = np.ascontiguousarray(wk.T)
    wvT = np.ascontiguousarray(wv.T)
    wgT = np.ascontiguousarray(wg.T)
    woT = np.ascontiguousarray(wo.T)
    wgk1T = np.ascontiguousarray(wgk1.T)
    wgk2T = np.ascontiguousarray(wgk2.T)
    wgk1T_f = np.ascontiguousarray(
        wgk1T.reshape(8, 128, R).transpose(1, 0, 2).reshape(128, 8 * R))
    def fmcols(v, nch):
        return np.ascontiguousarray(np.asarray(v, f32).reshape(nch, 128).T)
    nq_c = fmcols(inputs["L0_attn_nq"], 8)
    nk_c = fmcols(inputs["L0_attn_nk"], 8)
    nv_c = fmcols(inputs["L0_attn_nv"], 8)
    ng_c = fmcols(inputs["L0_attn_ng"], 8)
    ngk1_c = fmcols(inputs["L0_attn_ngk1"], 8)
    no_c = fmcols(inputs["L0_attn_no"], 16)
    ngk2_r = np.ascontiguousarray(
        np.broadcast_to(np.asarray(inputs["L0_attn_ngk2"], f32), (128, 16)))
    bgk2_row = np.ascontiguousarray(
        np.asarray(inputs["L0_attn_bgk2"], f32).reshape(8, 128).T)
    W_exp = np.asarray(inputs["L0_ffn_W_exp"], f32)
    W_con = np.asarray(inputs["L0_ffn_W_con"], f32)
    WexpT = np.ascontiguousarray(W_exp.T)
    upw = np.asarray(inputs["L0_ffn_up_weights"], f32)
    dnw = np.asarray(inputs["L0_ffn_down_weights"], f32)
    b_exp = np.asarray(inputs["L0_ffn_b_exp"], f32)
    up_bias = np.asarray(inputs["L0_ffn_up_bias"], f32)
    b_con = np.asarray(inputs["L0_ffn_b_con"], f32)
    dn_b = np.asarray(inputs["L0_ffn_down_bias"], f32)

    ident = np.eye(128, dtype=f32)
    jj, ii = np.meshgrid(np.arange(128), np.arange(128), indexing="ij")
    trimask = (jj <= ii).astype(f32)
    nelem = np.array([wq.size, wk.size, wv.size, wg.size, wo.size,
                      wgk1.size * NCORES, wgk2.size * NCORES, 1.0], f32)
    ninv = np.ascontiguousarray(np.broadcast_to(1.0 / nelem, (128, 8)))
    bexp_c = np.ascontiguousarray(b_exp.reshape(32, 128).T)

    def low_pack(W):
        out = np.zeros((NCORES, 7 * 4 * 4 * 64), f32)
        for st_ in range(7):
            sig = 2 ** st_
            Wv = W[st_].reshape(-1, sig, 2, 2)
            for c in range(NCORES):
                for j in range(4):
                    u = 4 * c + j
                    for w in range(4):
                        r_, c_ = divmod(w, 2)
                        vec = np.empty(64, f32)
                        for k in range(64):
                            vec[k] = Wv[u * (64 // sig) + k // sig, k % sig, r_, c_]
                        out[c, ((st_ * 4 + j) * 4 + w) * 64:((st_ * 4 + j) * 4 + w + 1) * 64] = vec
        return out

    def high_pack(W):
        out = np.zeros((128, 5 * 16 * 4), f32)
        for sidx in range(5):
            s_ = sidx + 7
            sig = 2 ** s_
            pairs = [u for u in range(32) if (u >> sidx) & 1 == 0]
            for pi, ua in enumerate(pairs):
                for t in range(128):
                    i = ua * 128 + t
                    k = (i >> (s_ + 1)) * sig + (i & (sig - 1))
                    for w in range(4):
                        r_, c_ = divmod(w, 2)
                        out[t, (sidx * 16 + pi) * 4 + w] = 0
                        out[t, (sidx * 16 + pi) * 4 + w] = W[s_][k, r_, c_]
        return out

    lowU = low_pack(upw)
    lowD = low_pack(dnw)
    uph = high_pack(upw)
    dnh = high_pack(dnw)

    in_maps = []
    for c in range(NCORES):
        base = 4 * (c // 4)
        am = np.zeros(8, f32)
        am[base:c] = 1.0
        woT_sl = woT[c * 256:(c + 1) * 256].reshape(2, 128, D).transpose(1, 0, 2).reshape(128, 2 * D)
        im = {
            "xs": np.ascontiguousarray(xflat[c * T:(c + 1) * T]),
            "ident": ident, "trimask": trimask,
            "wqT_sl": np.ascontiguousarray(wqT[c * 128:(c + 1) * 128]),
            "wkT_sl": np.ascontiguousarray(wkT[c * 128:(c + 1) * 128]),
            "wvT_sl": np.ascontiguousarray(wvT[c * 128:(c + 1) * 128]),
            "wgT_sl": np.ascontiguousarray(wgT[c * 128:(c + 1) * 128]),
            "woT_sl": np.ascontiguousarray(woT_sl),
            "wgk1T_f": wgk1T_f, "wgk2T_f": wgk2T,
            "nq_c": nq_c, "nk_c": nk_c, "nv_c": nv_c, "ng_c": ng_c,
            "ngk1_c": ngk1_c, "no_c": no_c, "ngk2_r": ngk2_r,
            "bgk2_row": bgk2_row, "ninv_row": ninv,
            "lowU_sl": lowU[c][None, :], "lowD_sl": lowD[c][None, :],
            "uph_w": uph, "dnh_w": dnh,
            "wexpT_sl": np.ascontiguousarray(WexpT[:, c * 128:(c + 1) * 128]),
            "wcon_sl": np.ascontiguousarray(W_con[:, c * 128:(c + 1) * 128]),
            "bexp_c": bexp_c,
            "upb_c": np.ascontiguousarray(up_bias.reshape(32, 128).T),
            "bcon_c": np.ascontiguousarray(b_con.reshape(8, 128).T),
            "dnb_c": np.ascontiguousarray(dn_b.reshape(8, 128).T),
            "ind4_in": _IND4,
            "amask": np.ascontiguousarray(np.broadcast_to(am, (128, 8))),
            "am1": np.ascontiguousarray(np.broadcast_to(1.0 - am, (128, 8))),
            "bmask": np.ascontiguousarray(np.broadcast_to(am, (128, 8))),
        }
        in_maps.append(im)
    return in_maps


def kernel(**inputs):
    x = np.asarray(inputs["L0_attn_hidden_states"])
    B, L, _ = x.shape
    T = B * L // NCORES
    in_maps = _host_inputs(inputs, T)
    nc = build_program(T)
    res = run_bass_kernel_spmd(nc, in_maps, core_ids=list(range(NCORES)))
    out = np.concatenate([res.results[c]["ys"] for c in range(NCORES)], axis=0)
    return out.reshape(B, L, D).astype(np.float32)
